# version 06
# baseline (speedup 1.0000x reference)
import math
from contextlib import ExitStack

import numpy as np
import ml_dtypes

import concourse.bass as bass
import concourse.mybir as mybir
from concourse.bass_utils import run_bass_kernel_spmd

F32 = mybir.dt.float32
BF16 = mybir.dt.bfloat16
AF = mybir.ActivationFunctionType
ALU = mybir.AluOpType
AX = mybir.AxisListType
BF = ml_dtypes.bfloat16

N_CORES = 8
B = 16
BPC = B // N_CORES
T = 2048
D = 2048
NT = T // 128
DC = D // 128
EPS_LN = 1e-5


COMPUTE_ENGINES = ("pe", "act", "dve", "pool", "sp")


DMA_SLOTS = 12


_EM_SEQ = [0]


class Em:
    def __init__(self, nc, sem_stack=None, shared=None):
        _EM_SEQ[0] += 1
        self.tag = f"p{_EM_SEQ[0]}_"
        self.nc = nc
        self.ops = []
        self.stack = ExitStack()
        self.sem_stack = sem_stack if sem_stack is not None else self.stack
        if shared is not None:
            self.counts = shared.setdefault("counts", {})
            self.sems = shared.setdefault("sems", {})
        else:
            self.counts = {}
            self.sems = {}
        self._sem_keys = []
        self.dma_idx = {}
        self.dma_slot_prev = {}

    def _tok(self, sem_key, inc):
        self.counts[sem_key] = self.counts.get(sem_key, 0) + inc
        if sem_key not in self._sem_keys:
            self._sem_keys.append(sem_key)
        return (sem_key, self.counts[sem_key])

    @staticmethod
    def _flat(deps):
        out = []
        for d in deps:
            if d is None:
                continue
            if isinstance(d, list):
                out.extend(t for t in d if t is not None)
            else:
                out.append(d)
        return tuple(out)

    def op(self, eng, fn, deps=(), lowp=False):
        tok = self._tok(eng, 1)
        if lowp:
            inner = fn

            def fn(e, inner=inner):
                with self.nc.allow_low_precision(reason="feeds bf16 matmul"):
                    return inner(e)
        self.ops.append((eng, fn, self._flat(deps), eng, 1))
        return tok

    def dma(self, eng, fn, deps=()):
        idx = self.dma_idx.get(eng, 0)
        self.dma_idx[eng] = idx + 1
        slot = idx % DMA_SLOTS
        key = f"{eng}_dma{slot}"
        slots = self.dma_slot_prev.setdefault(eng, {})
        prev_tok = slots.get(slot)
        tok = self._tok(key, 16)
        deps = self._flat(deps)
        if prev_tok is not None:
            deps = deps + (prev_tok,)
        self.ops.append((eng, fn, deps, key, 16))
        slots[slot] = tok
        return tok

    def qbar(self, eng):
        return list(self.dma_slot_prev.get(eng, {}).values())

    def emit(self):
        nc = self.nc
        for key in self._sem_keys:
            if key not in self.sems:
                self.sems[key] = self.sem_stack.enter_context(nc.semaphore("s_" + key))
        per_eng = {e: [] for e in COMPUTE_ENGINES}
        for rec in self.ops:
            per_eng[rec[0]].append(rec)
        waited = {e: {} for e in COMPUTE_ENGINES}

        def run_stream(eng_name, eng):
            for _, fn, deps, sem_key, inc in per_eng[eng_name]:
                w = waited[eng_name]
                for dep_key, dep_val in deps:
                    if w.get(dep_key, 0) < dep_val:
                        eng.wait_ge(self.sems[dep_key], dep_val)
                        w[dep_key] = dep_val
                ins = fn(eng)
                ins.then_inc(self.sems[sem_key], inc)

        with nc.Block() as blk:
            if per_eng["pe"]:
                blk.tensor(lambda e: run_stream("pe", e))
            if per_eng["act"]:
                blk.scalar(lambda e: run_stream("act", e))
            if per_eng["dve"]:
                blk.vector(lambda e: run_stream("dve", e))
            if per_eng["pool"]:
                blk.gpsimd(lambda e: run_stream("pool", e))
            if per_eng["sp"]:
                blk.sync(lambda e: run_stream("sp", e))
        final = dict(self.counts)
        with nc.Block() as blk2:
            def _fin(eng):
                for key, val in final.items():
                    eng.wait_ge(self.sems[key], val)
            blk2.sync(_fin)
        self.ops = []



def host_constants():
    c = {}
    hd = 64
    theta = 1.0 / (10000.0 ** (np.arange(0, hd, 2, dtype=np.float64) / hd))
    pos = np.arange(T, dtype=np.float64)
    ang = np.outer(pos, theta)
    cs = np.cos(ang)
    cos2 = np.concatenate([cs, cs], axis=1)
    sn = np.sin(ang)
    c["rope64_cos2"] = np.tile(cos2, (1, 8)).astype(np.float32)
    c["rope64_sin"] = np.tile(sn, (1, 8)).astype(np.float32)
    c["rope64_nsin"] = np.tile(-sn, (1, 8)).astype(np.float32)
    hd = 128
    inv = 1.0 / (10000.0 ** (np.arange(0, hd, 2, dtype=np.float64) / hd))
    ang = np.outer(pos, inv)
    cos128 = np.concatenate([np.cos(ang), np.cos(ang)], axis=1)
    ssin = np.concatenate([-np.sin(ang), np.sin(ang)], axis=1)
    c["rope128_cs"] = np.concatenate(
        [np.tile(cos128, (1, 4)), np.tile(ssin, (1, 4))], axis=1).astype(np.float32)
    return c



def make_identity_bf16(em, ident_sb):
    t0 = em.op("pool", lambda g: g.memset(ident_sb[:, :], 0.0))
    return em.op("pool", lambda g: g.affine_select(
        ident_sb[:, :], ident_sb[:, :], [[-1, 128]], ALU.not_equal, 1.0,
        base=0, channel_multiplier=1), deps=[t0])


def causal_keep_mask(em, mask_sb, delta):
    t0 = em.op("pool", lambda g: g.memset(mask_sb[:, :], 1.0))
    return em.op("pool", lambda g: g.affine_select(
        mask_sb[:, :], mask_sb[:, :], [[1, mask_sb.shape[1]]], ALU.is_ge, 0.0,
        base=-delta, channel_multiplier=-1), deps=[t0])



def emit_ln_pass(em, x_ap, xln_ap, sb, bi, store_stats=False):
    toks = []
    nbuf = len(sb["lnx"])
    prev = [dict(s2=None, s5=None, st=None) for _ in range(nbuf)]
    for t in range(NT):
        j = t % nbuf
        lnx, lnxln = sb["lnx"][j], sb["lnxln"][j]
        lnsq = lnxln
        co = 8 * j
        P = prev[j]
        ld = em.dma("sp", lambda e, lnx=lnx, t=t: e.dma_start(
            lnx[:, :], x_ap[128 * t:128 * (t + 1), :]),
            deps=[P["s5"], P["s2"]])
        s0 = em.op("dve", lambda e, lnx=lnx, co=co: e.reduce_sum(
            sb["lncol"][:, co:co + 1], lnx[:, :], axis=AX.X),
            deps=[ld, P["s5"], P["s2"]])
        s1 = em.op("dve", lambda e, co=co: e.tensor_scalar_mul(
            sb["lncol"][:, co + 1:co + 2], sb["lncol"][:, co:co + 1], -1.0 / D),
            deps=[s0])
        s2 = em.op("act", lambda e, lnx=lnx, lnsq=lnsq, co=co: e.activation(
            lnsq[:, :], lnx[:, :], AF.Square, bias=sb["lncol"][:, co + 1:co + 2],
            accum_out=sb["lncol"][:, co + 2:co + 3]), deps=[s1, ld, P["s2"], P["st"]])
        s2b = em.op("dve", lambda e, co=co: e.tensor_scalar(
            sb["lncol"][:, co + 5:co + 6], sb["lncol"][:, co + 2:co + 3],
            1.0 / D, EPS_LN, ALU.mult, ALU.add), deps=[s2])
        s3 = em.op("act", lambda e, co=co: e.activation(
            sb["lncol"][:, co + 3:co + 4], sb["lncol"][:, co + 5:co + 6],
            AF.Ln), deps=[s2b])
        s4 = em.op("act", lambda e, co=co: e.activation(
            sb["lncol"][:, co + 4:co + 5], sb["lncol"][:, co + 3:co + 4], AF.Exp,
            scale=-0.5), deps=[s3])
        if store_stats:
            c = 2 * (bi * NT + t)
            em.op("dve", lambda e, c=c, co=co: e.tensor_copy(
                sb["stats"][:, c:c + 1], sb["lncol"][:, co + 1:co + 2]), deps=[s1])
            em.op("dve", lambda e, c=c, co=co: e.tensor_copy(
                sb["stats"][:, c + 1:c + 2], sb["lncol"][:, co + 4:co + 5]), deps=[s4])
        s5 = em.op("dve", lambda e, lnx=lnx, lnxln=lnxln, co=co: e.tensor_scalar(
            lnxln[:, :], lnx[:, :], sb["lncol"][:, co + 1:co + 2],
            sb["lncol"][:, co + 4:co + 5], ALU.add, ALU.mult),
            deps=[s4, ld, P["st"]])
        st = em.dma("sp", lambda e, lnxln=lnxln, t=t: e.dma_start(
            xln_ap[128 * t:128 * (t + 1), :], lnxln[:, :]), deps=[s5])
        prev[j] = dict(s2=s2, s5=s5, st=st)
        toks.append(st)
    return toks



def alloc_l0_attn_sbuf(em):
    nc = em.nc
    st = em.stack
    sb = {}
    sb["lnx"] = [st.enter_context(nc.sbuf_tensor("lnx0", [128, D], F32))]
    sb["lnxln"] = [st.enter_context(nc.sbuf_tensor("lnxln0", [128, D], BF16))]
    sb["lncol"] = st.enter_context(nc.sbuf_tensor("lncol", [128, 16], F32))
    sb["big0"] = st.enter_context(nc.sbuf_tensor("big0", [128, 32768], BF16))
    sb["kq"] = [st.enter_context(nc.sbuf_tensor(f"kq{j}", [128, T], BF16)) for j in range(2)]
    sb["kk"] = [st.enter_context(nc.sbuf_tensor(f"kk{j}", [128, T], BF16)) for j in range(2)]
    sb["vaug"] = st.enter_context(nc.sbuf_tensor("vaug", [128, 520 * NT], BF16))
    sb["wbuf"] = [st.enter_context(nc.sbuf_tensor(f"wbuf{j}", [128, 512 * 16], BF16)) for j in range(2)]
    sb["ctab"] = st.enter_context(nc.sbuf_tensor("ctab", [128, 512 * NT], BF16))
    sb["stab"] = st.enter_context(nc.sbuf_tensor("stab", [128, 256 * NT], BF16))
    sb["nstab"] = st.enter_context(nc.sbuf_tensor("nstab", [128, 256 * NT], BF16))
    sb["rt1"] = [st.enter_context(nc.sbuf_tensor(f"rt1_{j}", [128, 512], BF16)) for j in range(2)]
    sb["rt2"] = [st.enter_context(nc.sbuf_tensor(f"rt2_{j}", [128, 512], BF16)) for j in range(2)]
    sb["qrot"] = [st.enter_context(nc.sbuf_tensor(f"qrot{j}", [128, 512], BF16)) for j in range(2)]
    sb["at"] = [st.enter_context(nc.sbuf_tensor(f"at{j}", [128, 512], BF16)) for j in range(2)]
    sb["at2"] = [st.enter_context(nc.sbuf_tensor(f"at2_{j}", [128, 512], BF16)) for j in range(2)]
    sb["mask"] = [st.enter_context(nc.sbuf_tensor(f"mask{j}", [128, 512], BF16)) for j in range(4)]
    sb["zrb"] = st.enter_context(nc.sbuf_tensor("zrb", [1, 512], BF16))
    sb["zb"] = [st.enter_context(nc.sbuf_tensor(f"zb{j}", [64, 512], BF16)) for j in range(3)]
    sb["ones1"] = st.enter_context(nc.sbuf_tensor("ones1", [1, 128], BF16))
    sb["xc"] = [st.enter_context(nc.sbuf_tensor(f"xc{j}", [128, 512], F32)) for j in range(2)]
    sb["x1c"] = [st.enter_context(nc.sbuf_tensor(f"x1c{j}", [128, 512], F32)) for j in range(2)]
    sb["pa"] = [st.enter_context(nc.psum_tensor(f"pa{j}", [128, 512], F32)) for j in range(2)]
    sb["pc"] = [st.enter_context(nc.psum_tensor(f"pc{j}", [128, 512], F32)) for j in range(3)]
    sb["pz"] = st.enter_context(nc.psum_tensor("pz", [64, 512], F32))
    sb["pp"] = [st.enter_context(nc.psum_tensor(f"pp{j}", [128, 512], F32)) for j in range(2)]
    return sb


def emit_l0_attn(em, aps, sb):
    mtoks = [causal_keep_mask(em, sb["mask"][j], 128 * j) for j in range(4)]
    ones_t = em.op("pool", lambda g: g.memset(sb["ones1"][:, :], 1.0))
    def _tab_ap(ap, w):
        return bass.AP(ap.tensor, ap.offset, [[w, 128], [128 * w, NT], [1, w]])
    tc_t = em.dma("pool", lambda e: e.dma_start(
        sb["ctab"][:, :], _tab_ap(aps["rope_cos2"], 512)))
    ts_t = em.dma("pool", lambda e: e.dma_start(
        sb["stab"][:, :], _tab_ap(aps["rope_sin"], 256)))
    tn_t = em.dma("pool", lambda e: e.dma_start(
        sb["nstab"][:, :], _tab_ap(aps["rope_nsin"], 256)))
    rope_ready = [tc_t, ts_t, tn_t]

    prev_wo_done = None
    prev_batch = {}

    for i in range(BPC):
        ln_toks = emit_ln_pass(em, aps["x"][i], aps["xln"][i], sb, i)
        xlnT = []
        for j in range(DC):
            dep = list(ln_toks) + ([prev_wo_done] if j == 0 else [])
            xlnT.append(em.dma("sp", lambda e, j=j, i=i: e.dma_start(
                sb["big0"][:, 2048 * j:2048 * (j + 1)],
                aps["xln"][i][:, 128 * j:128 * (j + 1)], transpose=True), deps=dep))
        xlnT_done = xlnT
        vone = em.op("pool", lambda g: g.memset(
            bass.AP(sb["vaug"], 64, [[520 * NT, 128], [520, NT], [65, 8], [1, 1]]), 1.0),
            deps=[prev_batch.get("av_done")])

        def load_w(ap_w, col0, ncols, jbuf, deps):
            wap = aps[ap_w]
            src_ap = bass.AP(wap.tensor, wap.offset + col0,
                             [[wap.shape[1], 128], [128 * wap.shape[1], DC], [1, ncols]])
            return em.dma("pool", lambda g: g.dma_start(
                sb["wbuf"][jbuf][:, 0:ncols * DC], src_ap), deps=deps)

        def proj_tile(t, jb, ncols, psl, deps_extra=()):
            mm = None
            for kc in range(DC):
                st_ = (kc == 0)
                sp_ = (kc == DC - 1)
                mm = em.op("pe", lambda e, t=t, kc=kc, jb=jb, ncols=ncols, psl=psl, st_=st_, sp_=sp_: e.matmul(
                    sb["pp"][psl][0:128, 0:ncols],
                    sb["big0"][:, 2048 * kc + 128 * t:2048 * kc + 128 * (t + 1)],
                    sb["wbuf"][jb][:, ncols * kc:ncols * (kc + 1)],
                    start=st_, stop=sp_),
                    deps=(list(deps_extra) + xlnT_done if kc == 0 else []))
            return mm

        def rope_evac(t, psl, dma_dst, rr, deps_mm, deps_dma=()):
            r1 = em.op("dve", lambda e, psl=psl, rr=rr, t=t: e.scalar_tensor_tensor(
                sb["rt1"][rr][:, :], sb["pp"][psl][:, :], 1.0,
                sb["ctab"][:, 512 * t:512 * (t + 1)],
                ALU.bypass, ALU.mult),
                deps=deps_mm + rope_ready + [prev_batch.get(("rope1", rr))])
            r2 = em.op("dve", lambda e, psl=psl, rr=rr, t=t: e.tensor_tensor(
                bass.AP(sb["rt2"][rr], 0, [[512, 128], [64, 8], [1, 32]]),
                bass.AP(sb["pp"][psl], 32, [[512, 128], [64, 8], [1, 32]]),
                bass.AP(sb["nstab"], 256 * t, [[256 * NT, 128], [32, 8], [1, 32]]),
                ALU.mult),
                deps=deps_mm + rope_ready + [prev_batch.get(("rope2", rr))])
            r3 = em.op("dve", lambda e, psl=psl, rr=rr, t=t: e.tensor_tensor(
                bass.AP(sb["rt2"][rr], 32, [[512, 128], [64, 8], [1, 32]]),
                bass.AP(sb["pp"][psl], 0, [[512, 128], [64, 8], [1, 32]]),
                bass.AP(sb["stab"], 256 * t, [[256 * NT, 128], [32, 8], [1, 32]]),
                ALU.mult),
                deps=deps_mm + [prev_batch.get(("rope2", rr))])
            r4 = em.op("dve", lambda e, rr=rr: e.tensor_add(
                sb["qrot"][rr][:, :], sb["rt1"][rr][:, :], sb["rt2"][rr][:, :]),
                deps=[r1, r2, r3, prev_batch.get(("rope4", rr))])
            prev_batch[("rope1", rr)] = r4
            prev_batch[("rope2", rr)] = r4
            if dma_dst is None:
                return r4, None
            dm = em.dma("sp", lambda e, rr=rr: e.dma_start(dma_dst, sb["qrot"][rr][:, :]),
                        deps=[r4] + list(deps_dma))
            prev_batch[("rope4", rr)] = dm
            return r4, dm

        scr_war = prev_batch.get("scr_load_last")
        wbuf_reader = prev_batch.get("wbuf_reader", [None, None])
        wk_t = load_w("wk", 0, 512, 0, [wbuf_reader[0]])
        scr_k_stores = []
        pp_free = prev_batch.get("pp_free", [None, None])
        rr = 0
        mm = None
        for t in range(NT):
            psl = t % 2
            mm = proj_tile(t, 0, 512, psl, deps_extra=[wk_t, pp_free[psl]])
            r4, _ = rope_evac(t, psl, None, rr, [mm])
            krr = rr
            rr ^= 1
            kap = aps["scr_k"]
            kdms = []
            for half in range(2):
                dst = bass.AP(kap.tensor,
                              kap.offset + (128 * t) * 1024 + 64 * half,
                              [[1024, 128], [128, 8], [1, 64]])
                dm = em.dma("sp", lambda e, dst=dst, krr=krr: e.dma_start(
                    dst, bass.AP(sb["qrot"][krr], 0, [[512, 128], [64, 8], [1, 64]])),
                    deps=[r4] + ([scr_war] if t == 0 and half == 0 else []))
                scr_k_stores.append(dm)
                kdms.append(dm)
            prev_batch[("rope4", krr)] = kdms
            pp_free[psl] = r4
        wbuf_reader[0] = mm
        wv_t = load_w("wv", 0, 512, 1, [wbuf_reader[1]])
        v_evacs = []
        for t in range(NT):
            psl = t % 2
            mm = proj_tile(t, 1, 512, psl, deps_extra=[wv_t, pp_free[psl]])
            ev = em.op("dve", lambda e, t=t, psl=psl: e.tensor_copy(
                bass.AP(sb["vaug"], 520 * t, [[520 * NT, 128], [65, 8], [1, 64]]),
                sb["pp"][psl][:, :]),
                deps=[mm, vone, prev_batch.get("av_done")])
            pp_free[psl] = ev
            v_evacs.append(ev)
        wbuf_reader[1] = mm
        v_done = v_evacs[-1]
        scr_q_stores = []
        for dqc in range(4):
            jb = dqc % 2
            wq_t = load_w("wq", 512 * dqc, 512, jb, [wbuf_reader[jb]])
            for t in range(NT):
                psl = t % 2
                mm = proj_tile(t, jb, 512, psl, deps_extra=[wq_t, pp_free[psl]])
                r4, dm = rope_evac(
                    t, psl, aps["scr_q"][128 * t:128 * (t + 1), 512 * dqc:512 * (dqc + 1)],
                    rr, [mm])
                pp_free[psl] = r4
                rr ^= 1
                scr_q_stores.append(dm)
            wbuf_reader[jb] = mm
        q_done = scr_q_stores
        k_done = scr_k_stores

        kq_free = [None, None]
        kk_free = [None, None]
        kq_tok = {}
        kk_tok = {}
        exp_toks = {}
        ctx_evac = [None, None, None]
        av_done_all = []
        qpair_prev = -1
        kpair_prev = -1
        pending_z = []

        def make_zflush(pcs, h, tc, last_av):
            def zflush():
                zc = em.op("dve", lambda e, pcs=pcs: e.reciprocal(
                    sb["zrb"][:, :], sb["pc"][pcs][64:65, :]),
                    deps=[last_av, prev_batch.get("zrb_free")], lowp=True)
                zmm = em.op("pe", lambda e: e.matmul(
                    sb["pz"][:, :], sb["ones1"][:, 0:64], sb["zrb"][:, :],
                    start=True, stop=True),
                    deps=[zc, prev_batch.get("pz_free")])
                prev_batch["zrb_free"] = zmm
                zcp = em.op("dve", lambda e, pcs=pcs: e.tensor_copy(
                    sb["zb"][pcs][:, :], sb["pz"][:, :]),
                    deps=[zmm, prev_batch.get(("zb_free", pcs))])
                prev_batch["pz_free"] = zcp
                cm = em.op("dve", lambda e, pcs=pcs, h=h, tc=tc: e.tensor_tensor(
                    sb["big0"][64 * (h % 2):64 * (h % 2) + 64,
                               2048 * (h // 2) + 512 * tc:2048 * (h // 2) + 512 * (tc + 1)],
                    sb["pc"][pcs][0:64, :], sb["zb"][pcs][:, :], ALU.mult),
                    deps=[zcp, last_av] + xlnT_done)
                prev_batch[("zb_free", pcs)] = cm
                ctx_evac[pcs] = cm
                return cm
            return zflush

        for h in range(32):
            qp = h // 2
            hq = qp % 2
            if qp != qpair_prev:
                qld = em.dma("sp", lambda e, qp=qp, hq=hq: e.dma_start(
                    sb["kq"][hq][:, :], aps["scr_q"][:, 128 * qp:128 * (qp + 1)],
                    transpose=True), deps=q_done + [kq_free[hq]])
                kq_tok[qp] = qld
                qpair_prev = qp
            hp = h // 4
            hk = hp % 2
            if hp != kpair_prev:
                kld = em.dma("sp", lambda e, hp=hp, hk=hk: e.dma_start(
                    sb["kk"][hk][:, :], aps["scr_k"][:, 128 * hp:128 * (hp + 1)],
                    transpose=True), deps=k_done + [kk_free[hk]])
                kk_tok[hp] = kld
                kpair_prev = hp
            qbase = 64 * (h % 2)
            kbase = qbase
            for tc in range(4):
                pcs = (4 * h + tc) % 3
                n_st = 4 * tc + 4
                avq = []
                last_av = None
                for st_i in range(n_st):
                    pas = st_i % 2
                    smm = em.op("pe", lambda e, st_i=st_i, tc=tc, hq=hq, hk=hk, pas=pas, qbase=qbase, kbase=kbase: e.matmul(
                        sb["pa"][pas][:, :],
                        sb["kk"][hk][kbase:kbase + 64, 128 * st_i:128 * (st_i + 1)],
                        sb["kq"][hq][qbase:qbase + 64, 512 * tc:512 * (tc + 1)],
                        start=True, stop=True),
                        deps=[kq_tok[qp], kk_tok[hp], exp_toks.get(pas)])
                    if st_i == 1 and len(pending_z) >= 2:
                        pending_z.pop(0)()
                    ex = em.op("act", lambda e, st_i=st_i, pas=pas: e.activation(
                        sb["at"][pas][:, :], sb["pa"][pas][:, :], AF.Exp, scale=0.125),
                        deps=[smm, prev_batch.get(("at", pas))])
                    exp_toks[pas] = ex
                    rhs_buf = sb["at"][pas]
                    if st_i >= 4 * tc:
                        mk = st_i - 4 * tc
                        ex = em.op("dve", lambda e, pas=pas, mk=mk: e.tensor_tensor(
                            sb["at2"][pas][:, :], sb["at"][pas][:, :],
                            sb["mask"][mk][:, :], ALU.mult),
                            deps=[ex, mtoks[mk], prev_batch.get(("at2", pas))])
                        rhs_buf = sb["at2"][pas]
                    avq.append((st_i, rhs_buf, ex, pas))
                    if len(avq) >= 2:
                        st_e, rb_e, ex_e, pas_e = avq.pop(0)
                        av = em.op("pe", lambda e, st_e=st_e, rb_e=rb_e, pcs=pcs, first=(st_e == 0), last=False, hp=hp: e.matmul(
                            sb["pc"][pcs][0:65, :],
                            sb["vaug"][:, 520 * st_e + 65 * hp:520 * st_e + 65 * (hp + 1)],
                            rb_e[:, :], start=first, stop=last),
                            deps=[ex_e, v_done, ctx_evac[pcs] if st_e == 0 else None])
                        prev_batch[("at", pas_e)] = av
                        prev_batch[("at2", pas_e)] = av
                        last_av = av
                st_e, rb_e, ex_e, pas_e = avq.pop(0)
                av = em.op("pe", lambda e, st_e=st_e, rb_e=rb_e, pcs=pcs, first=(st_e == 0), hp=hp: e.matmul(
                    sb["pc"][pcs][0:65, :],
                    sb["vaug"][:, 520 * st_e + 65 * hp:520 * st_e + 65 * (hp + 1)],
                    rb_e[:, :], start=first, stop=True),
                    deps=[ex_e, v_done, ctx_evac[pcs] if st_e == 0 else None])
                prev_batch[("at", pas_e)] = av
                prev_batch[("at2", pas_e)] = av
                last_av = av
                av_done_all.append(last_av)
                pending_z.append(make_zflush(pcs, h, tc, last_av))
            kq_free[hq] = smm
            kk_free[hk] = smm
        while pending_z:
            pending_z.pop(0)()
        cm = ctx_evac[0]
        av_done = av_done_all[-1]
        prev_batch["av_done"] = av_done
        prev_batch["scr_load_last"] = kq_tok[15]

        ctx_done = ctx_evac[0] if ctx_evac[0] else None
        ctx_all = [t_ for t_ in ctx_evac if t_ is not None]

        xc_free = prev_batch.get("xc_free", [None, None])
        x1_stores = []
        last_wo_mm = None
        for dc in range(4):
            jb = dc % 2
            wo_t = load_w("wo", 512 * dc, 512, jb, [wbuf_reader[jb]])
            for t in range(NT):
                psl = t % 2
                xl = em.dma("sp", lambda e, i=i, t=t, dc=dc, psl=psl: e.dma_start(
                    sb["xc"][psl][:, :],
                    aps["x"][i][128 * t:128 * (t + 1), 512 * dc:512 * (dc + 1)]),
                    deps=[xc_free[psl]])
                mm = None
                for kc in range(DC):
                    mm = em.op("pe", lambda e, kc=kc, t=t, jb=jb, psl=psl, st_=(kc == 0), sp_=(kc == DC - 1): e.matmul(
                        sb["pp"][psl][:, :],
                        sb["big0"][:, 2048 * kc + 128 * t:2048 * kc + 128 * (t + 1)],
                        sb["wbuf"][jb][:, 512 * kc:512 * (kc + 1)],
                        start=st_, stop=sp_),
                        deps=(ctx_all + [wo_t, pp_free[psl]]) if kc == 0 else [])
                ev = em.op("dve", lambda e, psl=psl: e.scalar_tensor_tensor(
                    sb["x1c"][psl][:, :], sb["pp"][psl][:, :], 1.0,
                    sb["xc"][psl][:, :], ALU.bypass, ALU.add),
                    deps=[mm, xl, prev_batch.get(("x1c", psl))])
                pp_free[psl] = ev
                xc_free[psl] = ev
                dmo = em.dma("sp", lambda e, i=i, t=t, dc=dc, psl=psl: e.dma_start(
                    aps["x1"][i][128 * t:128 * (t + 1), 512 * dc:512 * (dc + 1)],
                    sb["x1c"][psl][:, :]), deps=[ev])
                prev_batch[("x1c", psl)] = dmo
                x1_stores.append(dmo)
                last_wo_mm = mm
            wbuf_reader[jb] = last_wo_mm
        prev_wo_done = last_wo_mm
        prev_batch["wbuf_reader"] = wbuf_reader
        prev_batch["pp_free"] = pp_free
        prev_batch["xc_free"] = xc_free
        all_stores = x1_stores
    return all_stores



def build_l0_attn_program():
    nc = bass.Bass("TRN2", target_bir_lowering=False)
    x = nc.dram_tensor("x", [BPC * T, D], F32, kind="ExternalInput").ap()
    wq = nc.dram_tensor("wq", [D, D], F32, kind="ExternalInput").ap()
    wk = nc.dram_tensor("wk", [D, 512], F32, kind="ExternalInput").ap()
    wv = nc.dram_tensor("wv", [D, 512], F32, kind="ExternalInput").ap()
    wo = nc.dram_tensor("wo", [D, D], F32, kind="ExternalInput").ap()
    rc2 = nc.dram_tensor("rope_cos2", [T, 512], F32, kind="ExternalInput").ap()
    rs = nc.dram_tensor("rope_sin", [T, 256], F32, kind="ExternalInput").ap()
    rn = nc.dram_tensor("rope_nsin", [T, 256], F32, kind="ExternalInput").ap()
    x1 = nc.dram_tensor("x1", [BPC * T, D], F32, kind="ExternalOutput").ap()
    xln = nc.dram_tensor("xln", [BPC * T, D], BF16).ap()
    scr_q = nc.dram_tensor("scr_q", [T, D], BF16).ap()
    scr_k = nc.dram_tensor("scr_k", [T, 1024], BF16).ap()
    em = Em(nc)
    sb = alloc_l0_attn_sbuf(em)
    aps = {
        "x": [x[i * T:(i + 1) * T, :] for i in range(BPC)],
        "xln": [xln[i * T:(i + 1) * T, :] for i in range(BPC)],
        "x1": [x1[i * T:(i + 1) * T, :] for i in range(BPC)],
        "scr_q": scr_q, "scr_k": scr_k,
        "wq": wq, "wk": wk, "wv": wv, "wo": wo,
        "rope_cos2": rc2, "rope_sin": rs, "rope_nsin": rn,
    }
    emit_l0_attn(em, aps, sb)
    em.emit()
    return nc



def alloc_l0_ffn_sbuf(em):
    nc = em.nc
    st = em.stack
    sb = {}
    sb["bigA"] = st.enter_context(nc.sbuf_tensor("bigA", [128, 32768], BF16))
    sb["bigB"] = st.enter_context(nc.sbuf_tensor("bigB", [128, 32768], BF16))
    sb["egbuf"] = st.enter_context(nc.sbuf_tensor("egbuf", [128, 8192], BF16))
    sb["sitg"] = st.enter_context(nc.sbuf_tensor("sitg", [128, 8192], BF16))
    sb["w1buf"] = [st.enter_context(nc.sbuf_tensor("w1b0", [128, 4096], BF16))] * 2
    sb["w2buf"] = [st.enter_context(nc.sbuf_tensor("w2b0", [128, 2048], BF16))] * 2
    sb["lnc"] = [st.enter_context(nc.sbuf_tensor("lnc0", [128, 512], F32))]
    sb["lncol"] = st.enter_context(nc.sbuf_tensor("lncolB", [128, 32], F32))
    sb["etile"] = [st.enter_context(nc.sbuf_tensor(f"et{j}", [128, 512], BF16)) for j in range(2)]
    sb["h_sb"] = st.enter_context(nc.sbuf_tensor("h_sb", [128, 512], BF16))
    sb["hT"] = st.enter_context(nc.sbuf_tensor("hT", [128, 512], BF16))
    sb["lt2"] = [st.enter_context(nc.sbuf_tensor("lt2z", [128, 512], BF16))]
    sb["lt3"] = [st.enter_context(nc.sbuf_tensor("lt3z", [128, 512], BF16))]
    sb["intm"] = st.enter_context(nc.sbuf_tensor("intm", [1, 512], F32))
    sb["mrowb"] = st.enter_context(nc.sbuf_tensor("mrowb", [1, 512], BF16))
    sb["rbc"] = st.enter_context(nc.sbuf_tensor("rbc", [128, 512], BF16))
    sb["grow"] = st.enter_context(nc.sbuf_tensor("grow", [1, 512], BF16))
    sb["gbc"] = st.enter_context(nc.sbuf_tensor("gbc", [128, 2048], BF16))
    sb["bbc"] = st.enter_context(nc.sbuf_tensor("bbc", [128, 2048], BF16))
    sb["ones1"] = st.enter_context(nc.sbuf_tensor("ones1B", [1, 128], BF16))
    sb["onecol"] = st.enter_context(nc.sbuf_tensor("onecolB", [128, 1], BF16))
    sb["scl"] = st.enter_context(nc.sbuf_tensor("scl", [1, 4], F32))
    sb["scol"] = st.enter_context(nc.sbuf_tensor("scol", [128, 4], F32))
    sb["fcol"] = st.enter_context(nc.sbuf_tensor("fcol", [128, 40], F32))
    sb["rscol"] = st.enter_context(nc.sbuf_tensor("rscol", [128, 128], F32))
    sb["csrow"] = st.enter_context(nc.sbuf_tensor("csrow", [1, 2048], BF16))
    sb["rdcol"] = st.enter_context(nc.sbuf_tensor("rdcol", [128, 16], F32))
    sb["ident"] = st.enter_context(nc.sbuf_tensor("identB", [128, 128], BF16))
    sb["b1row"] = st.enter_context(nc.sbuf_tensor("b1row", [1, 512], BF16))
    sb["b2row"] = st.enter_context(nc.sbuf_tensor("b2row", [1, 512], BF16))
    sb["x1c"] = [st.enter_context(nc.sbuf_tensor("x1cB0", [128, 512], F32))] * 2
    sb["x2c"] = [st.enter_context(nc.sbuf_tensor(f"x2cB{j}", [128, 512], F32)) for j in range(2)]
    sb["psA"] = [st.enter_context(nc.psum_tensor(f"psA{j}", [128, 512], F32)) for j in range(2)]
    sb["psB"] = [st.enter_context(nc.psum_tensor(f"psB{j}", [128, 512], F32)) for j in range(2)]
    sb["pcol"] = st.enter_context(nc.psum_tensor("pcol", [1, 512], F32))
    sb["pcc"] = st.enter_context(nc.psum_tensor("pcc", [128, 1], F32))
    sb["pt"] = st.enter_context(nc.psum_tensor("pt", [128, 128], BF16))
    return sb


def emit_l0_ffn(em, aps, sb):
    E_EXP = 16

    ident_t = make_identity_bf16(em, sb["ident"])
    ones_t = em.op("pool", lambda g: g.memset(sb["ones1"][:, :], 1.0))
    onec_t = em.op("pool", lambda g: g.memset(sb["onecol"][:, :], 1.0))

    ps_reader = {}

    def psdep(name):
        return ps_reader.get(name)

    def psread(name, tok):
        ps_reader[name] = tok

    def _row_ap(ap, n):
        return bass.AP(ap.tensor, ap.offset, [[n, 1], [1, n]])
    s_t = em.dma("sp", lambda e: e.dma_start(sb["scl"][:, 0:1], _row_ap(aps["scale"], 1)))
    bc_toks = []
    ldprev = None
    for (wap, dst) in (("gamma", "gbc"), ("beta", "bbc")):
        for c in range(4):
            lr = em.dma("pool", lambda g, wap=wap, c=c: g.dma_start(
                sb["grow"][:, :],
                bass.AP(aps[wap].tensor, aps[wap].offset + 512 * c, [[512, 1], [1, 512]])),
                deps=[ldprev])
            mm = em.op("pe", lambda e: e.matmul(
                sb["psA"][0][:, :], sb["ones1"][:, :], sb["grow"][:, :],
                start=True, stop=True), deps=[lr, ones_t, psdep("psA0")])
            ldprev = mm
            cp = em.op("dve", lambda e, dst=dst, c=c: e.tensor_copy(
                sb[dst][:, 512 * c:512 * (c + 1)], sb["psA"][0][:, :]), deps=[mm])
            psread("psA0", cp)
            bc_toks.append(cp)
    sb16 = em.op("dve", lambda e: e.tensor_copy(sb["scl"][:, 1:2], sb["scl"][:, 0:1]),
                 deps=[s_t])
    sbf = em.op("act", lambda e: e.copy(sb["mrowb"][:, 0:1], sb["scl"][:, 0:1]), deps=[s_t])
    smm = em.op("pe", lambda e: e.matmul(
        sb["pcc"][:, :], sb["ones1"][:, :], sb["mrowb"][:, 0:1],
        start=True, stop=True), deps=[sbf, ones_t, psdep("pcc")])
    sc0 = em.op("dve", lambda e: e.tensor_copy(sb["scol"][:, 0:1], sb["pcc"][:, :]), deps=[smm])
    psread("pcc", sc0)
    sc1a = em.op("dve", lambda e: e.tensor_scalar_mul(
        sb["scol"][:, 3:4], sb["scol"][:, 0:1], -1.0), deps=[sc0])
    sc1 = em.op("dve", lambda e: e.tensor_max(
        sb["scol"][:, 1:2], sb["scol"][:, 0:1], sb["scol"][:, 3:4]), deps=[sc1a])
    sc2 = em.op("dve", lambda e: e.tensor_scalar_mul(
        sb["scol"][:, 2:3], sb["scol"][:, 1:2], -1.0), deps=[sc1])

    prev = {}
    for i in range(BPC):
        ln_done = []
        for t in range(NT):
            sums = []
            for c in range(4):
                j = 0
                ld = em.dma("sp", lambda e, i=i, t=t, c=c, j=j: e.dma_start(
                    sb["lnc"][j][:, :],
                    aps["x1"][i][128 * t:128 * (t + 1), 512 * c:512 * (c + 1)]),
                    deps=[prev.get(("lnld", j))])
                s0 = em.op("dve", lambda e, j=j, c=c: e.reduce_sum(
                    sb["lncol"][:, c:c + 1], sb["lnc"][j][:, :], axis=AX.X),
                    deps=[ld, prev.get("ln_tail")])
                prev[("lnld", j)] = s0
                sums.append(s0)
            a1 = em.op("dve", lambda e: e.tensor_add(
                sb["lncol"][:, 4:5], sb["lncol"][:, 0:1], sb["lncol"][:, 1:2]), deps=sums[:2])
            a2 = em.op("dve", lambda e: e.tensor_add(
                sb["lncol"][:, 5:6], sb["lncol"][:, 2:3], sb["lncol"][:, 3:4]), deps=sums[2:])
            a3 = em.op("dve", lambda e: e.tensor_add(
                sb["lncol"][:, 6:7], sb["lncol"][:, 4:5], sb["lncol"][:, 5:6]), deps=[a1, a2])
            negm = em.op("dve", lambda e: e.tensor_scalar_mul(
                sb["lncol"][:, 7:8], sb["lncol"][:, 6:7], -1.0 / D), deps=[a3])
            sqs = []
            for c in range(4):
                j = 0
                ld = em.dma("sp", lambda e, i=i, t=t, c=c, j=j: e.dma_start(
                    sb["lnc"][j][:, :],
                    aps["x1"][i][128 * t:128 * (t + 1), 512 * c:512 * (c + 1)]),
                    deps=[prev.get(("lnld", j))])
                sq = em.op("act", lambda e, j=j, c=c: e.activation(
                    sb["lt2"][j][:, :], sb["lnc"][j][:, :], AF.Square,
                    bias=sb["lncol"][:, 7:8], accum_out=sb["lncol"][:, 8 + c:9 + c]),
                    deps=[ld, negm, prev.get(("lnsq", j)), prev.get(("ln_st", j)),
                          prev.get(("ln_nq", j))])
                prev[("lnld", j)] = sq
                prev[("lnsq", j)] = sq
                sqs.append(sq)
            v1 = em.op("dve", lambda e: e.tensor_add(
                sb["lncol"][:, 12:13], sb["lncol"][:, 8:9], sb["lncol"][:, 9:10]), deps=sqs[:2])
            v2 = em.op("dve", lambda e: e.tensor_add(
                sb["lncol"][:, 13:14], sb["lncol"][:, 10:11], sb["lncol"][:, 11:12]), deps=sqs[2:])
            v3 = em.op("dve", lambda e: e.tensor_scalar(
                sb["lncol"][:, 14:15],
                sb["lncol"][:, 12:13], 1.0, None, ALU.add), deps=[v1])
            v4 = em.op("dve", lambda e: e.tensor_add(
                sb["lncol"][:, 15:16], sb["lncol"][:, 14:15], sb["lncol"][:, 13:14]), deps=[v3, v2])
            c0 = em.op("dve", lambda e: e.tensor_scalar(
                sb["lncol"][:, 16:17], sb["lncol"][:, 15:16], 1.0 / D, EPS_LN,
                ALU.mult, ALU.add), deps=[v4])
            c1 = em.op("dve", lambda e: e.reciprocal(
                sb["lncol"][:, 17:18], sb["lncol"][:, 16:17]), deps=[c0])
            c2 = em.op("dve", lambda e: e.scalar_tensor_tensor(
                sb["lncol"][:, 18:19], sb["lncol"][:, 15:16], 1.0 / D,
                sb["lncol"][:, 17:18], ALU.mult, ALU.mult), deps=[c1, v4])
            c3 = em.op("dve", lambda e: e.tensor_scalar(
                sb["lncol"][:, 19:20], sb["lncol"][:, 18:19], EPS_LN, None,
                ALU.add), deps=[c2])
            c4 = em.op("dve", lambda e: e.tensor_tensor(
                sb["lncol"][:, 20:21], sb["lncol"][:, 16:17], sb["lncol"][:, 19:20],
                ALU.mult), deps=[c3, c0])
            c5 = em.op("act", lambda e: e.activation(
                sb["lncol"][:, 21:22], sb["lncol"][:, 20:21], AF.Ln), deps=[c4])
            rho = em.op("act", lambda e: e.activation(
                sb["lncol"][:, 22:23], sb["lncol"][:, 21:22], AF.Exp, scale=-0.5), deps=[c5])
            nrm_parts = []
            for c in range(4):
                j = 0
                ld = em.dma("sp", lambda e, i=i, t=t, c=c, j=j: e.dma_start(
                    sb["lnc"][j][:, :],
                    aps["x1"][i][128 * t:128 * (t + 1), 512 * c:512 * (c + 1)]),
                    deps=[prev.get(("lnld", j))])
                y1 = em.op("dve", lambda e, j=j: e.tensor_scalar(
                    sb["lt2"][j][:, :], sb["lnc"][j][:, :],
                    sb["lncol"][:, 7:8], sb["lncol"][:, 22:23], ALU.add, ALU.mult),
                    deps=[ld, rho, prev.get(("ln_nq", j)), prev.get(("ln_st", j))])
                prev[("lnld", j)] = y1
                y2 = em.op("dve", lambda e, j=j, c=c: e.tensor_tensor(
                    sb["lt3"][j][:, :], sb["lt2"][j][:, :], sb["gbc"][:, 512 * c:512 * (c + 1)],
                    ALU.mult), deps=[y1, bc_toks[-1], prev.get(("ln_nq", j))])
                y3 = em.op("dve", lambda e, j=j, c=c: e.tensor_tensor(
                    sb["lt2"][j][:, :], sb["lt3"][j][:, :], sb["bbc"][:, 512 * c:512 * (c + 1)],
                    ALU.add), deps=[y2])
                nq = em.op("act", lambda e, j=j, c=c: e.activation(
                    sb["lt3"][j][:, :], sb["lt2"][j][:, :], AF.Square,
                    accum_out=sb["lncol"][:, 24 + c:25 + c]), deps=[y3])
                prev[("ln_nq", j)] = nq
                nrm_parts.append(nq)
                st_ = em.dma("sp", lambda e, i=i, t=t, c=c, j=j: e.dma_start(
                    aps["xln"][i][128 * t:128 * (t + 1), 512 * c:512 * (c + 1)],
                    sb["lt2"][j][:, :]), deps=[y3])
                prev[("ln_st", j)] = st_
                ln_done.append(st_)
            n1 = em.op("dve", lambda e: e.tensor_add(
                sb["lncol"][:, 28:29], sb["lncol"][:, 24:25], sb["lncol"][:, 25:26]),
                deps=nrm_parts[:2])
            n2 = em.op("dve", lambda e: e.tensor_add(
                sb["lncol"][:, 29:30], sb["lncol"][:, 26:27], sb["lncol"][:, 27:28]),
                deps=nrm_parts[2:])
            n3 = em.op("dve", lambda e: e.tensor_add(
                sb["lncol"][:, 30:31], sb["lncol"][:, 28:29], sb["lncol"][:, 29:30]),
                deps=[n1, n2])
            f1 = em.op("act", lambda e: e.activation(
                sb["lncol"][:, 31:32], sb["lncol"][:, 30:31], AF.Sqrt), deps=[n3])
            f2 = em.op("dve", lambda e: e.tensor_scalar(
                sb["lncol"][:, 23:24], sb["lncol"][:, 31:32], 1e-12, None,
                ALU.max), deps=[f1])
            f3 = em.op("dve", lambda e: e.reciprocal(
                sb["lncol"][:, 23:24], sb["lncol"][:, 23:24]), deps=[f2])
            f4 = em.op("dve", lambda e, i=i, t=t: e.tensor_tensor(
                sb["fcol"][:, i * NT + t:i * NT + t + 1], sb["lncol"][:, 23:24],
                sb["scol"][:, 0:1], ALU.mult), deps=[f3, sc0])
            prev[("fac", t)] = f4
            prev["ln_tail"] = f4
        mu_war = prev.get("bigB_free")
        mload = []
        for dc in range(DC):
            mload.append(em.dma("pool", lambda g, dc=dc: g.dma_start(
                sb["bigB"][:, 2048 * dc:2048 * (dc + 1)],
                aps["mu"][128 * dc:128 * (dc + 1), :]), deps=[mu_war]))
        mun_done = []
        sit_war = [prev.get(("sitg_free", k)) for k in range(DC)]
        for ec in range(4):
            mmt = None
            for dc in range(DC):
                sl = dc % 4
                ms = em.op("dve", lambda e, dc=dc, ec=ec, sl=sl: e.tensor_tensor(
                    sb["sitg"][:, 512 * sl:512 * (sl + 1)],
                    sb["bigB"][:, 2048 * dc + 512 * ec:2048 * dc + 512 * (ec + 1)],
                    sb["bigB"][:, 2048 * dc + 512 * ec:2048 * dc + 512 * (ec + 1)],
                    ALU.mult), deps=[mload[dc], prev.get(("stg_free", sl))] + sit_war)
                mmt = em.op("pe", lambda e, dc=dc, sl=sl: e.matmul(
                    sb["pcol"][:, :], sb["onecol"][:, :],
                    sb["sitg"][:, 512 * sl:512 * (sl + 1)],
                    start=(dc == 0), stop=(dc == DC - 1)),
                    deps=[ms, onec_t] + ([psdep("pcol")] if dc == 0 else []))
                prev[("stg_free", sl)] = mmt
            sit_war = []
            r1 = em.op("act", lambda e: e.activation(
                sb["intm"][:, :], sb["pcol"][:, :], AF.Ln),
                deps=[mmt, prev.get("intm_free")])
            psread("pcol", r1)
            r2 = em.op("act", lambda e: e.activation(
                sb["mrowb"][:, :], sb["intm"][:, :], AF.Exp, scale=-0.5),
                deps=[r1, prev.get("mrowb_free")])
            prev["intm_free"] = r2
            bmm = em.op("pe", lambda e: e.matmul(
                sb["psA"][0][:, :], sb["ones1"][:, :], sb["mrowb"][:, :],
                start=True, stop=True), deps=[r2, ones_t, psdep("psA0")])
            prev["mrowb_free"] = bmm
            rcp = em.op("dve", lambda e: e.tensor_copy(
                sb["rbc"][:, :], sb["psA"][0][:, :]), deps=[bmm, prev.get("rbc_free")])
            psread("psA0", rcp)
            for dc in range(DC):
                sl = dc % 4
                t1 = em.op("dve", lambda e, dc=dc, ec=ec, sl=sl: e.tensor_tensor(
                    sb["sitg"][:, 512 * sl:512 * (sl + 1)],
                    sb["bigB"][:, 2048 * dc + 512 * ec:2048 * dc + 512 * (ec + 1)],
                    sb["rbc"][:, :], ALU.mult),
                    deps=[rcp, prev.get(("stg_free", sl))])
                t2 = em.op("dve", lambda e, dc=dc, ec=ec, sl=sl: e.tensor_copy(
                    sb["bigB"][:, 2048 * dc + 512 * ec:2048 * dc + 512 * (ec + 1)],
                    sb["sitg"][:, 512 * sl:512 * (sl + 1)]), deps=[t1])
                prev[("stg_free", sl)] = t2
                mun_done.append(t2)
            prev["rbc_free"] = t1
        if i == 0:
            prev["mun_st"] = []
            for dc in range(DC):
                prev["mun_st"].append(em.dma("sp", lambda e, dc=dc: e.dma_start(
                    bass.AP(aps["scr_mun"].tensor, aps["scr_mun"].offset + dc * 128 * 2048,
                            [[2048, 128], [1, 2048]]),
                    sb["bigB"][:, 2048 * dc:2048 * (dc + 1)]), deps=mun_done))

        xlnT = []
        for j in range(DC):
            dep = ln_done + ([prev.get("bigA_free")] if j == 0 else [])
            xlnT.append(em.dma("sp" if j % 2 == 0 else "act",
                               lambda e, j=j, i=i: e.dma_start(
                sb["bigA"][:, 2048 * j:2048 * (j + 1)],
                aps["xln"][i][:, 128 * j:128 * (j + 1)], transpose=True), deps=dep))
        csum_done = []
        e_stores = []
        for ec in range(4):
            cs_mm = None
            cs_pend = None
            for nt in range(NT):
                psl = nt % 2
                mm = None
                for dc in range(DC):
                    mm = em.op("pe", lambda e, dc=dc, nt=nt, ec=ec, psl=psl,
                               st_=(dc == 0), sp_=(dc == DC - 1): e.matmul(
                        sb["psA"][psl][:, :],
                        sb["bigA"][:, 2048 * dc + 128 * nt:2048 * dc + 128 * (nt + 1)],
                        sb["bigB"][:, 2048 * dc + 512 * ec:2048 * dc + 512 * (ec + 1)],
                        start=st_, stop=sp_),
                        deps=(xlnT + mun_done + [psdep(f"psA{psl}")]) if dc == 0 else [])
                if cs_pend is not None:
                    cs_mm = cs_pend()
                    cs_pend = None
                et = em.op("act", lambda e, nt=nt, ec=ec, psl=psl, i=i: e.activation(
                    sb["etile"][psl][:, :], sb["psA"][psl][:, :], AF.Exp,
                    bias=sb["scol"][:, 2:3], scale=sb["fcol"][:, i * NT + nt:i * NT + nt + 1],
                    accum_out=sb["rscol"][:, 4 * nt + ec:4 * nt + ec + 1]),
                    deps=[mm, sc2, prev.get(("fac", nt)), prev.get(("etile_free", psl)),
                          prev.get(("rsa", nt)), prev.get(("rr_war", nt))])
                psread(f"psA{psl}", et)
                est = em.dma("act", lambda e, nt=nt, ec=ec, psl=psl: e.dma_start(
                    aps["scr_e"][128 * nt:128 * (nt + 1), 512 * ec:512 * (ec + 1)],
                    sb["etile"][psl][:, :]),
                    deps=[et] + (prev.get("scr_e_readers", []) if (nt == 0 and ec == 0) else []))
                e_stores.append(est)

                def _mk_cs(psl=psl, nt=nt, et=et, est=est):
                    tok = em.op("pe", lambda e, psl=psl, nt=nt,
                                st_=(nt == 0), sp_=(nt == NT - 1): e.matmul(
                        sb["pcol"][:, :], sb["onecol"][:, :], sb["etile"][psl][:, :],
                        start=st_, stop=sp_),
                        deps=[et, onec_t] + ([psdep("pcol")] if nt == 0 else []))
                    prev[("etile_free", psl)] = [tok, est]
                    return tok
                cs_pend = _mk_cs
            cs_mm = cs_pend()
            cv = em.op("dve", lambda e, ec=ec: e.tensor_copy(
                sb["csrow"][:, 512 * ec:512 * (ec + 1)], sb["pcol"][:, :]),
                deps=[cs_mm, prev.get("csrow_free")])
            psread("pcol", cv)
            csum_done.append(cv)
        prev["csrow_free"] = csum_done[-1]
        rr_done = {}
        for nt in range(NT):
            a1 = em.op("dve", lambda e, nt=nt: e.tensor_add(
                sb["rscol"][:, 64 + nt:65 + nt],
                sb["rscol"][:, 4 * nt:4 * nt + 1], sb["rscol"][:, 4 * nt + 1:4 * nt + 2]),
                deps=[psdep("psA0"), psdep("psA1"), prev.get(("rr_war", nt))])
            a2 = em.op("dve", lambda e, nt=nt: e.tensor_add(
                sb["rscol"][:, 80 + nt:81 + nt],
                sb["rscol"][:, 4 * nt + 2:4 * nt + 3], sb["rscol"][:, 4 * nt + 3:4 * nt + 4]),
                deps=[prev.get(("rr_war", nt))])
            a3 = em.op("dve", lambda e, nt=nt: e.tensor_add(
                sb["rscol"][:, 96 + nt:97 + nt],
                sb["rscol"][:, 64 + nt:65 + nt], sb["rscol"][:, 80 + nt:81 + nt]),
                deps=[a1, a2, prev.get(("rr_war2", nt))])
            rr = em.op("dve", lambda e, nt=nt: e.reciprocal(
                sb["rscol"][:, 112 + nt:113 + nt], sb["rscol"][:, 96 + nt:97 + nt]), deps=[a3])
            prev[("rr_war2", nt)] = rr
            prev[("rsa", nt)] = rr
            prev[("rr_war", nt)] = rr
            rr_done[nt] = rr
        csr_st = em.dma("sp", lambda e: e.dma_start(aps["scr_row"], sb["csrow"][:, :]),
                        deps=csum_done + [prev.get("csrow_free2")])
        csr_ld = em.dma("pool", lambda e: e.dma_start(
            sb["rdcol"][:, :],
            bass.AP(aps["scr_row"].tensor, aps["scr_row"].offset, [[1, 128], [128, 16]]),
            allow_slow_non_contiguous=True), deps=[csr_st, prev.get("rdcol_free")])
        rdc = em.op("dve", lambda e: e.reciprocal(sb["rdcol"][:, :], sb["rdcol"][:, :]),
                    deps=[csr_ld])
        prev["csrow_free2"] = csr_ld

        xln_ld = []
        for j in range(NT):
            dep = ([psdep("psA0"), psdep("psA1")] if j == 0 else [])
            xln_ld.append(em.dma("sp" if j % 2 == 0 else "act",
                                 lambda e, j=j, i=i: e.dma_start(
                sb["bigA"][:, 2048 * j:2048 * (j + 1)],
                aps["xln"][i][128 * j:128 * (j + 1), :]), deps=dep + ln_done))
        so_evacs = []
        gel = h_mm = mm = None
        for eg in range(4):
            egl = []
            for ncx in range(NT):
                dep = [prev.get("egbuf_free")] if ncx == 0 else []
                egl.append(em.dma("sp" if ncx % 2 == 0 else "act",
                                  lambda e, ncx=ncx, eg=eg: e.dma_start(
                    sb["egbuf"][:, 512 * ncx:512 * (ncx + 1)],
                    aps["scr_e"][128 * ncx:128 * (ncx + 1), 512 * eg:512 * (eg + 1)]),
                    deps=dep + e_stores))
            sit_done = []
            for dt in range(DC):
                psl = dt % 2
                mm = None
                for ncx in range(NT):
                    mm = em.op("pe", lambda e, ncx=ncx, dt=dt, psl=psl,
                               st_=(ncx == 0), sp_=(ncx == NT - 1): e.matmul(
                        sb["psB"][psl][:, :],
                        sb["bigA"][:, 2048 * ncx + 128 * dt:2048 * ncx + 128 * (dt + 1)],
                        sb["egbuf"][:, 512 * ncx:512 * (ncx + 1)],
                        start=st_, stop=sp_),
                        deps=(xln_ld + egl + [psdep(f"psB{psl}")]) if ncx == 0 else [])
                ev = em.op("dve", lambda e, dt=dt, psl=psl: e.tensor_copy(
                    sb["sitg"][:, 512 * dt:512 * (dt + 1)], sb["psB"][psl][:, :]),
                    deps=[mm, prev.get(("sitg_free", dt))])
                psread(f"psB{psl}", ev)
                sit_done.append(ev)
            prev["egbuf_free"] = mm
            for el in range(4):
                e_ = 4 * eg + el
                h_mm = None
                for half in range(2):
                    jb = half
                    w1l = em.dma("pool", lambda g, e_=e_, half=half, jb=jb: g.dma_start(
                        sb["w1buf"][jb][:, :],
                        bass.AP(aps["w1"].tensor,
                                aps["w1"].offset + e_ * D * 512 + half * 8 * 128 * 512,
                                [[512, 128], [128 * 512, 8], [1, 512]])),
                        deps=[prev.get("w1_free")])
                    for k in range(8):
                        dc = half * 8 + k
                        h_mm = em.op("pe", lambda e, dc=dc, el=el, jb=jb, k=k,
                                     st_=(dc == 0): e.matmul(
                            sb["psA"][0][:, :],
                            sb["sitg"][:, 512 * dc + 128 * el:512 * dc + 128 * (el + 1)],
                            sb["w1buf"][jb][:, 512 * k:512 * (k + 1)],
                            start=st_, stop=False),
                            deps=[w1l] + ((sit_done + [psdep("psA0")]) if dc == 0 else []))
                    prev["w1_free"] = h_mm
                for dtx in range(DC):
                    prev[("sitg_free", dtx)] = h_mm
                b1l = em.dma("pool", lambda g, e_=e_: g.dma_start(
                    sb["b1row"][:, :],
                    bass.AP(aps["b1"].tensor, aps["b1"].offset + 512 * e_, [[512, 1], [1, 512]])),
                    deps=[prev.get("b1row_free")])
                h_mm = em.op("pe", lambda e, e_=e_: e.matmul(
                    sb["psA"][0][:, :],
                    sb["csrow"][:, 128 * e_:128 * (e_ + 1)],
                    sb["b1row"][:, :], start=False, stop=True),
                    deps=[b1l, h_mm] + csum_done)
                prev["b1row_free"] = h_mm
                gel = em.op("act", lambda e, e_=e_: e.activation(
                    sb["h_sb"][:, :], sb["psA"][0][:, :], AF.Gelu,
                    scale=sb["rdcol"][:, e_:e_ + 1]),
                    deps=[h_mm, rdc, prev.get("h_sb_free")])
                psread("psA0", gel)
                tts = []
                for hc in range(4):
                    tt = em.op("pe", lambda e, hc=hc: e.transpose(
                        sb["pt"][:, :], sb["h_sb"][:, 128 * hc:128 * (hc + 1)],
                        sb["ident"][:, :]),
                        deps=[gel, ident_t, psdep("pt")])
                    tc_ = em.op("dve", lambda e, hc=hc: e.tensor_copy(
                        sb["hT"][:, 128 * hc:128 * (hc + 1)], sb["pt"][:, :]),
                        deps=[tt, prev.get(("hT_free", hc))])
                    psread("pt", tc_)
                    prev[("hT_free", hc)] = None
                    tts.append(tc_)
                prev["h_sb_free"] = tts[-1]
                for dc2 in range(4):
                    b2l = em.dma("pool", lambda g, e_=e_, dc2=dc2: g.dma_start(
                        sb["b2row"][:, :],
                        bass.AP(aps["b2"].tensor, aps["b2"].offset + 2048 * e_ + 512 * dc2,
                                [[512, 1], [1, 512]])),
                        deps=[prev.get("b2row_free")])
                    jb = dc2 % 2
                    w2l = em.dma("pool", lambda g, e_=e_, dc2=dc2, jb=jb: g.dma_start(
                        sb["w2buf"][jb][:, :],
                        bass.AP(aps["w2"].tensor,
                                aps["w2"].offset + e_ * 512 * D + 512 * dc2,
                                [[D, 128], [128 * D, 4], [1, 512]])),
                        deps=[prev.get("w2_free")])
                    so_mm = None
                    for hc in range(4):
                        so_mm = em.op("pe", lambda e, hc=hc, jb=jb, st_=(hc == 0): e.matmul(
                            sb["psB"][0][:, :],
                            sb["hT"][:, 128 * hc:128 * (hc + 1)],
                            sb["w2buf"][jb][:, 512 * hc:512 * (hc + 1)],
                            start=st_, stop=False),
                            deps=[w2l] + ((tts + [psdep("psB0")]) if hc == 0 else []))
                    so_mm = em.op("pe", lambda e: e.matmul(
                        sb["psB"][0][:, :], sb["ones1"][:, :],
                        sb["b2row"][:, :],
                        start=False, stop=True), deps=[b2l, so_mm])
                    prev["b2row_free"] = so_mm
                    prev["w2_free"] = so_mm
                    for hc in range(4):
                        prev[("hT_free", hc)] = so_mm
                    evo = em.op("dve", lambda e, e_=e_, dc2=dc2: e.tensor_copy(
                        sb["bigB"][:, 2048 * e_ + 512 * dc2:2048 * e_ + 512 * (dc2 + 1)],
                        sb["psB"][0][:, :]),
                        deps=[so_mm] + (mun_done if e_ == 0 and dc2 == 0 else []))
                    psread("psB0", evo)
                    so_evacs.append(evo)

        et_ld = []
        for j in range(NT):
            dep = ([psdep("psB0"), psdep("psB1"), mm] if j == 0 else [])
            et_ld.append(em.dma("sp" if j % 2 == 0 else "act",
                                lambda e, j=j: e.dma_start(
                sb["bigA"][:, 2048 * j:2048 * (j + 1)],
                aps["scr_e"][:, 128 * j:128 * (j + 1)], transpose=True),
                deps=dep + (e_stores if j == 0 else [])))
        x2_stores = []
        xc_free = prev.get("xc_freeB", [None, None])
        for nt in range(NT):
            for dc2 in range(4):
                psl = (nt * 4 + dc2) % 2
                xl = em.dma("sp", lambda e, i=i, nt=nt, dc2=dc2, psl=psl: e.dma_start(
                    sb["x1c"][psl][:, :],
                    aps["x1"][i][128 * nt:128 * (nt + 1), 512 * dc2:512 * (dc2 + 1)]),
                    deps=[xc_free[psl]])
                mm = None
                for ecc in range(NT):
                    mm = em.op("pe", lambda e, ecc=ecc, nt=nt, dc2=dc2, psl=psl,
                               st_=(ecc == 0), sp_=(ecc == NT - 1): e.matmul(
                        sb["psA"][psl][:, :],
                        sb["bigA"][:, 2048 * ecc + 128 * nt:2048 * ecc + 128 * (nt + 1)],
                        sb["bigB"][:, 2048 * ecc + 512 * dc2:2048 * ecc + 512 * (dc2 + 1)],
                        start=st_, stop=sp_),
                        deps=(et_ld + so_evacs + [psdep(f"psA{psl}")]) if ecc == 0 else [])
                ev = em.op("dve", lambda e, nt=nt, psl=psl: e.scalar_tensor_tensor(
                    sb["x2c"][psl][:, :], sb["psA"][psl][:, :],
                    sb["rscol"][:, 112 + nt:113 + nt], sb["x1c"][psl][:, :],
                    ALU.mult, ALU.add),
                    deps=[mm, xl, rr_done[nt], prev.get(("x2c_free", psl))])
                psread(f"psA{psl}", ev)
                xc_free[0] = ev
                xc_free[1] = ev
                dmo = em.dma("sp", lambda e, i=i, nt=nt, dc2=dc2, psl=psl: e.dma_start(
                    aps["x2"][i][128 * nt:128 * (nt + 1), 512 * dc2:512 * (dc2 + 1)],
                    sb["x2c"][psl][:, :]), deps=[ev])
                prev[("x2c_free", psl)] = dmo
                x2_stores.append(dmo)
        prev["xc_freeB"] = xc_free
        prev["bigA_free"] = mm
        prev["bigB_free"] = mm
        prev["scr_e_readers"] = et_ld + egl
        prev["csrowb_free"] = h_mm
        prev["rdcol_free"] = gel
        for dt in range(DC):
            prev[("sitg_free", dt)] = h_mm
    return x2_stores


def build_l0_ffn_program():
    nc = bass.Bass("TRN2", target_bir_lowering=False)
    x1 = nc.dram_tensor("x1", [BPC * T, D], F32, kind="ExternalInput").ap()
    mu = nc.dram_tensor("mu", [D, 2048], F32, kind="ExternalInput").ap()
    gamma = nc.dram_tensor("gamma", [D], F32, kind="ExternalInput").ap()
    beta = nc.dram_tensor("beta", [D], F32, kind="ExternalInput").ap()
    scale = nc.dram_tensor("scale", [1], F32, kind="ExternalInput").ap()
    w1 = nc.dram_tensor("w1", [16, D, 512], F32, kind="ExternalInput").ap()
    b1 = nc.dram_tensor("b1", [16, 512], F32, kind="ExternalInput").ap()
    w2 = nc.dram_tensor("w2", [16, 512, D], F32, kind="ExternalInput").ap()
    b2 = nc.dram_tensor("b2", [16, D], F32, kind="ExternalInput").ap()
    x2 = nc.dram_tensor("x2", [BPC * T, D], F32, kind="ExternalOutput").ap()
    xln = nc.dram_tensor("xlnB", [BPC * T, D], BF16).ap()
    scr_e = nc.dram_tensor("scr_e", [T, 2048], BF16).ap()
    scr_row = nc.dram_tensor("scr_row", [1, 2048], BF16).ap()
    scr_mun = nc.dram_tensor("scr_mun", [D * 2048], BF16).ap()
    em = Em(nc)
    sb = alloc_l0_ffn_sbuf(em)
    aps = {
        "x1": [x1[i * T:(i + 1) * T, :] for i in range(BPC)],
        "x2": [x2[i * T:(i + 1) * T, :] for i in range(BPC)],
        "xln": [xln[i * T:(i + 1) * T, :] for i in range(BPC)],
        "scr_e": scr_e, "scr_row": scr_row, "scr_mun": scr_mun,
        "mu": mu, "gamma": gamma, "beta": beta, "scale": scale,
        "w1": w1, "b1": b1, "w2": w2, "b2": b2,
    }
    emit_l0_ffn(em, aps, sb)
    em.emit()
    return nc



EPS_C = 1e-8
FD6 = 6


def alloc_l1_attn_sbuf(em):
    nc = em.nc
    st = em.stack
    sb = {}
    sb["bigA"] = st.enter_context(nc.sbuf_tensor("bigAC", [128, 32768], BF16))
    sb["wbuf"] = [st.enter_context(nc.sbuf_tensor("wbufC", [128, 8192], BF16))]
    sb["vbuf"] = st.enter_context(nc.sbuf_tensor("vbufC", [128, 8192], BF16))
    sb["mem"] = st.enter_context(nc.sbuf_tensor("memC", [128, 129 * FD6 * 16], BF16))
    sb["mkT"] = st.enter_context(nc.sbuf_tensor("mkTC", [128, 2048 * FD6], BF16))
    sb["wbuf"].append(sb["mkT"])
    sb["wmv4"] = st.enter_context(nc.sbuf_tensor("wmv4C", [128, 516 * NT], BF16))
    sb["mkt"] = [st.enter_context(nc.sbuf_tensor(f"mktC{j}", [128, 768], BF16)) for j in range(2)]
    sb["mb4"] = [st.enter_context(nc.sbuf_tensor(f"mb4C{j}", [128, 512], BF16)) for j in range(2)]
    sb["dstage"] = st.enter_context(nc.sbuf_tensor("dstageC", [128, 516], F32))
    sb["mv4"] = sb["dstage"]
    sb["rp"] = st.enter_context(nc.sbuf_tensor("rpC", [128, 1024], BF16))
    sb["junk"] = sb["rp"]
    sb["mst"] = st.enter_context(nc.sbuf_tensor("mstC", [128, 3072], BF16))
    sb["wst"] = sb["mst"]
    pass
    sb["rq"] = st.enter_context(nc.sbuf_tensor("rqC", [128, 640], F32))
    sb["rq2"] = st.enter_context(nc.sbuf_tensor("rq2C", [128, 640], F32))
    sb["colsc"] = st.enter_context(nc.sbuf_tensor("colscC", [128, 32], F32))
    sb["lnx"] = [st.enter_context(nc.sbuf_tensor("lnxC", [128, D], F32))]
    sb["lnxln"] = [st.enter_context(nc.sbuf_tensor("lnxlnC", [128, D], BF16))]
    sb["lncol"] = st.enter_context(nc.sbuf_tensor("lncolC", [128, 16], F32))
    sb["stats"] = st.enter_context(nc.sbuf_tensor("statsC", [128, 2 * NT * BPC], F32))
    sb["brow"] = st.enter_context(nc.sbuf_tensor("browC", [1, 512], BF16))
    sb["ones1"] = st.enter_context(nc.sbuf_tensor("ones1C", [1, 128], BF16))
    sb["x2c"] = [st.enter_context(nc.sbuf_tensor("x2cC0", [128, 512], F32))] * 2
    sb["mhc"] = [st.enter_context(nc.sbuf_tensor(f"mhcC{j}", [128, 512], BF16)) for j in range(2)]
    sb["gc"] = [st.enter_context(nc.sbuf_tensor("gcC0", [128, 512], BF16))] * 2
    sb["f1"] = st.enter_context(nc.sbuf_tensor("f1C", [128, 512], F32))
    sb["f2"] = st.enter_context(nc.sbuf_tensor("f2C", [128, 512], F32))
    sb["x3c"] = [st.enter_context(nc.sbuf_tensor("x3cC0", [128, 512], F32))] * 2
    sb["sg"] = [st.enter_context(nc.sbuf_tensor("sgC0", [128, 512], BF16))] * 2
    sb["psA"] = [st.enter_context(nc.psum_tensor(f"psAC{j}", [128, 512], F32)) for j in range(2)]
    sb["pden"] = st.enter_context(nc.psum_tensor("pdenC", [128, 16], F32))
    sb["pd"] = sb["psA"] + [st.enter_context(nc.psum_tensor(f"pdC{j}", [128, 512], F32)) for j in range(2, 6)]
    sb["pdn"] = st.enter_context(nc.psum_tensor("pdnC", [128, 32], F32))
    return sb


def emit_l1_attn(em, aps, sb, n_cores=8):
    P = {}
    ps_reader = {}

    def psdep(name):
        return ps_reader.get(name)

    def psread(name, tok):
        ps_reader[name] = tok

    ones_t = em.op("pool", lambda g: g.memset(sb["ones1"][:, :], 1.0))

    def psbuf_c2(psl):
        return sb["psA"][psl] if psl < 2 else sb["pd"][psl]

    def psnm_c2(psl):
        return f"psA{psl}" if psl < 2 else f"pd{psl}"

    def psdep_c2(psl):
        return ps_reader.get(psnm_c2(psl))

    wspec = [("wq", 2048, 0), ("wg", 2048, 2048 * 2048),
             ("wmb", 2048, 2 * 2048 * 2048), ("wk", 512, 3 * 2048 * 2048),
             ("wv", 512, 3 * 2048 * 2048 + 512 * 2048)]
    prep_done = []
    pst = None
    for (wn, rows, off) in wspec:
        for r0 in range(0, rows, 128):
            for half in range(2):
                ld = em.dma("pool", lambda g, wn=wn, r0=r0, half=half: g.dma_start(
                    sb["wst"][:, 0:1024], aps[wn][r0:r0 + 128, 1024 * half:1024 * (half + 1)]),
                    deps=[pst])
                st_ = em.dma("sp", lambda e, off=off, r0=r0, half=half: e.dma_start(
                    bass.AP(aps["scr_w"].tensor,
                            aps["scr_w"].offset + off + r0 * 2048 + 1024 * half,
                            [[2048, 128], [1, 1024]]),
                    sb["wst"][:, 0:1024]), deps=[ld])
                pst = st_
                prep_done.append(st_)
    P["mst_free"] = prep_done[-DMA_SLOTS:]
    mem_done = []
    for h in range(16):
        for fc in range(FD6):
            c0 = 129 * (FD6 * h + fc)
            m1 = em.dma("pool", lambda g, h=h, fc=fc, c0=c0: g.dma_start(
                sb["mem"][:, c0:c0 + 128],
                bass.AP(aps["memory"].tensor,
                        aps["memory"].offset + h * 768 * 128 + 128 * fc * 128,
                        [[128, 128], [1, 128]])))
            m2 = em.dma("pool", lambda g, h=h, fc=fc, c0=c0: g.dma_start(
                sb["mem"][:, c0 + 128:c0 + 129],
                bass.AP(aps["norm"].tensor, aps["norm"].offset + h * 768 + 128 * fc,
                        [[1, 128], [1, 1]])))
            mem_done += [m1, m2]

    kv_delta_evacs = []
    for i in range(BPC):
        ln_toks = emit_ln_pass(em, aps["x2"][i], aps["xln"][i], sb, i, store_stats=True)
        xlnT = []
        for j in range(DC):
            dep = ln_toks + ([P.get("bigA_free")] if j == 0 else [])
            xlnT.append(em.dma("sp", lambda e, j=j, i=i: e.dma_start(
                sb["bigA"][:, 2048 * j:2048 * (j + 1)],
                aps["xln"][i][:, 128 * j:128 * (j + 1)], transpose=True), deps=dep))

        def wchunk_load(woff, wrows, oc, jb, deps):
            toks = []
            for dcx in range(DC):
                toks.append(em.dma("sp", lambda e, woff=woff, oc=oc, dcx=dcx, jb=jb: e.dma_start(
                    sb["wbuf"][jb][:, 512 * dcx:512 * (dcx + 1)],
                    bass.AP(aps["scr_w"].tensor,
                            aps["scr_w"].offset + woff + 512 * oc * 2048 + 128 * dcx,
                            [[2048, 512], [1, 128]]), transpose=True),
                    deps=deps if dcx == 0 else []))
            return toks

        def bias_row(bname, oc, deps):
            return em.dma("pool", lambda g, bname=bname, oc=oc: g.dma_start(
                sb["brow"][:, :],
                bass.AP(aps[bname].tensor, aps[bname].offset + 512 * oc, [[512, 1], [1, 512]])),
                deps=deps)

        def proj_chunk(nt, jb, psl, deps0):
            mm = None
            for dcx in range(DC):
                mm = em.op("pe", lambda e, nt=nt, dcx=dcx, jb=jb, psl=psl,
                           st_=(dcx == 0): e.matmul(
                    psbuf_c2(psl)[:, 0:512],
                    sb["bigA"][:, 2048 * dcx + 128 * nt:2048 * dcx + 128 * (nt + 1)],
                    sb["wbuf"][jb][:, 512 * dcx:512 * (dcx + 1)],
                    start=st_, stop=False),
                    deps=deps0 if dcx == 0 else [])
            mm = em.op("pe", lambda e, psl=psl: e.matmul(
                psbuf_c2(psl)[:, 0:512], sb["ones1"][:, :], sb["brow"][:, :],
                start=False, stop=True), deps=[mm])
            return mm

        def dpfp_chain(nt, psl, mmtok, nheads4, scr_dst, rq_col0, junk_deps):
            t_rp1 = em.op("act", lambda e, psl=psl: e.activation(
                bass.AP(sb["rp"], 0, [[1024, 128], [256, 4], [1, 128]]),
                psbuf_c2(psl)[:, 0:512], AF.Relu), deps=[mmtok, P.get("rp_free")])
            t_rp2 = em.op("act", lambda e, psl=psl: e.activation(
                bass.AP(sb["rp"], 128, [[1024, 128], [256, 4], [1, 128]]),
                psbuf_c2(psl)[:, 0:512], AF.Relu, scale=-1.0), deps=[mmtok, P.get("rp_free")])
            prods = []
            for jr in range(1, 4):
                peng = "dve" if jr == 1 else "pool"
                pm = em.op(peng, lambda e, jr=jr: e.tensor_tensor(
                    bass.AP(sb["mst"], (jr - 1) * 256 + jr, [[3072, 128], [768, 4], [1, 256 - jr]]),
                    bass.AP(sb["rp"], jr, [[1024, 128], [256, 4], [1, 256 - jr]]),
                    bass.AP(sb["rp"], 0, [[1024, 128], [256, 4], [1, 256 - jr]]),
                    ALU.mult), deps=[t_rp1, t_rp2, P.get("mst_free")])
                pw = em.op(peng, lambda e, jr=jr: e.tensor_tensor(
                    bass.AP(sb["mst"], (jr - 1) * 256, [[3072, 128], [768, 4], [1, jr]]),
                    bass.AP(sb["rp"], 0, [[1024, 128], [256, 4], [1, jr]]),
                    bass.AP(sb["rp"], 256 - jr, [[1024, 128], [256, 4], [1, jr]]),
                    ALU.mult), deps=[t_rp1, t_rp2, P.get("mst_free")])
                prods += [pm, pw]
            rtoks = []
            for hh in range(4):
                if hh % 2 == 0:
                    sq = em.op("dve", lambda e, hh=hh: e.scalar_tensor_tensor(
                        sb["junk"][:, 0:768], sb["mst"][:, 768 * hh:768 * (hh + 1)], 1.0,
                        sb["mst"][:, 768 * hh:768 * (hh + 1)], ALU.bypass, ALU.mult,
                        accum_out=sb["colsc"][:, hh:hh + 1]),
                        deps=prods + [P.get("junk_free"), P.get(("colsc", hh))])
                else:
                    sq = em.op("act", lambda e, hh=hh: e.activation(
                        sb["junk"][:, 0:768], sb["mst"][:, 768 * hh:768 * (hh + 1)], AF.Square,
                        accum_out=sb["colsc"][:, hh:hh + 1]),
                        deps=prods + [P.get("junk_free"), P.get(("colsc", hh))])
                P["junk_free"] = sq
                s1 = em.op("act", lambda e, hh=hh: e.activation(
                    sb["colsc"][:, 4 + hh:5 + hh], sb["colsc"][:, hh:hh + 1], AF.Sqrt),
                    deps=[sq])
                s2 = em.op("dve", lambda e, hh=hh: e.tensor_scalar(
                    sb["colsc"][:, 8 + hh:9 + hh], sb["colsc"][:, 4 + hh:5 + hh],
                    1e-12, None, ALU.max), deps=[s1])
                cr = rq_col0 + hh
                s3 = em.op("dve", lambda e, hh=hh, cr=cr: e.tensor_copy(
                    sb["rq2"][:, cr:cr + 1], sb["colsc"][:, 8 + hh:9 + hh]), deps=[s2])
                s4 = em.op("dve", lambda e, hh=hh, cr=cr: e.reciprocal(
                    sb["rq"][:, cr:cr + 1], sb["colsc"][:, 8 + hh:9 + hh]), deps=[s2],
                    lowp=True)
                P[("colsc", hh)] = s4
                rtoks += [s3, s4]
            stm = em.dma("sp", lambda e: e.dma_start(scr_dst, sb["mst"][:, :]),
                         deps=prods)
            P["mst_free"] = [stm] + [t for t in rtoks]
            P["rp_free"] = rtoks + prods
            return stm, rtoks

        wl = wchunk_load(wspec[3][2], 512, 0, 0, [P.get(("wbuf_free", 0))] + xlnT)
        bl = bias_row("bk", 0, [P.get("brow_free")])
        mk_stores = []
        rk_done = []
        for nt in range(NT):
            psl = nt % 4
            mm = proj_chunk(nt, 0, psl, wl + [bl, psdep_c2(psl), ones_t])
            stm, rt = dpfp_chain(nt, psl, mm,
                                 4, aps["scr_mk"][i][128 * nt:128 * (nt + 1), :],
                                 i * 320 + 256 + 4 * nt, [])
            rk_done.extend(rt)
            psread(psnm_c2(psl), stm)
            mk_stores.append(stm)
        P["brow_free"] = mm
        P["wbuf_free"] = mm
        wl = wchunk_load(wspec[4][2], 512, 0, 1, [P.get(("wbuf_free", 1)), P.get("mkT_free")])
        bl = bias_row("bv", 0, [P.get("brow_free")])
        v_evs = []
        for nt in range(NT):
            psl = nt % 4
            mm = proj_chunk(nt, 1, psl, wl + [bl, psdep_c2(psl)])
            ev = em.op("dve", lambda e, nt=nt, psl=psl: e.tensor_copy(
                sb["vbuf"][:, 512 * nt:512 * (nt + 1)], psbuf_c2(psl)[:, :]),
                deps=[mm, P.get("vbuf_free")])
            psread(psnm_c2(psl), ev)
            v_evs.append(ev)
        P["brow_free"] = mm
        P[("wbuf_free", 1)] = mm
        mb_stores = []
        P[("mb_scr_rd", i)] = []
        for (wn, boff, bname, scr) in (("wmb", 2, "bmb", "scr_mb"), ("wg", 1, "bg", "scr_g")):
            woff = wspec[2][2] if wn == "wmb" else wspec[1][2]
            for oc in range(4):
                jbw = oc % 2
                wl = wchunk_load(woff, 2048, oc, jbw,
                                 [P.get(("wbuf_free", jbw))] + ([P.get("mkT_free")] if jbw == 1 else []))
                bl = bias_row(bname, oc, [P.get("brow_free")])
                for nt in range(NT):
                    psl = nt % 4
                    mm = proj_chunk(nt, jbw, psl, wl + [bl, psdep_c2(psl)])
                    sg = em.op("act", lambda e, psl=psl: e.activation(
                        sb["sg"][psl % 2][:, :], psbuf_c2(psl)[:, :], AF.Sigmoid),
                        deps=[mm, P.get("sg_free")])
                    psread(psnm_c2(psl), sg)
                    dmo = em.dma("sp", lambda e, i=i, nt=nt, oc=oc, psl=psl, scr=scr: e.dma_start(
                        aps[scr][i][128 * nt:128 * (nt + 1), 512 * oc:512 * (oc + 1)],
                        sb["sg"][psl % 2][:, :]),
                        deps=[sg] + (P.get(("mh_scr_rd", i), []) if (scr == "scr_mb" and nt == 0 and oc == 0) else []))
                    P["sg_free"] = dmo
                    if scr == "scr_mb":
                        mb_stores.append(dmo)
                P["brow_free"] = mm
                P[("wbuf_free", jbw)] = mm
        mq_stores = []
        rq_done = []
        P[("rq_done", i)] = rq_done
        P[("mq_stores", i)] = mq_stores
        for oc in range(4):
            jbw = oc % 2
            wl = wchunk_load(wspec[0][2], 2048, oc, jbw,
                             [P.get(("wbuf_free", jbw))] + ([P.get("mkT_free")] if jbw == 1 else []))
            bl = bias_row("bq", oc, [P.get("brow_free")])
            for nt in range(NT):
                psl = nt % 4
                mm = proj_chunk(nt, jbw, psl, wl + [bl, psdep_c2(psl)])
                stm, rt = dpfp_chain(
                    nt, psl, mm, 4,
                    aps["scr_mq"][i][128 * nt:128 * (nt + 1), 3072 * oc:3072 * (oc + 1)],
                    i * 320 + 16 * nt + 4 * oc, [])
                rq_done.extend(rt)
                psread(psnm_c2(psl), stm)
                mq_stores.append(stm)
            P["brow_free"] = mm
            P[("wbuf_free", jbw)] = mm
        P["bigA_free"] = mm
        for hp in range(4):
            mkT_toks = []
            for fc in range(FD6):
                mkT_toks.append(em.dma("sp" if fc % 2 == 0 else "act",
                                       lambda e, i=i, hp=hp, fc=fc: e.dma_start(
                    sb["mkT"][:, 2048 * fc:2048 * (fc + 1)],
                    aps["scr_mk"][i][:, 768 * hp + 128 * fc:768 * hp + 128 * (fc + 1)],
                    transpose=True),
                    deps=(mk_stores + [P.get("mkT_free"), P.get(("wbuf_free", 1))]) if fc < 2 else mk_stores))
            wmv_toks = []
            for nt in range(NT):
                psl = nt % 2
                mm = None
                for fc in range(FD6):
                    mm = em.op("pe", lambda e, hp=hp, nt=nt, fc=fc, psl=psl,
                               st_=(fc == 0), sp_=(fc == FD6 - 1): e.matmul(
                        sb["psA"][psl][:, :],
                        sb["mkT"][:, 2048 * fc + 128 * nt:2048 * fc + 128 * (nt + 1)],
                        bass.AP(sb["mem"], 129 * (FD6 * 4 * hp) + 129 * fc,
                                [[129 * FD6 * 16, 128], [129 * FD6, 4], [1, 128]]),
                        start=st_, stop=sp_),
                        deps=[mkT_toks[fc]] + ((mem_done + [psdep(f"psA{psl}")]) if fc == 0 else []))
                md = None
                for fc in range(FD6):
                    md = em.op("pe", lambda e, hp=hp, nt=nt, fc=fc,
                               st_=(fc == 0), sp_=(fc == FD6 - 1): e.matmul(
                        sb["pden"][:, 0:4],
                        sb["mkT"][:, 2048 * fc + 128 * nt:2048 * fc + 128 * (nt + 1)],
                        bass.AP(sb["mem"], 129 * (FD6 * 4 * hp) + 129 * fc + 128,
                                [[129 * FD6 * 16, 128], [129 * FD6, 4], [1, 1]]),
                        start=st_, stop=sp_),
                        deps=[psdep("pden")] if fc == 0 else [])
                mbl = em.dma("sp", lambda e, i=i, hp=hp, nt=nt, j=nt % 2: e.dma_start(
                    sb["mb4"][nt % 2][:, :],
                    aps["scr_mb"][i][128 * nt:128 * (nt + 1), 512 * hp:512 * (hp + 1)]),
                    deps=[P.get(("mb4_free", nt % 2))] + mb_stores)
                P.setdefault(("mb_rd_done", i), []).append(mbl)
                kc = i * 320 + 256 + 4 * nt + hp
                c2 = em.op("dve", lambda e, kc=kc: e.tensor_scalar(
                    sb["colsc"][:, 20:24], sb["pden"][:, 0:4],
                    sb["colsc"][:, 15:16], None, ALU.add),
                    deps=[md, P.get("colsc2_free"),
                          em.op("dve", lambda e2, kc=kc: e2.tensor_scalar(
                              sb["colsc"][:, 15:16], sb["rq2"][:, kc:kc + 1],
                              EPS_C, None, ALU.mult),
                              deps=rk_done + [P.get("colsc15_free")])])
                P["colsc15_free"] = c2
                c3 = em.op("dve", lambda e: e.reciprocal(
                    sb["colsc"][:, 24:28], sb["colsc"][:, 20:24]), deps=[c2])
                c4 = em.op("dve", lambda e, kc=kc: e.tensor_scalar(
                    sb["colsc"][:, 16:20], sb["pden"][:, 0:4],
                    sb["rq"][:, kc:kc + 1], EPS_C, ALU.mult, ALU.add),
                    deps=[md, c2] + rk_done)
                c5 = em.op("dve", lambda e: e.tensor_scalar(
                    sb["colsc"][:, 20:24], sb["colsc"][:, 16:20], 1.0, -1.0,
                    ALU.subtract, ALU.mult), deps=[c4, c3])
                c6 = em.op("dve", lambda e: e.tensor_scalar(
                    sb["colsc"][:, 16:20], sb["colsc"][:, 20:24], 0.0, 1.0,
                    ALU.max, ALU.min), deps=[c5])
                c7 = em.op("dve", lambda e, kc=kc: e.tensor_scalar(
                    sb["colsc"][:, 28:32], sb["colsc"][:, 16:20],
                    sb["rq"][:, kc:kc + 1], None, ALU.mult), deps=[c6])
                psread("pden", c5)
                mvts = []
                for hh in range(4):
                    ng = em.op("dve", lambda e, hh=hh: e.tensor_scalar_mul(
                        sb["colsc"][:, 16 + hh:17 + hh], sb["colsc"][:, 24 + hh:25 + hh],
                        -1.0), deps=[c3, c7])
                    mv = em.op("dve", lambda e, hh=hh, hp=hp, nt=nt, psl=psl: e.scalar_tensor_tensor(
                        sb["mv4"][:, 128 * hh:128 * (hh + 1)],
                        sb["psA"][psl][:, 128 * hh:128 * (hh + 1)],
                        sb["colsc"][:, 16 + hh:17 + hh],
                        sb["vbuf"][:, 512 * nt + 128 * hp:512 * nt + 128 * (hp + 1)],
                        ALU.mult, ALU.add),
                        deps=[ng, mm, P.get("mv4_free"), P.get("dstage_free")] + v_evs)
                    mvts.append(mv)
                psread(f"psA{psl}", mvts[-1])
                w1 = em.op("dve", lambda e, nt=nt: e.tensor_tensor(
                    sb["mv4"][:, 0:512], sb["mv4"][:, 0:512], sb["mb4"][nt % 2][:, :],
                    ALU.mult), deps=mvts + [mbl])
                P[("mb4_free", nt % 2)] = w1
                P["mv4_free"] = w1
                wts = []
                for hh in range(4):
                    kc = i * 320 + 256 + 4 * nt + hp
                    wv_ = em.op("dve", lambda e, hh=hh, nt=nt, kc=kc: e.tensor_scalar(
                        bass.AP(sb["wmv4"], 516 * nt + 129 * hh, [[516 * NT, 128], [1, 128]]),
                        sb["mv4"][:, 128 * hh:128 * (hh + 1)],
                        sb["rq"][:, kc:kc + 1], None, ALU.mult),
                        deps=[w1, P.get(("wmv4_free", nt))])
                    P["mv4_free"] = wv_
                    cc_ = em.op("dve", lambda e, hh=hh, nt=nt: e.tensor_copy(
                        bass.AP(sb["wmv4"], 516 * nt + 129 * hh + 128, [[516 * NT, 128], [1, 1]]),
                        sb["colsc"][:, 28 + hh:29 + hh]), deps=[c7, P.get(("wmv4_free", nt))])
                    wts += [wv_, cc_]
                P["colsc2_free"] = wts[-1]
                wmv_toks.append(wts)
                P[("wmv", nt)] = wts
            P["mkT_free"] = md
            dmm = {ft: None for ft in range(FD6)}
            dnm = None
            for nt in range(NT):
                j = nt % 2
                mkl = em.dma("sp", lambda e, i=i, hp=hp, nt=nt, j=j: e.dma_start(
                    sb["mkt"][j][:, :],
                    aps["scr_mk"][i][128 * nt:128 * (nt + 1), 768 * hp:768 * (hp + 1)]),
                    deps=[P.get(("mkt_free", j))] + mk_stores)
                for ft in range(FD6):
                    pnm = f"psA{ft}" if ft < 2 else f"pd{ft}"
                    dmm[ft] = em.op("pe", lambda e, ft=ft, nt=nt, j=j,
                                    st_=(nt == 0), sp_=(nt == NT - 1): e.matmul(
                        sb["pd"][ft][:, :],
                        sb["mkt"][j][:, 128 * ft:128 * (ft + 1)],
                        bass.AP(sb["wmv4"], 516 * nt, [[516 * NT, 128], [129, 4], [1, 128]]),
                        start=st_, stop=sp_),
                        deps=[mkl] + P[("wmv", nt)] + ([psdep(pnm)] if nt == 0 else []))
                for ft in range(FD6):
                    dnm = em.op("pe", lambda e, ft=ft, nt=nt, j=j,
                                st_=(nt == 0 and ft == 0),
                                sp_=(nt == NT - 1 and ft == FD6 - 1): e.matmul(
                        sb["pdn"][:, 4 * ft:4 * (ft + 1)],
                        sb["mkt"][j][:, 128 * ft:128 * (ft + 1)],
                        bass.AP(sb["wmv4"], 516 * nt + 128, [[516 * NT, 128], [129, 4], [1, 1]]),
                        start=st_, stop=sp_, skip_group_check=True),
                        deps=[mkl] + P[("wmv", nt)] + ([psdep("pdn")] if nt == 0 and ft == 0 else []))
                P[("mkt_free", j)] = [dmm[FD6 - 1], dnm]
            for nt in range(NT):
                P[("wmv4_free", nt)] = [dmm[FD6 - 1], dnm]
            for ft in range(FD6):
                pnm = f"psA{ft}" if ft < 2 else f"pd{ft}"
                ev = em.op("dve", lambda e, ft=ft: e.tensor_copy(
                    sb["dstage"][:, 0:512], sb["pd"][ft][:, :]),
                    deps=[dmm[ft], P.get("dstage_free")])
                psread(pnm, ev)
                dst_off = (hp * FD6 + ft) * 512
                dmo = em.dma("pool", lambda g, dst_off=dst_off, first=(i == 0): g.dma_start(
                    bass.AP(aps["cc_in"].tensor, aps["cc_in"].offset + dst_off,
                            [[12384, 128], [1, 512]]),
                    sb["dstage"][:, 0:512],
                    accum_op=(ALU.bypass if first else ALU.add)),
                    deps=[ev, P.get(("delta_dma", hp, ft))])
                P[("delta_dma", hp, ft)] = dmo
                P["dstage_free"] = dmo
                kv_delta_evacs.append(dmo)
            evn = em.op("dve", lambda e: e.tensor_copy(
                sb["dstage"][:, 0:24], sb["pdn"][:, 0:24]),
                deps=[dnm, P.get("dstage_free")])
            psread("pdn", evn)
            dst_off = 4 * FD6 * 512 + hp * 24
            dmo = em.dma("pool", lambda g, dst_off=dst_off, first=(i == 0): g.dma_start(
                bass.AP(aps["cc_in"].tensor, aps["cc_in"].offset + dst_off,
                        [[12384, 128], [1, 24]]),
                sb["dstage"][:, 0:24],
                accum_op=(ALU.bypass if first else ALU.add)),
                deps=[evn, P.get(("delta_dma_n", hp))])
            P[("delta_dma_n", hp)] = dmo
            P["dstage_free"] = dmo
            kv_delta_evacs.append(dmo)
        P["vbuf_free"] = wts[-1]

    cc = em.op("pool", lambda g: g.collective_compute(
        "AllReduce", ALU.add, replica_groups=[list(range(n_cores))],
        ins=[aps["cc_in"].opt()], outs=[aps["cc_out"].opt()]),
        deps=kv_delta_evacs)
    newmem_toks = []
    for h in range(16):
        hp, hh = h // 4, h % 4
        for fc in range(FD6):
            c0 = 129 * (FD6 * h + fc)
            l1 = em.dma("sp", lambda e, h=h, fc=fc: e.dma_start(
                sb["f1"][:, 0:128],
                bass.AP(aps["memory"].tensor,
                        aps["memory"].offset + h * 768 * 128 + 128 * fc * 128,
                        [[128, 128], [1, 128]])), deps=[P.get("f1_free"), cc])
            l2 = em.dma("sp", lambda e, hp=hp, fc=fc, hh=hh: e.dma_start(
                sb["f2"][:, 0:128],
                bass.AP(aps["cc_out"].tensor,
                        aps["cc_out"].offset + (hp * FD6 + fc) * 512 + 128 * hh,
                        [[12384, 128], [1, 128]])), deps=[P.get("f2_free"), cc])
            a1 = em.op("dve", lambda e, c0=c0: e.tensor_add(
                sb["mem"][:, c0:c0 + 128], sb["f1"][:, 0:128], sb["f2"][:, 0:128]),
                deps=[l1, l2, P.get("mem_w_ok")])
            P["f1_free"] = a1
            P["f2_free"] = a1
            l3 = em.dma("sp", lambda e, h=h, fc=fc: e.dma_start(
                sb["f1"][:, 128:129],
                bass.AP(aps["norm"].tensor, aps["norm"].offset + h * 768 + 128 * fc,
                        [[1, 128], [1, 1]])), deps=[a1])
            l4 = em.dma("sp", lambda e, hp=hp, fc=fc, hh=hh: e.dma_start(
                sb["f2"][:, 128:129],
                bass.AP(aps["cc_out"].tensor,
                        aps["cc_out"].offset + 4 * FD6 * 512 + hp * 24 + fc * 4 + hh,
                        [[12384, 128], [1, 1]]),
                allow_slow_non_contiguous=True), deps=[a1])
            a2 = em.op("dve", lambda e, c0=c0: e.tensor_add(
                sb["mem"][:, c0 + 128:c0 + 129], sb["f1"][:, 128:129], sb["f2"][:, 128:129]),
                deps=[l3, l4])
            P["f1_free"] = a2
            P["f2_free"] = a2
            newmem_toks.append(a2)
    P["mem_w_ok"] = None

    x3_stores = []
    RING = [(sb["mkT"], 2048 * j) for j in range(FD6)] + \
           [(sb["wbuf"][0], 2048 * j) for j in range(4)]
    ring_reader = {}
    gtile = [0]
    for i in range(BPC):
        mh_stores = []
        v_evs_last = [P.get("vbuf_free")]
        for h in range(16):
            mh_toks = []
            tile_sl = []
            tile_tok = []
            for fc in range(FD6):
                g = gtile[0]
                gtile[0] += 1
                sl = g % len(RING)
                buf, c0 = RING[sl]
                war = [ring_reader.get(sl)]
                if sl not in ring_reader:
                    war += ([P.get("mkT_free"), P.get(("wbuf_free", 1))]
                            if sl < FD6 else [P.get(("wbuf_free", 0))])
                tile_tok.append(em.dma("sp" if g % 2 == 0 else "act",
                                       lambda e, i=i, h=h, fc=fc, buf=buf, c0=c0: e.dma_start(
                    buf[:, c0:c0 + 2048],
                    aps["scr_mq"][i][:, 768 * h + 128 * fc:768 * h + 128 * (fc + 1)],
                    transpose=True),
                    deps=war + P.get(("mq_stores", i), [])))
                tile_sl.append(sl)
            for nt in range(NT):
                psl = nt % 2
                mm = None
                for fc in range(FD6):
                    buf, c0 = RING[tile_sl[fc]]
                    mm = em.op("pe", lambda e, h=h, nt=nt, fc=fc, psl=psl, buf=buf, c0=c0,
                               st_=(fc == 0), sp_=(fc == FD6 - 1): e.matmul(
                        sb["psA"][psl][:, 0:129],
                        buf[:, c0 + 128 * nt:c0 + 128 * (nt + 1)],
                        sb["mem"][:, 129 * (FD6 * h + fc):129 * (FD6 * h + fc) + 129],
                        start=st_, stop=sp_),
                        deps=[tile_tok[fc]] + ((newmem_toks + [psdep(f"psA{psl}")]) if fc == 0 else []))
                qcr = i * 320 + 16 * nt + h
                d1 = em.op("dve", lambda e, qcr=qcr, psl=psl: e.scalar_tensor_tensor(
                    sb["colsc"][:, 0:1], sb["rq2"][:, qcr:qcr + 1], EPS_C,
                    sb["psA"][psl][:, 128:129], ALU.mult, ALU.add),
                    deps=[mm, P.get("colsc9_free")] + P.get(("rq_done", i), []))
                d2 = em.op("dve", lambda e: e.reciprocal(
                    sb["colsc"][:, 1:2], sb["colsc"][:, 0:1]), deps=[d1])
                mh = em.op("dve", lambda e, psl=psl, nt=nt: e.tensor_scalar(
                    sb["vbuf"][:, 128 * nt:128 * (nt + 1)], sb["psA"][psl][:, 0:128],
                    sb["colsc"][:, 1:2], None, ALU.mult),
                    deps=[d2, mm, P.get(("mhh_free", nt))] + (
                        v_evs_last if (h == 0 and nt == 0) else []))
                P["colsc9_free"] = mh
                psread(f"psA{psl}", mh)
                mh_toks.append(mh)
            dmo = em.dma("sp", lambda e, i=i, h=h: e.dma_start(
                bass.AP(aps["scr_mb"][i].tensor,
                        aps["scr_mb"][i].offset + 128 * h,
                        [[2048, 128], [128 * 2048, NT], [1, 128]]),
                sb["vbuf"][:, 0:2048]),
                deps=mh_toks + (P.get(("mb_rd_done", i), []) if h == 0 else []))
            for nt in range(NT):
                P[("mhh_free", nt)] = dmo
            mh_stores.append(dmo)
            for sl in tile_sl:
                ring_reader[sl] = mm
            P["mkT_free"] = mm
        for nt in range(NT):
            for dc2 in range(4):
                psl = (nt * 4 + dc2) % 2
                lx = em.dma("sp", lambda e, i=i, nt=nt, dc2=dc2, psl=psl: e.dma_start(
                    sb["x2c"][psl][:, :],
                    aps["x2"][i][128 * nt:128 * (nt + 1), 512 * dc2:512 * (dc2 + 1)]),
                    deps=[P.get("x2c_free")])
                lm = em.dma("sp", lambda e, i=i, nt=nt, dc2=dc2, psl=psl: e.dma_start(
                    sb["mhc"][1][:, :],
                    aps["scr_mb"][i][128 * nt:128 * (nt + 1), 512 * dc2:512 * (dc2 + 1)]),
                    deps=[P.get("mhc1_free")] + mh_stores)
                lg = em.dma("sp", lambda e, i=i, nt=nt, dc2=dc2, psl=psl: e.dma_start(
                    sb["gc"][psl][:, :],
                    aps["scr_g"][i][128 * nt:128 * (nt + 1), 512 * dc2:512 * (dc2 + 1)]),
                    deps=[P.get("gc_free")])
                c = 2 * (i * NT + nt)
                o1 = em.op("dve", lambda e, c=c, psl=psl: e.tensor_scalar(
                    sb["f1"][:, :], sb["x2c"][psl][:, :],
                    sb["stats"][:, c:c + 1], sb["stats"][:, c + 1:c + 2],
                    ALU.add, ALU.mult), deps=[lx, P.get("f1_free")])
                o2 = em.op("dve", lambda e, psl=psl: e.tensor_tensor(
                    sb["f2"][:, :], sb["gc"][psl][:, :], sb["mhc"][1][:, :],
                    ALU.mult), deps=[lg, lm, P.get("f2_free")])
                P["mhc1_free"] = o2
                o3 = em.op("dve", lambda e, psl=psl: e.tensor_add(
                    sb["x3c"][psl][:, :], sb["f1"][:, :], sb["f2"][:, :]),
                    deps=[o1, o2, P.get("x3c_free")])
                P["f2_free"] = o3
                P["gc_free"] = o2
                o4 = em.op("dve", lambda e, psl=psl: e.tensor_add(
                    sb["f1"][:, :], sb["x3c"][psl][:, :], sb["x2c"][psl][:, :]),
                    deps=[o3])
                P["x3c_free"] = o4
                P["x2c_free"] = o4
                dmo = em.dma("sp", lambda e, i=i, nt=nt, dc2=dc2, psl=psl: e.dma_start(
                    aps["x3"][i][128 * nt:128 * (nt + 1), 512 * dc2:512 * (dc2 + 1)],
                    sb["f1"][:, :]), deps=[o4])
                P["f1_free"] = dmo
                x3_stores.append(dmo)
    return x3_stores


def build_l1_attn_program(n_cores=8):
    nc = bass.Bass("TRN2", target_bir_lowering=False)
    x2 = nc.dram_tensor("x2", [BPC * T, D], F32, kind="ExternalInput").ap()
    wq = nc.dram_tensor("wq", [D, D], F32, kind="ExternalInput").ap()
    wk = nc.dram_tensor("wk", [512, D], F32, kind="ExternalInput").ap()
    wv = nc.dram_tensor("wv", [512, D], F32, kind="ExternalInput").ap()
    wg = nc.dram_tensor("wg", [D, D], F32, kind="ExternalInput").ap()
    wmb = nc.dram_tensor("wmb", [D, D], F32, kind="ExternalInput").ap()
    bq = nc.dram_tensor("bq", [D], F32, kind="ExternalInput").ap()
    bk = nc.dram_tensor("bk", [512], F32, kind="ExternalInput").ap()
    bv = nc.dram_tensor("bv", [512], F32, kind="ExternalInput").ap()
    bg = nc.dram_tensor("bg", [D], F32, kind="ExternalInput").ap()
    bmb = nc.dram_tensor("bmb", [D], F32, kind="ExternalInput").ap()
    memory = nc.dram_tensor("memory", [16, 768, 128], F32, kind="ExternalInput").ap()
    norm = nc.dram_tensor("norm", [16, 768], F32, kind="ExternalInput").ap()
    x3 = nc.dram_tensor("x3", [BPC * T, D], F32, kind="ExternalOutput").ap()
    xln = nc.dram_tensor("xlnC", [BPC * T, D], BF16).ap()
    scr_w = nc.dram_tensor("scr_wC", [3 * D * D + 2 * 512 * D], BF16).ap()
    scr_mk = nc.dram_tensor("scr_mkC", [BPC * T, 3072], BF16).ap()
    scr_mq = nc.dram_tensor("scr_mqC", [BPC * T, 12288], BF16).ap()
    scr_mb = nc.dram_tensor("scr_mbC", [BPC * T, 2048], BF16).ap()
    scr_g = nc.dram_tensor("scr_gC", [BPC * T, 2048], BF16).ap()
    cc_in = nc.dram_tensor("cc_inC", [128, 12384], F32).ap()
    cc_out = nc.dram_tensor("cc_outC", [128, 12384], F32).ap()
    em = Em(nc)
    sb = alloc_l1_attn_sbuf(em)
    aps = {
        "x2": [x2[i * T:(i + 1) * T, :] for i in range(BPC)],
        "x3": [x3[i * T:(i + 1) * T, :] for i in range(BPC)],
        "xln": [xln[i * T:(i + 1) * T, :] for i in range(BPC)],
        "scr_mk": [scr_mk[i * T:(i + 1) * T, :] for i in range(BPC)],
        "scr_mq": [scr_mq[i * T:(i + 1) * T, :] for i in range(BPC)],
        "scr_mb": [scr_mb[i * T:(i + 1) * T, :] for i in range(BPC)],
        "scr_g": [scr_g[i * T:(i + 1) * T, :] for i in range(BPC)],
        "scr_w": scr_w, "cc_in": cc_in, "cc_out": cc_out,
        "wq": wq, "wk": wk, "wv": wv, "wg": wg, "wmb": wmb,
        "bq": bq, "bk": bk, "bv": bv, "bg": bg, "bmb": bmb,
        "memory": memory, "norm": norm,
    }
    emit_l1_attn(em, aps, sb, n_cores=n_cores)
    em.emit()
    return nc



from concourse.aot_env import allow_ici_all_to_all

NH_D = 16
HD_D = 128
SEL = 256
ST_D = 4096


def alloc_l1_ffn_sbuf(em):
    nc = em.nc
    st = em.stack
    sb = {}
    sb["lnx"] = [st.enter_context(nc.sbuf_tensor("lnxD", [128, D], F32))]
    sb["lnxln"] = [st.enter_context(nc.sbuf_tensor("lnxlnD", [128, D], BF16))]
    sb["lncol"] = st.enter_context(nc.sbuf_tensor("lncolD", [128, 32], F32))
    sb["stats"] = st.enter_context(nc.sbuf_tensor("statsD", [128, 2 * NT * BPC], F32))
    sb["xt"] = [st.enter_context(nc.sbuf_tensor(f"xtD{j}", [128, 2048], BF16)) for j in range(2)]
    sb["wrt"] = st.enter_context(nc.sbuf_tensor("wrtD", [128, 16], BF16))
    sb["scol"] = st.enter_context(nc.sbuf_tensor("scolD", [128, 192], F32))
    sb["sbc"] = st.enter_context(nc.sbuf_tensor("sbcD", [128, 2048], F32))
    sb["cmp"] = st.enter_context(nc.sbuf_tensor("cmpD", [128, 2048], BF16))
    sb["tri1"] = st.enter_context(nc.sbuf_tensor("tri1D", [128, 128], BF16))
    sb["triu"] = st.enter_context(nc.sbuf_tensor("triuD", [128, 128], BF16))
    sb["iota"] = st.enter_context(nc.sbuf_tensor("iotaD", [128, 256], F32))
    sb["iotc"] = st.enter_context(nc.sbuf_tensor("iotcD", [128, 2], F32))
    sb["Pm"] = [st.enter_context(nc.sbuf_tensor("PmD0", [128, 4096], BF16))] * 2
    sb["PT"] = st.enter_context(nc.sbuf_tensor("PTD", [128, 4096], BF16))
    sb["xsel"] = [st.enter_context(nc.sbuf_tensor(f"xselD{j}", [128, 4096], BF16)) for j in range(2)]
    sb["cs4"] = st.enter_context(nc.sbuf_tensor("cs4D", [128, 1024], BF16))
    sb["cs4b"] = st.enter_context(nc.sbuf_tensor("cs4bD", [128, 1024], BF16))
    sb["bigD"] = st.enter_context(nc.sbuf_tensor("bigDD", [128, 16384], BF16))
    class _View:
        def __init__(self, t, off, size):
            self.t, self.off, self.size = t, off, size
        def __getitem__(self, idx):
            p, f = idx
            f0 = 0 if f.start is None else f.start
            f1 = self.size if f.stop is None else f.stop
            return self.t[p, f0 + self.off:f1 + self.off]
    sb["hD"] = _View(sb["bigD"], 0, 4096)
    sb["hT"] = _View(sb["bigD"], 4096, 4096)
    sb["kq"] = _View(sb["bigD"], 0, 4096)
    sb["kk"] = _View(sb["bigD"], 4096, 4096)
    sb["vv"] = _View(sb["bigD"], 8192, 4096)
    sb["ctxT"] = _View(sb["bigD"], 12288, 4096)
    sb["aout"] = _View(sb["bigD"], 0, 8192)
    sb["h2T"] = _View(sb["bigD"], 8192, 8192)
    sb["ln1b"] = st.enter_context(nc.sbuf_tensor("ln1bD", [128, 2048], BF16))
    sb["ln2b"] = st.enter_context(nc.sbuf_tensor("ln2bD", [128, 2048], BF16))
    sb["wbuf"] = [st.enter_context(nc.sbuf_tensor(f"wbufD{j}", [128, 8192], BF16)) for j in range(2)]
    sb["brow"] = st.enter_context(nc.sbuf_tensor("browD", [1, 512], BF16))
    sb["ones1"] = st.enter_context(nc.sbuf_tensor("ones1D", [1, 128], BF16))
    sb["onesf"] = st.enter_context(nc.sbuf_tensor("onesfD", [1, 128], F32))
    sb["onecol"] = st.enter_context(nc.sbuf_tensor("onecolD", [128, 1], BF16))
    sb["rt1"] = st.enter_context(nc.sbuf_tensor("rt1D", [128, 512], BF16))
    sb["rt2"] = st.enter_context(nc.sbuf_tensor("rt2D", [128, 512], BF16))
    sb["qro"] = [st.enter_context(nc.sbuf_tensor(f"qroD{j}", [128, 512], BF16)) for j in range(2)]
    sb["at"] = [st.enter_context(nc.sbuf_tensor(f"atD{j}", [128, 512], BF16)) for j in range(2)]
    sb["at2"] = [st.enter_context(nc.sbuf_tensor(f"at2D{j}", [128, 512], BF16)) for j in range(2)]
    sb["mask"] = [st.enter_context(nc.sbuf_tensor(f"maskD{j}", [128, 512], BF16)) for j in range(4)]
    sb["zrb"] = st.enter_context(nc.sbuf_tensor("zrbD", [1, 512], BF16))
    sb["zb"] = st.enter_context(nc.sbuf_tensor("zbD", [128, 512], BF16))
    sb["cxl"] = [st.enter_context(nc.sbuf_tensor(f"cxlD{j}", [128, 2048], BF16)) for j in range(2)]
    sb["selb"] = st.enter_context(nc.sbuf_tensor("selbD", [128, 16], BF16))
    sb["scb"] = st.enter_context(nc.sbuf_tensor("scbD", [128, 16], BF16))
    sb["x1t"] = st.enter_context(nc.sbuf_tensor("x1tD", [128, 2048], F32))
    sb["h2"] = st.enter_context(nc.sbuf_tensor("h2D", [128, 2048], BF16))
    sb["silu"] = st.enter_context(nc.sbuf_tensor("siluD", [128, 512], BF16))
    sb["mit"] = st.enter_context(nc.sbuf_tensor("mitD", [128, 512], BF16))
    sb["gcol"] = st.enter_context(nc.sbuf_tensor("gcolD", [128, 8], F32))
    sb["gD"] = st.enter_context(nc.sbuf_tensor("gDD", [128, 8192], BF16))
    sb["x3c"] = [st.enter_context(nc.sbuf_tensor("x3cD0", [128, 512], F32))] * 2
    sb["f1"] = st.enter_context(nc.sbuf_tensor("f1D", [128, 512], F32))
    sb["f2"] = st.enter_context(nc.sbuf_tensor("f2D", [128, 512], F32))
    sb["x4c"] = [st.enter_context(nc.sbuf_tensor("x4cD0", [128, 512], F32))] * 2
    sb["ident"] = st.enter_context(nc.sbuf_tensor("identD", [128, 128], BF16))
    sb["psA"] = [st.enter_context(nc.psum_tensor(f"psAD{j}", [128, 512], F32)) for j in range(2)]
    sb["psB"] = [st.enter_context(nc.psum_tensor(f"psBD{j}", [128, 512], F32)) for j in range(2)]
    sb["psc"] = st.enter_context(nc.psum_tensor("pscD", [128, 16], F32))
    sb["pz"] = st.enter_context(nc.psum_tensor("pzD", [128, 512], F32))
    sb["pz1"] = st.enter_context(nc.psum_tensor("pz1D", [1, 512], F32))
    sb["pt"] = st.enter_context(nc.psum_tensor("ptD", [128, 128], BF16))
    return sb


def emit_l1_ffn(em, aps, sb, n_cores=8):
    P = {}
    ps_reader = {}

    def psdep(name):
        return ps_reader.get(name)

    def psread(name, tok):
        ps_reader[name] = tok

    ident_t = make_identity_bf16(em, sb["ident"])
    ones_t = em.op("pool", lambda g: g.memset(sb["ones1"][:, :], 1.0))
    onesf_t = em.op("pool", lambda g: g.memset(sb["onesf"][:, :], 1.0))
    onec_t = em.op("pool", lambda g: g.memset(sb["onecol"][:, :], 1.0))
    t1_t = em.op("pool", lambda g: g.memset(sb["tri1"][:, :], 1.0))
    tu0 = em.op("pool", lambda g: g.memset(sb["triu"][:, :], 1.0))
    tu_t = em.op("pool", lambda g: g.affine_select(
        sb["triu"][:, :], sb["triu"][:, :], [[1, 128]], ALU.is_gt, 0.0,
        base=0, channel_multiplier=-1), deps=[tu0])
    io_t = em.op("pool", lambda g: g.iota(
        sb["iota"][:, :], [[1, 256]], base=0, channel_multiplier=0,
        allow_small_or_imprecise_dtypes=True))
    ioc_t = em.op("pool", lambda g: g.iota(
        sb["iotc"][:, :], [[128, 2]], base=0, channel_multiplier=1,
        allow_small_or_imprecise_dtypes=True))
    mtoks = [causal_keep_mask(em, sb["mask"][j], 128 * j) for j in range(4)]
    wrt_t = em.dma("pool", lambda g: g.dma_start(
        sb["wrt"][:, :],
        bass.AP(aps["w_router"].tensor, aps["w_router"].offset, [[1, 128], [128, 16]]),
        allow_slow_non_contiguous=True))
    bc_done = []
    prevmm = None
    for (wn, dst) in (("ln1_w", "ln1b"), ("ln2_w", "ln2b")):
        for c in range(4):
            lr = em.dma("pool", lambda g, wn=wn, c=c: g.dma_start(
                sb["brow"][:, :],
                bass.AP(aps[wn].tensor, aps[wn].offset + 512 * c, [[512, 1], [1, 512]])),
                deps=[prevmm])
            mm = em.op("pe", lambda e: e.matmul(
                sb["psA"][0][:, :], sb["ones1"][:, :], sb["brow"][:, :],
                start=True, stop=True), deps=[lr, ones_t, psdep("psA0")])
            prevmm = mm
            cp = em.op("dve", lambda e, dst=dst, c=c: e.tensor_copy(
                sb[dst][:, 512 * c:512 * (c + 1)], sb["psA"][0][:, :]), deps=[mm])
            psread("psA0", cp)
            bc_done.append(cp)
    wd_prep = []
    pst = None
    for r0 in range(0, 8192, 128):
        ld = em.dma("pool", lambda g, r0=r0: g.dma_start(
            sb["cxl"][0][:, :], aps["w_down"][r0:r0 + 128, :]), deps=[pst])
        st_ = em.dma("sp", lambda e, r0=r0: e.dma_start(
            bass.AP(aps["scr_wd"].tensor, aps["scr_wd"].offset + r0 * D,
                    [[D, 128], [1, D]]),
            sb["cxl"][0][:, :]), deps=[ld])
        pst = st_
        wd_prep.append(st_)
    P["cxl0_free"] = pst

    a2a_stores = []
    for i in range(BPC):
        ln_toks = emit_ln_pass(em, aps["x3"][i], aps["xln"][i], sb, i, store_stats=True)
        sc0 = i * 96
        rmm = None
        for dc in range(DC):
            j = dc % 2
            xl = em.dma("sp", lambda e, i=i, dc=dc, j=j: e.dma_start(
                sb["xt"][j][:, :], aps["xln"][i][:, 128 * dc:128 * (dc + 1)],
                transpose=True), deps=ln_toks + [P.get(("xt_free", j))])
            for nt in range(NT):
                rmm = em.op("pe", lambda e, dc=dc, nt=nt, j=j,
                            st_=(dc == 0 and nt == 0),
                            sp_=(dc == DC - 1 and nt == NT - 1): e.matmul(
                    sb["psc"][:, nt:nt + 1],
                    sb["xt"][j][:, 128 * nt:128 * (nt + 1)],
                    sb["wrt"][:, dc:dc + 1],
                    start=st_, stop=sp_, skip_group_check=True),
                    deps=[xl, wrt_t] + ([psdep("psc")] if dc == 0 and nt == 0 else []))
            P[("xt_free", j)] = rmm
        sev = em.op("dve", lambda e, sc0=sc0: e.tensor_copy(
            sb["scol"][:, sc0:sc0 + 16], sb["psc"][:, :]), deps=[rmm, P.get("scol_w")])
        psread("psc", sev)
        scb_t = em.op("act", lambda e, sc0=sc0: e.copy(
            sb["scb"][:, :], sb["scol"][:, sc0:sc0 + 16]), deps=[sev, P.get("scb_free")])
        r1 = em.dma("sp", lambda e, sc0=sc0: e.dma_start(
            bass.AP(aps["scr_srow"].tensor, aps["scr_srow"].offset, [[1, 128], [128, 16]]),
            sb["scol"][:, sc0:sc0 + 16], allow_slow_non_contiguous=True),
            deps=[sev, P.get("srow_free")])
        r2 = em.dma("sp", lambda e: e.dma_start(
            sb["sbc"][0:1, :],
            bass.AP(aps["scr_srow"].tensor, aps["scr_srow"].offset, [[2048, 1], [1, 2048]])),
            deps=[r1, P.get("sbc_free")])
        r3s = []
        for bcc in range(4):
            bmm = em.op("pe", lambda e, bcc=bcc: e.matmul(
                sb["pz"][:, :], sb["onesf"][:, :], sb["sbc"][0:1, 512 * bcc:512 * (bcc + 1)],
                start=True, stop=True), deps=[r2, onesf_t, psdep("pz")])
            bcp = em.op("dve", lambda e, bcc=bcc: e.tensor_copy(
                sb["sbc"][:, 512 * bcc:512 * (bcc + 1)], sb["pz"][:, :]), deps=[bmm])
            psread("pz", bcp)
            r3s.append(bcp)
        r3 = r3s
        pass
        selb_toks = []
        for nt in range(NT):
            cm = em.op("dve", lambda e, nt=nt, sc0=sc0: e.tensor_scalar(
                sb["cmp"][:, :], sb["sbc"][:, :],
                sb["scol"][:, sc0 + nt:sc0 + nt + 1], None, ALU.is_gt),
                deps=[r3, sev, P.get("cmp_free")])
            rk = em.op("dve", lambda e, nt=nt, sc0=sc0: e.reduce_sum(
                sb["scol"][:, sc0 + 16 + nt:sc0 + 17 + nt], sb["cmp"][:, :], axis=AX.X),
                deps=[cm])
            P["cmp_free"] = rk
            sl = em.op("dve", lambda e, nt=nt, sc0=sc0: e.tensor_scalar(
                sb["scol"][:, sc0 + 32 + nt:sc0 + 33 + nt],
                sb["scol"][:, sc0 + 16 + nt:sc0 + 17 + nt], 256.0, None, ALU.is_lt),
                deps=[rk])
            sb_ = em.op("dve", lambda e, nt=nt, sc0=sc0: e.tensor_copy(
                sb["selb"][:, nt:nt + 1],
                sb["scol"][:, sc0 + 32 + nt:sc0 + 33 + nt]), deps=[sl, P.get("selb_free")])
            selb_toks.append(sb_)
        P["sbc_free"] = P["cmp_free"]
        omm = None
        for nt in range(NT):
            for c in range(nt + 1):
                omm = em.op("pe", lambda e, nt=nt, c=c,
                            st_=(nt == 0 and c == 0),
                            sp_=(nt == NT - 1 and c == NT - 1): e.matmul(
                    sb["psc"][:, nt:nt + 1],
                    sb["tri1"][:, :] if c < nt else sb["triu"][:, :],
                    sb["selb"][:, c:c + 1],
                    start=st_, stop=sp_, skip_group_check=True),
                    deps=selb_toks + [t1_t, tu_t] + ([psdep("psc")] if nt == 0 and c == 0 else []))
        oev = em.op("dve", lambda e, sc0=sc0: e.tensor_copy(
            sb["scol"][:, sc0 + 48:sc0 + 64], sb["psc"][:, :]), deps=[omm])
        psread("psc", oev)
        om1 = em.op("dve", lambda e, sc0=sc0: e.tensor_scalar(
            sb["scol"][:, sc0 + 80:sc0 + 96], sb["scol"][:, sc0 + 48:sc0 + 64],
            1.0, None, ALU.add), deps=[oev])
        om2 = em.op("dve", lambda e, sc0=sc0: e.tensor_tensor(
            sb["scol"][:, sc0 + 64:sc0 + 80], sb["scol"][:, sc0 + 80:sc0 + 96],
            sb["scol"][:, sc0 + 32:sc0 + 48], ALU.mult), deps=[om1])
        om3 = em.op("dve", lambda e, sc0=sc0: e.tensor_scalar(
            sb["scol"][:, sc0 + 64:sc0 + 80], sb["scol"][:, sc0 + 64:sc0 + 80],
            1.0, None, ALU.subtract), deps=[om2])
        P["scol_w"] = om3
        p_toks = []
        for nt in range(NT):
            pt_ = em.op("dve", lambda e, nt=nt, i=i, sc0=sc0: e.tensor_scalar(
                sb["Pm"][i][:, 256 * nt:256 * (nt + 1)], sb["iota"][:, :],
                sb["scol"][:, sc0 + 64 + nt:sc0 + 65 + nt],
                sb["scol"][:, sc0 + 32 + nt:sc0 + 33 + nt],
                ALU.is_equal, ALU.mult),
                deps=[om3, io_t, P.get("Pm_free")])
            p_toks.append(pt_)
        P[("p_toks", i)] = p_toks
        xs_toks = []
        for dc in range(4):
            mm0 = mm1 = None
            for ncx in range(NT):
                j = ncx % 2
                xl = em.dma("sp", lambda e, i=i, ncx=ncx, dc=dc, j=j: e.dma_start(
                    sb["xt"][j][:, 0:512],
                    aps["xln"][i][128 * ncx:128 * (ncx + 1), 512 * dc:512 * (dc + 1)]),
                    deps=ln_toks + [P.get(("xt_free", j))])
                mm0 = em.op("pe", lambda e, ncx=ncx, dc=dc, i=i, j=j,
                            st_=(ncx == 0), sp_=(ncx == NT - 1): e.matmul(
                    sb["psA"][0][:, :],
                    sb["Pm"][i][:, 256 * ncx:256 * ncx + 128],
                    sb["xt"][j][:, 0:512], start=st_, stop=sp_),
                    deps=[xl] + (p_toks + [psdep("psA0")] if ncx == 0 else []))
                mm1 = em.op("pe", lambda e, ncx=ncx, dc=dc, i=i, j=j,
                            st_=(ncx == 0), sp_=(ncx == NT - 1): e.matmul(
                    sb["psA"][1][:, :],
                    sb["Pm"][i][:, 256 * ncx + 128:256 * (ncx + 1)],
                    sb["xt"][j][:, 0:512], start=st_, stop=sp_),
                    deps=[xl] + (p_toks + [psdep("psA1")] if ncx == 0 else []))
                P[("xt_free", j)] = [mm0, mm1]
            for jt in range(2):
                ev = em.op("dve", lambda e, i=i, dc=dc, jt=jt: e.tensor_copy(
                    sb["xsel"][i][:, 2048 * jt + 512 * dc:2048 * jt + 512 * (dc + 1)],
                    sb["psA"][jt][:, :]), deps=[mm0 if jt == 0 else mm1,
                                                P.get(("xsel_free", i))])
                psread(f"psA{jt}", ev)
                xs_toks.append(ev)
        cs_toks = []
        for half in range(2):
            mm0 = mm1 = None
            for ncx in range(NT):
                j = ncx % 2
                cl = em.dma("pool", lambda g, ncx=ncx, half=half, j=j: g.dma_start(
                    sb["xt"][j][:, 0:512],
                    aps["rope_cs"][128 * ncx:128 * (ncx + 1), 512 * half:512 * (half + 1)]),
                    deps=[P.get(("xt_free", j))])
                mm0 = em.op("pe", lambda e, ncx=ncx, i=i, j=j,
                            st_=(ncx == 0), sp_=(ncx == NT - 1): e.matmul(
                    sb["psA"][0][:, :],
                    sb["Pm"][i][:, 256 * ncx:256 * ncx + 128],
                    sb["xt"][j][:, 0:512], start=st_, stop=sp_),
                    deps=[cl] + ([psdep("psA0")] if ncx == 0 else []))
                mm1 = em.op("pe", lambda e, ncx=ncx, i=i, j=j,
                            st_=(ncx == 0), sp_=(ncx == NT - 1): e.matmul(
                    sb["psA"][1][:, :],
                    sb["Pm"][i][:, 256 * ncx + 128:256 * (ncx + 1)],
                    sb["xt"][j][:, 0:512], start=st_, stop=sp_),
                    deps=[cl] + ([psdep("psA1")] if ncx == 0 else []))
                P[("xt_free", j)] = [mm0, mm1]
            dstb = sb["cs4"] if half == 0 else sb["cs4b"]
            for jt in range(2):
                ev = em.op("dve", lambda e, dstb=dstb, jt=jt: e.tensor_copy(
                    dstb[:, 512 * jt:512 * (jt + 1)], sb["psA"][jt][:, :]),
                    deps=[mm0 if jt == 0 else mm1, P.get("cs_free")])
                psread(f"psA{jt}", ev)
                cs_toks.append(ev)
        gmm = {}
        for ncx in range(NT):
            for jt in range(2):
                gmm[jt] = em.op("pe", lambda e, ncx=ncx, i=i, jt=jt,
                                st_=(ncx == 0), sp_=(ncx == NT - 1): e.matmul(
                    sb["psc"][:, 2 * i + jt:2 * i + jt + 1],
                    sb["Pm"][i][:, 256 * ncx + 128 * jt:256 * ncx + 128 * (jt + 1)],
                    sb["scb"][:, ncx:ncx + 1],
                    start=(st_ and i == 0 and jt == 0), sp_=None, stop=(sp_ and jt == 1),
                    skip_group_check=True) if False else e.matmul(
                    sb["psc"][:, 2 * i + jt:2 * i + jt + 1],
                    sb["Pm"][i][:, 256 * ncx + 128 * jt:256 * ncx + 128 * (jt + 1)],
                    sb["scb"][:, ncx:ncx + 1],
                    start=(ncx == 0 and jt == 0), stop=(ncx == NT - 1 and jt == 1),
                    skip_group_check=True),
                    deps=[scb_t] + (p_toks + [psdep("psc")] if ncx == 0 and jt == 0 else []))
        gsg = em.op("act", lambda e, i=i: e.activation(
            sb["gcol"][:, 2 * i:2 * i + 2], sb["psc"][:, 2 * i:2 * i + 2], AF.Sigmoid),
            deps=[gmm[1], P.get("gcol_free")])
        psread("psc", gsg)
        P[("gsg", i)] = gsg
        h_toks = []
        for jt in range(2):
            sqs = []
            for c in range(4):
                sq = em.op("act", lambda e, i=i, jt=jt, c=c: e.activation(
                    sb["cmp"][:, 0:512],
                    sb["xsel"][i][:, 2048 * jt + 512 * c:2048 * jt + 512 * (c + 1)],
                    AF.Square, accum_out=sb["lncol"][:, 20 + c:21 + c]),
                    deps=xs_toks + [P.get("cmp_free"), P.get(("lncol8", c))])
                P["cmp_free"] = sq
                sqs.append(sq)
            v1 = em.op("dve", lambda e: e.tensor_add(
                sb["lncol"][:, 24:25], sb["lncol"][:, 20:21], sb["lncol"][:, 21:22]),
                deps=sqs[:2])
            v2 = em.op("dve", lambda e: e.tensor_add(
                sb["lncol"][:, 25:26], sb["lncol"][:, 22:23], sb["lncol"][:, 23:24]),
                deps=sqs[2:])
            v3 = em.op("dve", lambda e: e.tensor_add(
                sb["lncol"][:, 24:25], sb["lncol"][:, 24:25], sb["lncol"][:, 25:26]),
                deps=[v1, v2]) if False else em.op("dve", lambda e: e.tensor_add(
                sb["lncol"][:, 26:27], sb["lncol"][:, 24:25], sb["lncol"][:, 25:26]),
                deps=[v1, v2])
            c0 = em.op("dve", lambda e: e.tensor_scalar(
                sb["lncol"][:, 27:28], sb["lncol"][:, 26:27], 1.0 / D, 1e-6,
                ALU.mult, ALU.add), deps=[v3])
            c1 = em.op("act", lambda e: e.activation(
                sb["lncol"][:, 24:25], sb["lncol"][:, 27:28], AF.Ln), deps=[c0])
            rst = em.op("act", lambda e: e.activation(
                sb["lncol"][:, 25:26], sb["lncol"][:, 24:25], AF.Exp, scale=-0.5),
                deps=[c1])
            for c in range(4):
                y1 = em.op("dve", lambda e, i=i, jt=jt, c=c: e.tensor_scalar(
                    sb["cmp"][:, 0:512],
                    sb["xsel"][i][:, 2048 * jt + 512 * c:2048 * jt + 512 * (c + 1)],
                    sb["lncol"][:, 25:26], None, ALU.mult),
                    deps=[rst, P.get("cmp_free")])
                y2 = em.op("dve", lambda e, jt=jt, c=c: e.tensor_tensor(
                    sb["hD"][:, 2048 * jt + 512 * c:2048 * jt + 512 * (c + 1)],
                    sb["cmp"][:, 0:512], sb["ln1b"][:, 512 * c:512 * (c + 1)],
                    ALU.mult), deps=[y1, bc_done[3], P.get(("hD_free", jt))])
                P["cmp_free"] = y2
                P[("lncol8", c)] = y1
                h_toks.append(y2)
        ht_toks = []
        for jt in range(2):
            for dcx in range(DC):
                tt_ = em.op("pe", lambda e, jt=jt, dcx=dcx: e.transpose(
                    sb["pt"][:, :], sb["hD"][:, 2048 * jt + 128 * dcx:2048 * jt + 128 * (dcx + 1)],
                    sb["ident"][:, :]), deps=h_toks + [ident_t, psdep("pt")])
                tc_ = em.op("dve", lambda e, jt=jt, dcx=dcx: e.tensor_copy(
                    sb["hT"][:, 256 * dcx + 128 * jt:256 * dcx + 128 * (jt + 1)],
                    sb["pt"][:, :]), deps=[tt_, P.get(("hT_free2", jt, dcx))])
                psread("pt", tc_)
                ht_toks.append(tc_)
        def rope_d(psl, jt, deps_mm):
            r1 = em.op("dve", lambda e, psl=psl, jt=jt: e.scalar_tensor_tensor(
                sb["rt1"][:, :], sb["psA"][psl][:, :], 1.0,
                sb["cs4"][:, 512 * jt:512 * (jt + 1)], ALU.bypass, ALU.mult),
                deps=deps_mm + cs_toks + [P.get("rt1_free")])
            r2 = em.op("dve", lambda e, psl=psl, jt=jt: e.tensor_tensor(
                bass.AP(sb["rt2"], 0, [[512, 128], [128, 4], [1, 64]]),
                bass.AP(sb["psA"][psl], 64, [[512, 128], [128, 4], [1, 64]]),
                bass.AP(sb["cs4b"], 512 * jt, [[1024, 128], [128, 4], [1, 64]]),
                ALU.mult), deps=deps_mm + cs_toks + [P.get("rt2_free")])
            r3 = em.op("dve", lambda e, psl=psl, jt=jt: e.tensor_tensor(
                bass.AP(sb["rt2"], 64, [[512, 128], [128, 4], [1, 64]]),
                bass.AP(sb["psA"][psl], 0, [[512, 128], [128, 4], [1, 64]]),
                bass.AP(sb["cs4b"], 512 * jt + 64, [[1024, 128], [128, 4], [1, 64]]),
                ALU.mult), deps=deps_mm + [P.get("rt2_free")])
            rr = jt
            r4 = em.op("dve", lambda e, rr=rr: e.tensor_add(
                sb["qro"][rr][:, :], sb["rt1"][:, :], sb["rt2"][:, :]),
                deps=[r1, r2, r3] + (P.get(("qro_free", rr)) or []))
            P["rt1_free"] = r4
            P["rt2_free"] = r4
            psread(f"psA{psl}", r4)
            return r4, rr

        def proj_d(jb, jt, psl, wl, bl, deps0):
            mm = None
            for dcx in range(DC):
                mm = em.op("pe", lambda e, dcx=dcx, jt=jt, jb=jb, psl=psl,
                           st_=(dcx == 0): e.matmul(
                    sb["psA"][psl][:, :],
                    sb["hT"][:, 256 * dcx + 128 * jt:256 * dcx + 128 * (jt + 1)],
                    sb["wbuf"][jb][:, 512 * dcx:512 * (dcx + 1)],
                    start=st_, stop=False),
                    deps=[wl] + (deps0 if dcx == 0 else []))
            mm = em.op("pe", lambda e, psl=psl: e.matmul(
                sb["psA"][psl][:, :], sb["ones1"][:, :], sb["brow"][:, :],
                start=False, stop=True), deps=[bl, mm, ones_t])
            return mm

        def wload_d(wname, oc, jb, ncols=512):
            wap = aps[wname]
            return em.dma("pool", lambda g, oc=oc, jb=jb, wap=wap, ncols=ncols: g.dma_start(
                sb["wbuf"][jb][:, 0:ncols * DC],
                bass.AP(wap.tensor, wap.offset + ncols * oc,
                        [[wap.shape[1], 128], [128 * wap.shape[1], DC], [1, ncols]])),
                deps=[P.get(("wbufD_free", jb))])

        def brow_d(bname, oc):
            return em.dma("pool", lambda g, bname=bname, oc=oc: g.dma_start(
                sb["brow"][:, :],
                bass.AP(aps[bname].tensor, aps[bname].offset + 512 * oc, [[512, 1], [1, 512]])),
                deps=[P.get("browD_free")])

        for oc in range(4):
            jb = oc % 2
            wl = wload_d("wq", oc, jb)
            bl = brow_d("bq", oc)
            for jt in range(2):
                psl = jt
                mm = proj_d(jb, jt, psl, wl, bl,
                            ht_toks + [psdep(f"psA{psl}")])
                P["browD_free"] = mm
                r4, rr = rope_d(psl, jt, [mm])
                dms = []
                for s_ in range(2):
                    dm = em.dma("sp", lambda e, oc=oc, s_=s_, i=i, jt=jt, rr=rr: e.dma_start(
                        bass.AP(aps["cc_a2a_in"].tensor,
                                aps["cc_a2a_in"].offset + (2 * oc + s_) * 512 * 512
                                + (256 * i + 128 * jt) * 512,
                                [[512, 128], [1, 256]]),
                        bass.AP(sb["qro"][rr], 256 * s_, [[512, 128], [1, 256]])),
                        deps=[r4] + (P.get("a2a_war") or []))
                    dms.append(dm)
                P[("qro_free", rr)] = dms
                a2a_stores.extend(dms)
            P[("wbufD_free", jb)] = mm
        for oc2 in range(2):
            jb = oc2 % 2
            wl = wload_d("wk", oc2, jb)
            bl = brow_d("bk", oc2)
            for jt in range(2):
                psl = jt
                mm = proj_d(jb, jt, psl, wl, bl, ht_toks + [psdep(f"psA{psl}")])
                P["browD_free"] = mm
                r4, rr = rope_d(psl, jt, [mm])
                dms = []
                for m_ in range(4):
                    dm = em.dma("sp", lambda e, oc2=oc2, m_=m_, i=i, jt=jt, rr=rr: e.dma_start(
                        bass.AP(aps["cc_a2a_in"].tensor,
                                aps["cc_a2a_in"].offset + (4 * oc2 + m_) * 512 * 512
                                + (256 * i + 128 * jt) * 512 + 256,
                                [[512, 128], [1, 128]]),
                        bass.AP(sb["qro"][rr], 128 * m_, [[512, 128], [1, 128]])),
                        deps=[r4] + (P.get("a2a_war") or []))
                    dms.append(dm)
                P[("qro_free", rr)] = dms
                a2a_stores.extend(dms)
            P[("wbufD_free", jb)] = mm
        for oc2 in range(2):
            jb = oc2 % 2
            wl = wload_d("wv", oc2, jb)
            bl = brow_d("bv", oc2)
            for jt in range(2):
                psl = jt
                mm = proj_d(jb, jt, psl, wl, bl, ht_toks + [psdep(f"psA{psl}")])
                P["browD_free"] = mm
                rr = jt
                cv_ = em.op("dve", lambda e, psl=psl, rr=rr: e.tensor_copy(
                    sb["qro"][rr][:, :], sb["psA"][psl][:, :]),
                    deps=[mm] + (P.get(("qro_free", rr)) or []))
                psread(f"psA{psl}", cv_)
                dms = []
                for m_ in range(4):
                    dm = em.dma("sp", lambda e, oc2=oc2, m_=m_, i=i, jt=jt, rr=rr: e.dma_start(
                        bass.AP(aps["cc_a2a_in"].tensor,
                                aps["cc_a2a_in"].offset + (4 * oc2 + m_) * 512 * 512
                                + (256 * i + 128 * jt) * 512 + 384,
                                [[512, 128], [1, 128]]),
                        bass.AP(sb["qro"][rr], 128 * m_, [[512, 128], [1, 128]])),
                        deps=[cv_] + (P.get("a2a_war") or []))
                    dms.append(dm)
                P[("qro_free", rr)] = dms
                a2a_stores.extend(dms)
            P[("wbufD_free", jb)] = mm
        P[("xsel_free", i)] = None
        P["Pm_free"] = gmm[1]
        P["cs_free"] = r4
        P[("hD_free", 0)] = mm
        P[("hD_free", 1)] = mm
        for jt in range(2):
            for dcx in range(DC):
                P[("hT_free2", jt, dcx)] = mm
        P["gcol_free"] = None
        P["selb_free"] = omm
        P["scb_free"] = gmm[1]

    def _a2a1(g):
        with allow_ici_all_to_all():
            return g.collective_compute(
                "AllToAll", ALU.bypass, replica_groups=[list(range(n_cores))],
                ins=[aps["cc_a2a_in"].opt()], outs=[aps["cc_a2a_out"].opt()])
    a2a1 = em.op("pool", _a2a1, deps=a2a_stores)
    kkl = em.dma("sp", lambda e: e.dma_start(
        sb["kk"][:, :], aps["cc_a2a_out"][:, 256:384], transpose=True), deps=[a2a1])
    vvl = em.dma("sp", lambda e: e.dma_start(
        sb["vv"][:, :],
        bass.AP(aps["cc_a2a_out"].tensor, aps["cc_a2a_out"].offset + 384,
                [[512, 128], [128 * 512, 32], [1, 128]])), deps=[a2a1])
    ctx_st = []
    for hl in range(2):
        kql = em.dma("sp", lambda e, hl=hl: e.dma_start(
            sb["kq"][:, :], aps["cc_a2a_out"][:, 128 * hl:128 * (hl + 1)],
            transpose=True), deps=[a2a1, P.get("kq_free")])
        exp_toks = {}
        for tc in range(8):
            n_st = 4 * tc + 4
            last_av = None
            zmm = None
            avq = []
            def em_av(st_e, rb_e, ex_e, pas_e, last):
                av = em.op("pe", lambda e, st_e=st_e, rb_e=rb_e, first=(st_e == 0), last=last: e.matmul(
                    sb["pz"][:, :],
                    sb["vv"][:, 128 * st_e:128 * (st_e + 1)],
                    rb_e[:, :], start=first, stop=last),
                    deps=[ex_e, vvl] + ([psdep("pz")] if st_e == 0 else []))
                zm = em.op("pe", lambda e, st_e=st_e, rb_e=rb_e, first=(st_e == 0), last=last: e.matmul(
                    sb["pz1"][:, :], sb["onecol"][:, :],
                    rb_e[:, :], start=first, stop=last),
                    deps=[onec_t] + ([psdep("pz1")] if st_e == 0 else []))
                P[("atD", pas_e)] = [av, zm]
                P[("at2D", pas_e)] = [av, zm]
                return av, zm
            for st_i in range(n_st):
                pas = st_i % 2
                smm = em.op("pe", lambda e, st_i=st_i, tc=tc, pas=pas: e.matmul(
                    sb["psA"][pas][:, :],
                    sb["kk"][:, 128 * st_i:128 * (st_i + 1)],
                    sb["kq"][:, 512 * tc:512 * (tc + 1)],
                    start=True, stop=True),
                    deps=[kql, kkl, exp_toks.get(pas)])
                if st_i == 1 and P.get("pendD") is not None:
                    P["pendD"]()
                    P["pendD"] = None
                ex = em.op("act", lambda e, pas=pas: e.activation(
                    sb["at"][pas][:, :], sb["psA"][pas][:, :], AF.Exp,
                    scale=1.0 / np.sqrt(128.0)),
                    deps=[smm, P.get(("atD", pas))])
                exp_toks[pas] = ex
                rhs_buf = sb["at"][pas]
                if st_i >= 4 * tc:
                    mk = st_i - 4 * tc
                    ex = em.op("dve", lambda e, pas=pas, mk=mk: e.tensor_tensor(
                        sb["at2"][pas][:, :], sb["at"][pas][:, :],
                        sb["mask"][mk][:, :], ALU.mult),
                        deps=[ex, mtoks[mk], P.get(("at2D", pas))])
                    rhs_buf = sb["at2"][pas]
                avq.append((st_i, rhs_buf, ex, pas))
                if len(avq) >= 2:
                    last_av, zmm = em_av(*avq.pop(0), False)
            last_av, zmm = em_av(*avq.pop(0), True)

            def mk_zflushD(hl=hl, tc=tc, last_av=last_av, zmm=zmm):
                def zf():
                    zc = em.op("dve", lambda e: e.reciprocal(
                        sb["zrb"][:, :], sb["pz1"][0:1, :]),
                        deps=[zmm, P.get("zrbD_free")], lowp=True)
                    psread("pz1", zc)
                    zbm = em.op("pe", lambda e: e.matmul(
                        sb["psB"][0][:, :], sb["ones1"][:, :], sb["zrb"][:, :],
                        start=True, stop=True), deps=[zc, psdep("psB0")])
                    P["zrbD_free"] = zbm
                    zcp = em.op("dve", lambda e: e.tensor_copy(
                        sb["zb"][:, :], sb["psB"][0][:, :]), deps=[zbm, P.get("zbD_free")])
                    psread("psB0", zcp)
                    cm = em.op("dve", lambda e, tc=tc: e.tensor_tensor(
                        sb["ctxT"][:, 512 * tc:512 * (tc + 1)],
                        sb["pz"][:, :], sb["zb"][:, :], ALU.mult),
                        deps=[zcp, last_av, P.get(("ctxT_free", tc))])
                    P["zbD_free"] = cm
                    psread("pz", cm)
                    dm = em.dma("sp", lambda e, hl=hl, tc=tc: e.dma_start(
                        bass.AP(aps["cc_ctx_in"].tensor,
                                aps["cc_ctx_in"].offset + (256 * tc + 128 * hl) * 512,
                                [[512, 128], [1, 512]]),
                        sb["ctxT"][:, 512 * tc:512 * (tc + 1)]),
                        deps=[cm] + (P.get("ctx_war") or []))
                    P[("ctxT_free", tc)] = dm
                    ctx_st.append(dm)
                return zf
            P["pendD"] = mk_zflushD()
        if P.get("pendD") is not None:
            P["pendD"]()
            P["pendD"] = None
        P["kq_free"] = smm
    def _a2a2(g):
        with allow_ici_all_to_all():
            return g.collective_compute(
                "AllToAll", ALU.bypass, replica_groups=[list(range(n_cores))],
                ins=[aps["cc_ctx_in"].opt()], outs=[aps["cc_ctx_out"].opt()])
    a2a2 = em.op("pool", _a2a2, deps=ctx_st)
    for tt in range(4):
        i_, jt = tt // 2, tt % 2
        jb = tt % 2
        cxl_t = em.dma("sp", lambda e, tt=tt, jb=jb: e.dma_start(
            sb["cxl"][jb][:, :],
            bass.AP(aps["cc_ctx_out"].tensor, aps["cc_ctx_out"].offset + 128 * tt,
                    [[512, 128], [128 * 512, DC], [1, 128]]),
            allow_slow_non_contiguous=False),
            deps=[a2a2, P.get(("cxl_free", jb))] + ([P.get("cxl0_free")] if jb == 0 else []))
        a_toks = []
        for dc2 in range(4):
            jw = dc2 % 2
            wl = em.dma("pool", lambda g, dc2=dc2, jw=jw: g.dma_start(
                sb["wbuf"][jw][:, :],
                bass.AP(aps["wo"].tensor, aps["wo"].offset + 512 * dc2,
                        [[2048, 128], [128 * 2048, DC], [1, 512]])),
                deps=[P.get(("wbufD_free", jw))])
            psl = dc2 % 2
            mm = None
            for kc in range(DC):
                mm = em.op("pe", lambda e, kc=kc, dc2=dc2, jb=jb, jw=jw, psl=psl,
                           st_=(kc == 0), sp_=(kc == DC - 1): e.matmul(
                    sb["psA"][psl][:, :],
                    sb["cxl"][jb][:, 128 * kc:128 * (kc + 1)],
                    sb["wbuf"][jw][:, 512 * kc:512 * (kc + 1)],
                    start=st_, stop=sp_),
                    deps=[wl, cxl_t] + ([psdep(f"psA{psl}")] if kc == 0 else []))
            P[("wbufD_free", jw)] = mm
            P[("cxl_free", jb)] = mm
            ao = em.op("dve", lambda e, tt=tt, dc2=dc2, psl=psl: e.tensor_copy(
                sb["aout"][:, 2048 * tt + 512 * dc2:2048 * tt + 512 * (dc2 + 1)],
                sb["psA"][psl][:, :]), deps=[mm, P.get(("aout_free", tt, dc2))])
            x1_ = em.op("dve", lambda e, i_=i_, jt=jt, dc2=dc2, psl=psl: e.scalar_tensor_tensor(
                sb["x1t"][:, 512 * dc2:512 * (dc2 + 1)], sb["psA"][psl][:, :], 1.0,
                sb["xsel"][i_][:, 2048 * jt + 512 * dc2:2048 * jt + 512 * (dc2 + 1)],
                ALU.bypass, ALU.add), deps=[mm, P.get("x1t_free")])
            psread(f"psA{psl}", x1_)
            a_toks.append([ao, x1_])
        sqs = []
        for c in range(4):
            sq = em.op("act", lambda e, c=c: e.activation(
                sb["cmp"][:, 0:512], sb["x1t"][:, 512 * c:512 * (c + 1)],
                AF.Square, accum_out=sb["lncol"][:, 20 + c:21 + c]),
                deps=[a_toks[c][1], P.get("cmp_free"), P.get(("lncol8", c))])
            P["cmp_free"] = sq
            sqs.append(sq)
        v1 = em.op("dve", lambda e: e.tensor_add(
            sb["lncol"][:, 24:25], sb["lncol"][:, 20:21], sb["lncol"][:, 21:22]), deps=sqs[:2])
        v2 = em.op("dve", lambda e: e.tensor_add(
            sb["lncol"][:, 25:26], sb["lncol"][:, 22:23], sb["lncol"][:, 23:24]), deps=sqs[2:])
        v3 = em.op("dve", lambda e: e.tensor_add(
            sb["lncol"][:, 26:27], sb["lncol"][:, 24:25], sb["lncol"][:, 25:26]), deps=[v1, v2])
        c0 = em.op("dve", lambda e: e.tensor_scalar(
            sb["lncol"][:, 27:28], sb["lncol"][:, 26:27], 1.0 / D, 1e-6,
            ALU.mult, ALU.add), deps=[v3])
        c1 = em.op("act", lambda e: e.activation(
            sb["lncol"][:, 24:25], sb["lncol"][:, 27:28], AF.Ln), deps=[c0])
        rst = em.op("act", lambda e: e.activation(
            sb["lncol"][:, 25:26], sb["lncol"][:, 24:25], AF.Exp, scale=-0.5), deps=[c1])
        h2_toks = []
        for c in range(4):
            y1 = em.op("dve", lambda e, c=c: e.tensor_scalar(
                sb["cmp"][:, 0:512], sb["x1t"][:, 512 * c:512 * (c + 1)],
                sb["lncol"][:, 25:26], None, ALU.mult),
                deps=[rst, P.get("cmp_free")])
            P[("lncol8", c)] = y1
            y2 = em.op("dve", lambda e, c=c: e.tensor_tensor(
                sb["h2"][:, 512 * c:512 * (c + 1)], sb["cmp"][:, 0:512],
                sb["ln2b"][:, 512 * c:512 * (c + 1)], ALU.mult),
                deps=[y1, bc_done[7], P.get(("h2_free", c))])
            P["cmp_free"] = y2
            h2_toks.append(y2)
        P["x1t_free"] = sqs + h2_toks
        for dcx in range(DC):
            tt_ = em.op("pe", lambda e, dcx=dcx: e.transpose(
                sb["pt"][:, :], sb["h2"][:, 128 * dcx:128 * (dcx + 1)],
                sb["ident"][:, :]), deps=h2_toks + [ident_t, psdep("pt")])
            tc_ = em.op("dve", lambda e, dcx=dcx, tt=tt: e.tensor_copy(
                sb["h2T"][:, 512 * dcx + 128 * tt:512 * dcx + 128 * (tt + 1)],
                sb["pt"][:, :]), deps=[tt_, P.get(("h2T_free", dcx, tt))])
            psread("pt", tc_)
            P.setdefault("h2T_toks", []).append(tc_)
        for c in range(4):
            P[("h2_free", c)] = P["h2T_toks"][-DC:]
    mi_st = []
    for ict in range(64):
        jb = ict % 2
        wl1 = em.dma("pool", lambda g, ict=ict, jb=jb: g.dma_start(
            sb["wbuf"][jb][:, 0:2048],
            bass.AP(aps["w_gate"].tensor, aps["w_gate"].offset + 128 * ict,
                    [[8192, 128], [128 * 8192, DC], [1, 128]])),
            deps=[P.get(("wbufD_free", jb))])
        wl2 = em.dma("pool", lambda g, ict=ict, jb=jb: g.dma_start(
            sb["wbuf"][jb][:, 2048:4096],
            bass.AP(aps["w_up"].tensor, aps["w_up"].offset + 128 * ict,
                    [[8192, 128], [128 * 8192, DC], [1, 128]])),
            deps=[P.get(("wbufD_free", jb))])
        mmg = mmu = None
        for dcx in range(DC):
            mmg = em.op("pe", lambda e, dcx=dcx, jb=jb, st_=(dcx == 0),
                        sp_=(dcx == DC - 1): e.matmul(
                sb["psA"][0][:, :],
                sb["wbuf"][jb][:, 128 * dcx:128 * (dcx + 1)],
                sb["h2T"][:, 512 * dcx:512 * (dcx + 1)],
                start=st_, stop=sp_),
                deps=[wl1] + (P.get("h2T_toks", []) + [psdep("psA0")] if dcx == 0 else []))
            mmu = em.op("pe", lambda e, dcx=dcx, jb=jb, st_=(dcx == 0),
                        sp_=(dcx == DC - 1): e.matmul(
                sb["psA"][1][:, :],
                sb["wbuf"][jb][:, 2048 + 128 * dcx:2048 + 128 * (dcx + 1)],
                sb["h2T"][:, 512 * dcx:512 * (dcx + 1)],
                start=st_, stop=sp_),
                deps=[wl2] + ([psdep("psA1")] if dcx == 0 else []))
        P[("wbufD_free", jb)] = [mmg, mmu]
        P["h2T_rd"] = [mmg, mmu]
        sl_ = em.op("act", lambda e: e.activation(
            sb["silu"][:, :], sb["psA"][0][:, :], AF.Silu),
            deps=[mmg, P.get("silu_free")])
        psread("psA0", sl_)
        mi_ = em.op("dve", lambda e: e.tensor_tensor(
            sb["mit"][:, :], sb["psA"][1][:, :], sb["silu"][:, :], ALU.mult),
            deps=[sl_, mmu, P.get("mit_free")])
        psread("psA1", mi_)
        P["silu_free"] = mi_
        dm = em.dma("sp", lambda e, ict=ict: e.dma_start(
            aps["scr_mi"][128 * ict:128 * (ict + 1), :], sb["mit"][:, :]), deps=[mi_])
        P["mit_free"] = dm
        mi_st.append(dm)
    for tt in range(4):
        i_, jt = tt // 2, tt % 2
        ml = em.dma("sp", lambda e, tt=tt: e.dma_start(
            sb["h2T"][:, :],
            bass.AP(aps["scr_mi"].tensor, aps["scr_mi"].offset + 128 * tt,
                    [[512, 128], [128 * 512, 64], [1, 128]])),
            deps=mi_st + [P.get("h2T_rd")])
        for dc2 in range(4):
            psl = dc2 % 2
            mm = None
            for q4 in range(4):
                jw = q4 % 2
                wl = em.dma("pool", lambda g, dc2=dc2, q4=q4, jw=jw: g.dma_start(
                    sb["wbuf"][jw][:, 0:512 * DC],
                    bass.AP(aps["scr_wd"].tensor,
                            aps["scr_wd"].offset + (2048 * q4) * D + 512 * dc2,
                            [[D, 128], [128 * D, DC], [1, 512]])),
                    deps=[P.get(("wbufD_free", jw))])
                for icr in range(DC):
                    icc = 16 * q4 + icr
                    mm = em.op("pe", lambda e, icc=icc, icr=icr, jw=jw, psl=psl,
                               st_=(icc == 0), sp_=(icc == 63): e.matmul(
                        sb["psB"][psl][:, :],
                        sb["h2T"][:, 128 * icc:128 * (icc + 1)],
                        sb["wbuf"][jw][:, 512 * icr:512 * (icr + 1)],
                        start=st_, stop=sp_),
                        deps=[wl, ml] + ([psdep(f"psB{psl}")] if icc == 0 else []))
                P[("wbufD_free", jw)] = mm
            P["h2T_rd"] = mm
            g1 = em.op("dve", lambda e, tt=tt, dc2=dc2, psl=psl: e.scalar_tensor_tensor(
                sb["f1"][:, :], sb["psB"][psl][:, :], 1.0,
                sb["aout"][:, 2048 * tt + 512 * dc2:2048 * tt + 512 * (dc2 + 1)],
                ALU.bypass, ALU.add), deps=[mm, P.get("f1D_free")])
            psread(f"psB{psl}", g1)
            g2 = em.op("dve", lambda e, tt=tt, dc2=dc2: e.tensor_scalar(
                sb["gD"][:, 2048 * tt + 512 * dc2:2048 * tt + 512 * (dc2 + 1)],
                sb["f1"][:, :], sb["gcol"][:, tt:tt + 1], None, ALU.mult),
                deps=[g1, P[("gsg", i_)], P.get(("gD_free", tt, dc2))])
            P["f1D_free"] = g2
            P.setdefault("gD_toks", []).append(g2)
            P[("aout_free", tt, dc2)] = None
    x4_stores = []
    for i in range(BPC):
        sc0 = i * 96
        b1 = em.dma("sp", lambda e, sc0=sc0: e.dma_start(
            bass.AP(aps["scr_srow"].tensor, aps["scr_srow"].offset, [[1, 128], [128, 16]]),
            sb["scol"][:, sc0 + 64:sc0 + 80], allow_slow_non_contiguous=True),
            deps=[P["scol_w"]])
        b2 = em.dma("sp", lambda e: e.dma_start(
            sb["sbc"][0:1, :],
            bass.AP(aps["scr_srow"].tensor, aps["scr_srow"].offset, [[2048, 1], [1, 2048]])),
            deps=[b1, P.get("sbc_free")])
        b3s = []
        for bcc in range(4):
            bmm = em.op("pe", lambda e, bcc=bcc: e.matmul(
                sb["pz"][:, :], sb["onesf"][:, :], sb["sbc"][0:1, 512 * bcc:512 * (bcc + 1)],
                start=True, stop=True), deps=[b2, onesf_t, psdep("pz")])
            bcp = em.op("dve", lambda e, bcc=bcc: e.tensor_copy(
                sb["sbc"][:, 512 * bcc:512 * (bcc + 1)], sb["pz"][:, :]), deps=[bmm])
            psread("pz", bcp)
            b3s.append(bcp)
        b3 = b3s
        pass
        pt_toks = []
        for jc in range(2):
            pt_ = em.op("dve", lambda e, jc=jc: e.tensor_scalar(
                sb["PT"][:, 2048 * jc:2048 * (jc + 1)], sb["sbc"][:, :],
                sb["iotc"][:, jc:jc + 1], None, ALU.is_equal),
                deps=[b3, ioc_t, P.get("PT_free")])
            pt_toks.append(pt_)
        P["sbc_free"] = pt_toks
        for nt in range(NT):
            for dc2 in range(4):
                psl = (nt * 4 + dc2) % 2
                lx = em.dma("sp", lambda e, i=i, nt=nt, dc2=dc2, psl=psl: e.dma_start(
                    sb["x3c"][psl][:, :],
                    aps["x3"][i][128 * nt:128 * (nt + 1), 512 * dc2:512 * (dc2 + 1)]),
                    deps=[P.get("x3cD_free")])
                mm = None
                for jc in range(2):
                    mm = em.op("pe", lambda e, jc=jc, nt=nt, dc2=dc2, i=i, psl=psl,
                               st_=(jc == 0), sp_=(jc == 1): e.matmul(
                        sb["psA"][psl][:, :],
                        sb["PT"][:, 2048 * jc + 128 * nt:2048 * jc + 128 * (nt + 1)],
                        sb["gD"][:, 2048 * (2 * i + jc) + 512 * dc2:2048 * (2 * i + jc) + 512 * (dc2 + 1)],
                        start=st_, stop=sp_),
                        deps=pt_toks + (P.get("gD_toks", []) + [psdep(f"psA{psl}")] if jc == 0 else []))
                c = 2 * (i * NT + nt)
                o1 = em.op("dve", lambda e, c=c, psl=psl: e.tensor_scalar(
                    sb["f2"][:, :], sb["x3c"][psl][:, :],
                    sb["stats"][:, c:c + 1], sb["stats"][:, c + 1:c + 2],
                    ALU.add, ALU.mult), deps=[lx, P.get("f2D_free")])
                o2 = em.op("dve", lambda e, psl=psl: e.scalar_tensor_tensor(
                    sb["x4c"][psl][:, :], sb["psA"][psl][:, :], 1.0,
                    sb["f2"][:, :], ALU.bypass, ALU.add),
                    deps=[mm, o1, P.get("x4c_free")])
                psread(f"psA{psl}", o2)
                P["f2D_free"] = o2
                o3 = em.op("dve", lambda e, psl=psl: e.tensor_add(
                    sb["f2"][:, :], sb["x4c"][psl][:, :], sb["x3c"][psl][:, :]),
                    deps=[o2])
                P["x3cD_free"] = o3
                P["x4c_free"] = o3
                dmo = em.dma("sp", lambda e, i=i, nt=nt, dc2=dc2: e.dma_start(
                    aps["x4"][i][128 * nt:128 * (nt + 1), 512 * dc2:512 * (dc2 + 1)],
                    sb["f2"][:, :]), deps=[o3])
                P["f2D_free"] = dmo
                x4_stores.append(dmo)
        P["PT_free"] = mm
    return x4_stores


def build_l1_ffn_program(n_cores=8):
    nc = bass.Bass("TRN2", target_bir_lowering=False)
    x3 = nc.dram_tensor("x3", [BPC * T, D], F32, kind="ExternalInput").ap()
    w_router = nc.dram_tensor("w_router", [D], F32, kind="ExternalInput").ap()
    wq = nc.dram_tensor("wq", [D, 2048], F32, kind="ExternalInput").ap()
    bq = nc.dram_tensor("bq", [2048], F32, kind="ExternalInput").ap()
    wk = nc.dram_tensor("wk", [D, 1024], F32, kind="ExternalInput").ap()
    bk = nc.dram_tensor("bk", [1024], F32, kind="ExternalInput").ap()
    wv = nc.dram_tensor("wv", [D, 1024], F32, kind="ExternalInput").ap()
    bv = nc.dram_tensor("bv", [1024], F32, kind="ExternalInput").ap()
    wo = nc.dram_tensor("wo", [2048, D], F32, kind="ExternalInput").ap()
    w_gate = nc.dram_tensor("w_gate", [D, 8192], F32, kind="ExternalInput").ap()
    w_up = nc.dram_tensor("w_up", [D, 8192], F32, kind="ExternalInput").ap()
    w_down = nc.dram_tensor("w_down", [8192, D], F32, kind="ExternalInput").ap()
    ln1_w = nc.dram_tensor("ln1_w", [D], F32, kind="ExternalInput").ap()
    ln2_w = nc.dram_tensor("ln2_w", [D], F32, kind="ExternalInput").ap()
    rope_cs = nc.dram_tensor("rope_cs", [T, 1024], F32, kind="ExternalInput").ap()
    x4 = nc.dram_tensor("x4", [BPC * T, D], F32, kind="ExternalOutput").ap()
    xln = nc.dram_tensor("xlnD", [BPC * T, D], BF16).ap()
    scr_srow = nc.dram_tensor("scr_srowD", [2048], F32).ap()
    scr_mi = nc.dram_tensor("scr_miD", [8192, 512], BF16).ap()
    scr_wd = nc.dram_tensor("scr_wdD", [8192 * D], BF16).ap()
    cc_a2a_in = nc.dram_tensor("cc_a2a_inD", [4096, 512], BF16).ap()
    cc_a2a_out = nc.dram_tensor("cc_a2a_outD", [4096, 512], BF16).ap()
    cc_ctx_in = nc.dram_tensor("cc_ctx_inD", [2048, 512], BF16).ap()
    cc_ctx_out = nc.dram_tensor("cc_ctx_outD", [2048, 512], BF16).ap()
    em = Em(nc)
    sb = alloc_l1_ffn_sbuf(em)
    aps = {
        "x3": [x3[i * T:(i + 1) * T, :] for i in range(BPC)],
        "x4": [x4[i * T:(i + 1) * T, :] for i in range(BPC)],
        "xln": [xln[i * T:(i + 1) * T, :] for i in range(BPC)],
        "w_router": w_router, "wq": wq, "bq": bq, "wk": wk, "bk": bk,
        "wv": wv, "bv": bv, "wo": wo, "w_gate": w_gate, "w_up": w_up,
        "w_down": w_down, "ln1_w": ln1_w, "ln2_w": ln2_w, "rope_cs": rope_cs,
        "scr_srow": scr_srow, "scr_mi": scr_mi, "scr_wd": scr_wd,
        "cc_a2a_in": cc_a2a_in, "cc_a2a_out": cc_a2a_out,
        "cc_ctx_in": cc_ctx_in, "cc_ctx_out": cc_ctx_out,
    }
    emit_l1_ffn(em, aps, sb, n_cores=n_cores)
    em.emit()
    return nc



_NC_CACHE = {}


def _get_nc(name, builder):
    if name not in _NC_CACHE:
        _NC_CACHE[name] = builder()
    return _NC_CACHE[name]


def kernel_unfused(**inp):
    f32 = np.float32
    consts = host_constants()
    x = np.ascontiguousarray(np.asarray(inp["L0_attn_x"], f32))

    def shards(arr):
        return [np.ascontiguousarray(arr[2 * c:2 * c + 2].reshape(BPC * T, D))
                for c in range(N_CORES)]

    ncA = _get_nc("A", build_l0_attn_program)
    common = {
        "wq": np.asarray(inp["L0_attn_Wq"], f32), "wk": np.asarray(inp["L0_attn_Wk"], f32),
        "wv": np.asarray(inp["L0_attn_Wv"], f32), "wo": np.asarray(inp["L0_attn_Wo"], f32),
        "rope_cos2": consts["rope64_cos2"], "rope_sin": consts["rope64_sin"],
        "rope_nsin": consts["rope64_nsin"],
    }
    im = [dict(common, x=s) for s in shards(x)]
    res = run_bass_kernel_spmd(ncA, im, core_ids=list(range(N_CORES)))
    x1s = [r["x1"] for r in res.results]

    ncB = _get_nc("B", build_l0_ffn_program)
    common = {
        "mu": np.ascontiguousarray(np.asarray(inp["L0_ffn_mu"], f32).reshape(D, 2048)),
        "gamma": np.asarray(inp["L0_ffn_gamma"], f32),
        "beta": np.asarray(inp["L0_ffn_beta"], f32),
        "scale": np.asarray(inp["L0_ffn_scale"], f32),
        "w1": np.asarray(inp["L0_ffn_W1"], f32), "b1": np.asarray(inp["L0_ffn_b1"], f32),
        "w2": np.asarray(inp["L0_ffn_W2"], f32), "b2": np.asarray(inp["L0_ffn_b2"], f32),
    }
    im = [dict(common, x1=x1s[c]) for c in range(N_CORES)]
    res = run_bass_kernel_spmd(ncB, im, core_ids=list(range(N_CORES)))
    x2s = [r["x2"] for r in res.results]

    ncC = _get_nc("C", build_l1_attn_program)
    common = {
        "wq": np.asarray(inp["L1_attn_Wq"], f32), "bq": np.asarray(inp["L1_attn_bq"], f32),
        "wk": np.asarray(inp["L1_attn_Wk"], f32), "bk": np.asarray(inp["L1_attn_bk"], f32),
        "wv": np.asarray(inp["L1_attn_Wv"], f32), "bv": np.asarray(inp["L1_attn_bv"], f32),
        "wg": np.asarray(inp["L1_attn_Wg"], f32), "bg": np.asarray(inp["L1_attn_bg"], f32),
        "wmb": np.asarray(inp["L1_attn_Wmb"], f32), "bmb": np.asarray(inp["L1_attn_bmb"], f32),
        "memory": np.asarray(inp["L1_attn_memory"], f32),
        "norm": np.asarray(inp["L1_attn_norm"], f32),
    }
    im = [dict(common, x2=x2s[c]) for c in range(N_CORES)]
    res = run_bass_kernel_spmd(ncC, im, core_ids=list(range(N_CORES)))
    x3s = [r["x3"] for r in res.results]

    ncD = _get_nc("D", lambda: build_l1_ffn_program(n_cores=8))
    common = {
        "w_router": np.asarray(inp["L1_ffn_w_router"], f32),
        "wq": np.asarray(inp["L1_ffn_wq"], f32), "bq": np.asarray(inp["L1_ffn_bq"], f32),
        "wk": np.asarray(inp["L1_ffn_wk"], f32), "bk": np.asarray(inp["L1_ffn_bk"], f32),
        "wv": np.asarray(inp["L1_ffn_wv"], f32), "bv": np.asarray(inp["L1_ffn_bv"], f32),
        "wo": np.asarray(inp["L1_ffn_wo"], f32),
        "w_gate": np.asarray(inp["L1_ffn_w_gate"], f32),
        "w_up": np.asarray(inp["L1_ffn_w_up"], f32),
        "w_down": np.asarray(inp["L1_ffn_w_down"], f32),
        "ln1_w": np.asarray(inp["L1_ffn_ln1_w"], f32),
        "ln2_w": np.asarray(inp["L1_ffn_ln2_w"], f32),
        "rope_cs": consts["rope128_cs"],
    }
    im = [dict(common, x3=x3s[c]) for c in range(N_CORES)]
    res = run_bass_kernel_spmd(ncD, im, core_ids=list(range(N_CORES)))
    out = np.concatenate([r["x4"].reshape(BPC, T, D) for r in res.results], axis=0)
    return out



def build_fused_program():
    nc = bass.Bass("TRN2", target_bir_lowering=False)
    sem_stack = ExitStack()
    shared = {}

    x = nc.dram_tensor("x", [BPC * T, D], F32, kind="ExternalInput").ap()
    out = nc.dram_tensor("out", [BPC * T, D], F32, kind="ExternalOutput").ap()
    x1 = nc.dram_tensor("fx1", [BPC * T, D], F32).ap()
    x2 = nc.dram_tensor("fx2", [BPC * T, D], F32).ap()
    x3 = nc.dram_tensor("fx3", [BPC * T, D], F32).ap()

    def ext(name, shape):
        return nc.dram_tensor(name, shape, F32, kind="ExternalInput").ap()

    a_aps = {
        "x": [x[i * T:(i + 1) * T, :] for i in range(BPC)],
        "x1": [x1[i * T:(i + 1) * T, :] for i in range(BPC)],
        "scr_q": nc.dram_tensor("a_scr_q", [T, D], BF16).ap(),
        "scr_k": nc.dram_tensor("a_scr_k", [T, 1024], BF16).ap(),
        "wq": ext("a_wq", [D, D]), "wk": ext("a_wk", [D, 512]),
        "wv": ext("a_wv", [D, 512]), "wo": ext("a_wo", [D, D]),
        "rope_cos2": ext("a_rc2", [T, 512]), "rope_sin": ext("a_rs", [T, 256]),
        "rope_nsin": ext("a_rn", [T, 256]),
    }
    axln = nc.dram_tensor("a_xln", [BPC * T, D], BF16).ap()
    a_aps["xln"] = [axln[i * T:(i + 1) * T, :] for i in range(BPC)]
    em = Em(nc, sem_stack, shared)
    sb = alloc_l0_attn_sbuf(em)
    emit_l0_attn(em, a_aps, sb)
    em.emit()
    em.stack.close()

    bxln = nc.dram_tensor("b_xln", [BPC * T, D], BF16).ap()
    b_aps = {
        "x1": [x1[i * T:(i + 1) * T, :] for i in range(BPC)],
        "x2": [x2[i * T:(i + 1) * T, :] for i in range(BPC)],
        "xln": [bxln[i * T:(i + 1) * T, :] for i in range(BPC)],
        "scr_e": nc.dram_tensor("b_scr_e", [T, 2048], BF16).ap(),
        "scr_row": nc.dram_tensor("b_scr_row", [1, 2048], BF16).ap(),
        "scr_mun": nc.dram_tensor("b_scr_mun", [D * 2048], BF16).ap(),
        "mu": ext("b_mu", [D, 2048]), "gamma": ext("b_gamma", [D]),
        "beta": ext("b_beta", [D]), "scale": ext("b_scale", [1]),
        "w1": ext("b_w1", [16, D, 512]), "b1": ext("b_b1", [16, 512]),
        "w2": ext("b_w2", [16, 512, D]), "b2": ext("b_b2", [16, D]),
    }
    em = Em(nc, sem_stack, shared)
    sb = alloc_l0_ffn_sbuf(em)
    emit_l0_ffn(em, b_aps, sb)
    em.emit()
    em.stack.close()

    cxln = nc.dram_tensor("c_xln", [BPC * T, D], BF16).ap()
    cmk = nc.dram_tensor("c_scr_mk", [BPC * T, 3072], BF16).ap()
    cmq = nc.dram_tensor("c_scr_mq", [BPC * T, 12288], BF16).ap()
    cmb = nc.dram_tensor("c_scr_mb", [BPC * T, 2048], BF16).ap()
    cg = nc.dram_tensor("c_scr_g", [BPC * T, 2048], BF16).ap()
    c_aps = {
        "x2": [x2[i * T:(i + 1) * T, :] for i in range(BPC)],
        "x3": [x3[i * T:(i + 1) * T, :] for i in range(BPC)],
        "xln": [cxln[i * T:(i + 1) * T, :] for i in range(BPC)],
        "scr_mk": [cmk[i * T:(i + 1) * T, :] for i in range(BPC)],
        "scr_mq": [cmq[i * T:(i + 1) * T, :] for i in range(BPC)],
        "scr_mb": [cmb[i * T:(i + 1) * T, :] for i in range(BPC)],
        "scr_g": [cg[i * T:(i + 1) * T, :] for i in range(BPC)],
        "scr_w": nc.dram_tensor("c_scr_w", [3 * D * D + 2 * 512 * D], BF16).ap(),
        "cc_in": nc.dram_tensor("c_cc_in", [128, 12384], F32).ap(),
        "cc_out": nc.dram_tensor("c_cc_out", [128, 12384], F32).ap(),
        "wq": ext("c_wq", [D, D]), "wk": ext("c_wk", [512, D]),
        "wv": ext("c_wv", [512, D]), "wg": ext("c_wg", [D, D]),
        "wmb": ext("c_wmb", [D, D]),
        "bq": ext("c_bq", [D]), "bk": ext("c_bk", [512]), "bv": ext("c_bv", [512]),
        "bg": ext("c_bg", [D]), "bmb": ext("c_bmb", [D]),
        "memory": ext("c_memory", [16, 768, 128]), "norm": ext("c_norm", [16, 768]),
    }
    em = Em(nc, sem_stack, shared)
    sb = alloc_l1_attn_sbuf(em)
    emit_l1_attn(em, c_aps, sb, n_cores=N_CORES)
    em.emit()
    em.stack.close()

    dxln = nc.dram_tensor("d_xln", [BPC * T, D], BF16).ap()
    d_aps = {
        "x3": [x3[i * T:(i + 1) * T, :] for i in range(BPC)],
        "x4": [out[i * T:(i + 1) * T, :] for i in range(BPC)],
        "xln": [dxln[i * T:(i + 1) * T, :] for i in range(BPC)],
        "w_router": ext("d_w_router", [D]),
        "wq": ext("d_wq", [D, 2048]), "bq": ext("d_bq", [2048]),
        "wk": ext("d_wk", [D, 1024]), "bk": ext("d_bk", [1024]),
        "wv": ext("d_wv", [D, 1024]), "bv": ext("d_bv", [1024]),
        "wo": ext("d_wo", [2048, D]),
        "w_gate": ext("d_w_gate", [D, 8192]), "w_up": ext("d_w_up", [D, 8192]),
        "w_down": ext("d_w_down", [8192, D]),
        "ln1_w": ext("d_ln1_w", [D]), "ln2_w": ext("d_ln2_w", [D]),
        "rope_cs": ext("d_rope_cs", [T, 1024]),
        "scr_srow": nc.dram_tensor("d_scr_srow", [2048], F32).ap(),
        "scr_mi": nc.dram_tensor("d_scr_mi", [8192, 512], BF16).ap(),
        "scr_wd": nc.dram_tensor("d_scr_wd", [8192 * D], BF16).ap(),
        "cc_a2a_in": nc.dram_tensor("d_cc_a2a_in", [4096, 512], BF16).ap(),
        "cc_a2a_out": nc.dram_tensor("d_cc_a2a_out", [4096, 512], BF16).ap(),
        "cc_ctx_in": nc.dram_tensor("d_cc_ctx_in", [2048, 512], BF16).ap(),
        "cc_ctx_out": nc.dram_tensor("d_cc_ctx_out", [2048, 512], BF16).ap(),
    }
    em = Em(nc, sem_stack, shared)
    sb = alloc_l1_ffn_sbuf(em)
    emit_l1_ffn(em, d_aps, sb, n_cores=N_CORES)
    em.emit()
    em.stack.close()
    return nc


def kernel_fused(**inp):
    f32 = np.float32
    consts = host_constants()
    x = np.ascontiguousarray(np.asarray(inp["L0_attn_x"], f32))
    nc = _get_nc("FUSED", build_fused_program)
    common = {
        "a_wq": np.asarray(inp["L0_attn_Wq"], f32), "a_wk": np.asarray(inp["L0_attn_Wk"], f32),
        "a_wv": np.asarray(inp["L0_attn_Wv"], f32), "a_wo": np.asarray(inp["L0_attn_Wo"], f32),
        "a_rc2": consts["rope64_cos2"], "a_rs": consts["rope64_sin"],
        "a_rn": consts["rope64_nsin"],
        "b_mu": np.ascontiguousarray(np.asarray(inp["L0_ffn_mu"], f32).reshape(D, 2048)),
        "b_gamma": np.asarray(inp["L0_ffn_gamma"], f32),
        "b_beta": np.asarray(inp["L0_ffn_beta"], f32),
        "b_scale": np.asarray(inp["L0_ffn_scale"], f32),
        "b_w1": np.asarray(inp["L0_ffn_W1"], f32), "b_b1": np.asarray(inp["L0_ffn_b1"], f32),
        "b_w2": np.asarray(inp["L0_ffn_W2"], f32), "b_b2": np.asarray(inp["L0_ffn_b2"], f32),
        "c_wq": np.asarray(inp["L1_attn_Wq"], f32), "c_bq": np.asarray(inp["L1_attn_bq"], f32),
        "c_wk": np.asarray(inp["L1_attn_Wk"], f32), "c_bk": np.asarray(inp["L1_attn_bk"], f32),
        "c_wv": np.asarray(inp["L1_attn_Wv"], f32), "c_bv": np.asarray(inp["L1_attn_bv"], f32),
        "c_wg": np.asarray(inp["L1_attn_Wg"], f32), "c_bg": np.asarray(inp["L1_attn_bg"], f32),
        "c_wmb": np.asarray(inp["L1_attn_Wmb"], f32),
        "c_bmb": np.asarray(inp["L1_attn_bmb"], f32),
        "c_memory": np.asarray(inp["L1_attn_memory"], f32),
        "c_norm": np.asarray(inp["L1_attn_norm"], f32),
        "d_w_router": np.asarray(inp["L1_ffn_w_router"], f32),
        "d_wq": np.asarray(inp["L1_ffn_wq"], f32), "d_bq": np.asarray(inp["L1_ffn_bq"], f32),
        "d_wk": np.asarray(inp["L1_ffn_wk"], f32), "d_bk": np.asarray(inp["L1_ffn_bk"], f32),
        "d_wv": np.asarray(inp["L1_ffn_wv"], f32), "d_bv": np.asarray(inp["L1_ffn_bv"], f32),
        "d_wo": np.asarray(inp["L1_ffn_wo"], f32),
        "d_w_gate": np.asarray(inp["L1_ffn_w_gate"], f32),
        "d_w_up": np.asarray(inp["L1_ffn_w_up"], f32),
        "d_w_down": np.asarray(inp["L1_ffn_w_down"], f32),
        "d_ln1_w": np.asarray(inp["L1_ffn_ln1_w"], f32),
        "d_ln2_w": np.asarray(inp["L1_ffn_ln2_w"], f32),
        "d_rope_cs": consts["rope128_cs"],
    }
    im = [dict(common, x=np.ascontiguousarray(x[2 * c:2 * c + 2].reshape(BPC * T, D)))
          for c in range(N_CORES)]
    res = run_bass_kernel_spmd(nc, im, core_ids=list(range(N_CORES)))
    return np.concatenate([r["out"].reshape(BPC, T, D) for r in res.results], axis=0)


def kernel(**inp):
    return kernel_fused(**inp)


# version 09 (final)
# speedup vs baseline: 1.0091x; 1.0091x over previous
import math
from contextlib import ExitStack

import numpy as np
import ml_dtypes

import concourse.bass as bass
import concourse.mybir as mybir
from concourse.bass_utils import run_bass_kernel_spmd

F32 = mybir.dt.float32
BF16 = mybir.dt.bfloat16
AF = mybir.ActivationFunctionType
ALU = mybir.AluOpType
AX = mybir.AxisListType
BF = ml_dtypes.bfloat16

N_CORES = 8
B = 16
BPC = B // N_CORES
T = 2048
D = 2048
NT = T // 128
DC = D // 128
EPS_LN = 1e-5


COMPUTE_ENGINES = ("pe", "act", "dve", "pool", "sp")


DMA_SLOTS = 12


_EM_SEQ = [0]


class Em:
    def __init__(self, nc, sem_stack=None, shared=None):
        _EM_SEQ[0] += 1
        self.tag = f"p{_EM_SEQ[0]}_"
        self.nc = nc
        self.ops = []
        self.stack = ExitStack()
        self.sem_stack = sem_stack if sem_stack is not None else self.stack
        if shared is not None:
            self.counts = shared.setdefault("counts", {})
            self.sems = shared.setdefault("sems", {})
        else:
            self.counts = {}
            self.sems = {}
        self._sem_keys = []
        self.dma_idx = {}
        self.dma_slot_prev = {}

    def _tok(self, sem_key, inc):
        self.counts[sem_key] = self.counts.get(sem_key, 0) + inc
        if sem_key not in self._sem_keys:
            self._sem_keys.append(sem_key)
        return (sem_key, self.counts[sem_key])

    @staticmethod
    def _flat(deps):
        out = []
        for d in deps:
            if d is None:
                continue
            if isinstance(d, list):
                out.extend(t for t in d if t is not None)
            else:
                out.append(d)
        return tuple(out)

    def op(self, eng, fn, deps=(), lowp=False):
        tok = self._tok(eng, 1)
        if lowp:
            inner = fn

            def fn(e, inner=inner):
                with self.nc.allow_low_precision(reason="feeds bf16 matmul"):
                    return inner(e)
        self.ops.append((eng, fn, self._flat(deps), eng, 1))
        return tok

    def dma(self, eng, fn, deps=()):
        idx = self.dma_idx.get(eng, 0)
        self.dma_idx[eng] = idx + 1
        slot = idx % DMA_SLOTS
        key = f"{eng}_dma{slot}"
        slots = self.dma_slot_prev.setdefault(eng, {})
        prev_tok = slots.get(slot)
        tok = self._tok(key, 16)
        deps = self._flat(deps)
        if prev_tok is not None:
            deps = deps + (prev_tok,)
        self.ops.append((eng, fn, deps, key, 16))
        slots[slot] = tok
        return tok

    def qbar(self, eng):
        return list(self.dma_slot_prev.get(eng, {}).values())

    def emit(self):
        nc = self.nc
        for key in self._sem_keys:
            if key not in self.sems:
                self.sems[key] = self.sem_stack.enter_context(nc.semaphore("s_" + key))
        per_eng = {e: [] for e in COMPUTE_ENGINES}
        for rec in self.ops:
            per_eng[rec[0]].append(rec)
        waited = {e: {} for e in COMPUTE_ENGINES}

        def run_stream(eng_name, eng):
            for _, fn, deps, sem_key, inc in per_eng[eng_name]:
                w = waited[eng_name]
                for dep_key, dep_val in deps:
                    if w.get(dep_key, 0) < dep_val:
                        eng.wait_ge(self.sems[dep_key], dep_val)
                        w[dep_key] = dep_val
                ins = fn(eng)
                ins.then_inc(self.sems[sem_key], inc)

        with nc.Block() as blk:
            if per_eng["pe"]:
                blk.tensor(lambda e: run_stream("pe", e))
            if per_eng["act"]:
                blk.scalar(lambda e: run_stream("act", e))
            if per_eng["dve"]:
                blk.vector(lambda e: run_stream("dve", e))
            if per_eng["pool"]:
                blk.gpsimd(lambda e: run_stream("pool", e))
            if per_eng["sp"]:
                blk.sync(lambda e: run_stream("sp", e))
        final = dict(self.counts)
        with nc.Block() as blk2:
            def _fin(eng):
                for key, val in final.items():
                    eng.wait_ge(self.sems[key], val)
            blk2.sync(_fin)
        self.ops = []



def host_constants():
    c = {}
    hd = 64
    theta = 1.0 / (10000.0 ** (np.arange(0, hd, 2, dtype=np.float64) / hd))
    pos = np.arange(T, dtype=np.float64)
    ang = np.outer(pos, theta)
    cs = np.cos(ang)
    cos2 = np.concatenate([cs, cs], axis=1)
    sn = np.sin(ang)
    c["rope64_cos2"] = np.tile(cos2, (1, 8)).astype(np.float32)
    c["rope64_sin"] = np.tile(sn, (1, 8)).astype(np.float32)
    c["rope64_nsin"] = np.tile(-sn, (1, 8)).astype(np.float32)
    hd = 128
    inv = 1.0 / (10000.0 ** (np.arange(0, hd, 2, dtype=np.float64) / hd))
    ang = np.outer(pos, inv)
    cos128 = np.concatenate([np.cos(ang), np.cos(ang)], axis=1)
    ssin = np.concatenate([-np.sin(ang), np.sin(ang)], axis=1)
    c["rope128_cs"] = np.concatenate(
        [np.tile(cos128, (1, 4)), np.tile(ssin, (1, 4))], axis=1).astype(np.float32)
    return c



def make_identity_bf16(em, ident_sb):
    t0 = em.op("pool", lambda g: g.memset(ident_sb[:, :], 0.0))
    return em.op("pool", lambda g: g.affine_select(
        ident_sb[:, :], ident_sb[:, :], [[-1, 128]], ALU.not_equal, 1.0,
        base=0, channel_multiplier=1), deps=[t0])


def causal_keep_mask(em, mask_sb, delta):
    t0 = em.op("pool", lambda g: g.memset(mask_sb[:, :], 1.0))
    return em.op("pool", lambda g: g.affine_select(
        mask_sb[:, :], mask_sb[:, :], [[1, mask_sb.shape[1]]], ALU.is_ge, 0.0,
        base=-delta, channel_multiplier=-1), deps=[t0])



def emit_ln_pass(em, x_ap, xln_ap, sb, bi, store_stats=False):
    toks = []
    nbuf = len(sb["lnx"])
    prev = [dict(s2=None, s5=None, st=None) for _ in range(nbuf)]
    for t in range(NT):
        j = t % nbuf
        lnx, lnxln = sb["lnx"][j], sb["lnxln"][j]
        lnsq = lnxln
        co = 8 * j
        P = prev[j]
        ld = em.dma("sp", lambda e, lnx=lnx, t=t: e.dma_start(
            lnx[:, :], x_ap[128 * t:128 * (t + 1), :]),
            deps=[P["s5"], P["s2"]])
        s0 = em.op("dve", lambda e, lnx=lnx, co=co: e.reduce_sum(
            sb["lncol"][:, co:co + 1], lnx[:, :], axis=AX.X),
            deps=[ld, P["s5"], P["s2"]])
        s1 = em.op("dve", lambda e, co=co: e.tensor_scalar_mul(
            sb["lncol"][:, co + 1:co + 2], sb["lncol"][:, co:co + 1], -1.0 / D),
            deps=[s0])
        s2 = em.op("act", lambda e, lnx=lnx, lnsq=lnsq, co=co: e.activation(
            lnsq[:, :], lnx[:, :], AF.Square, bias=sb["lncol"][:, co + 1:co + 2],
            accum_out=sb["lncol"][:, co + 2:co + 3]), deps=[s1, ld, P["s2"], P["st"]])
        s2b = em.op("dve", lambda e, co=co: e.tensor_scalar(
            sb["lncol"][:, co + 5:co + 6], sb["lncol"][:, co + 2:co + 3],
            1.0 / D, EPS_LN, ALU.mult, ALU.add), deps=[s2])
        s3 = em.op("act", lambda e, co=co: e.activation(
            sb["lncol"][:, co + 3:co + 4], sb["lncol"][:, co + 5:co + 6],
            AF.Ln), deps=[s2b])
        s4 = em.op("act", lambda e, co=co: e.activation(
            sb["lncol"][:, co + 4:co + 5], sb["lncol"][:, co + 3:co + 4], AF.Exp,
            scale=-0.5), deps=[s3])
        if store_stats:
            c = 2 * (bi * NT + t)
            em.op("dve", lambda e, c=c, co=co: e.tensor_copy(
                sb["stats"][:, c:c + 1], sb["lncol"][:, co + 1:co + 2]), deps=[s1])
            em.op("dve", lambda e, c=c, co=co: e.tensor_copy(
                sb["stats"][:, c + 1:c + 2], sb["lncol"][:, co + 4:co + 5]), deps=[s4])
        s5 = em.op("dve", lambda e, lnx=lnx, lnxln=lnxln, co=co: e.tensor_scalar(
            lnxln[:, :], lnx[:, :], sb["lncol"][:, co + 1:co + 2],
            sb["lncol"][:, co + 4:co + 5], ALU.add, ALU.mult),
            deps=[s4, ld, P["st"]])
        st = em.dma("sp", lambda e, lnxln=lnxln, t=t: e.dma_start(
            xln_ap[128 * t:128 * (t + 1), :], lnxln[:, :]), deps=[s5])
        prev[j] = dict(s2=s2, s5=s5, st=st)
        toks.append(st)
    return toks



def alloc_l0_attn_sbuf(em):
    nc = em.nc
    st = em.stack
    sb = {}
    sb["lnx"] = [st.enter_context(nc.sbuf_tensor("lnx0", [128, D], F32))]
    sb["lnxln"] = [st.enter_context(nc.sbuf_tensor("lnxln0", [128, D], BF16))]
    sb["lncol"] = st.enter_context(nc.sbuf_tensor("lncol", [128, 16], F32))
    sb["big0"] = st.enter_context(nc.sbuf_tensor("big0", [128, 32768], BF16))
    sb["kq"] = [st.enter_context(nc.sbuf_tensor(f"kq{j}", [128, T], BF16)) for j in range(2)]
    sb["kk"] = [st.enter_context(nc.sbuf_tensor(f"kk{j}", [128, T], BF16)) for j in range(2)]
    sb["vaug"] = st.enter_context(nc.sbuf_tensor("vaug", [128, 520 * NT], BF16))
    sb["wbuf"] = [st.enter_context(nc.sbuf_tensor(f"wbuf{j}", [128, 512 * 16], BF16)) for j in range(2)]
    sb["ctab"] = st.enter_context(nc.sbuf_tensor("ctab", [128, 512 * NT], BF16))
    sb["stab"] = st.enter_context(nc.sbuf_tensor("stab", [128, 256 * NT], BF16))
    sb["nstab"] = st.enter_context(nc.sbuf_tensor("nstab", [128, 256 * NT], BF16))
    sb["rt1"] = [st.enter_context(nc.sbuf_tensor(f"rt1_{j}", [128, 512], BF16)) for j in range(2)]
    sb["rt2"] = [st.enter_context(nc.sbuf_tensor(f"rt2_{j}", [128, 512], BF16)) for j in range(2)]
    sb["qrot"] = [st.enter_context(nc.sbuf_tensor(f"qrot{j}", [128, 512], BF16)) for j in range(2)]
    sb["at"] = [st.enter_context(nc.sbuf_tensor(f"at{j}", [128, 512], BF16)) for j in range(2)]
    sb["at2"] = [st.enter_context(nc.sbuf_tensor(f"at2_{j}", [128, 512], BF16)) for j in range(2)]
    sb["mask"] = [st.enter_context(nc.sbuf_tensor(f"mask{j}", [128, 512], BF16)) for j in range(4)]
    sb["zrb"] = st.enter_context(nc.sbuf_tensor("zrb", [1, 512], BF16))
    sb["zb"] = [st.enter_context(nc.sbuf_tensor(f"zb{j}", [64, 512], BF16)) for j in range(3)]
    sb["ones1"] = st.enter_context(nc.sbuf_tensor("ones1", [1, 128], BF16))
    sb["xc"] = [st.enter_context(nc.sbuf_tensor(f"xc{j}", [128, 512], F32)) for j in range(2)]
    sb["x1c"] = [st.enter_context(nc.sbuf_tensor(f"x1c{j}", [128, 512], F32)) for j in range(2)]
    sb["pa"] = [st.enter_context(nc.psum_tensor(f"pa{j}", [128, 512], F32)) for j in range(2)]
    sb["pc"] = [st.enter_context(nc.psum_tensor(f"pc{j}", [128, 512], F32)) for j in range(3)]
    sb["pz"] = st.enter_context(nc.psum_tensor("pz", [64, 512], F32))
    sb["pp"] = [st.enter_context(nc.psum_tensor(f"pp{j}", [128, 512], F32)) for j in range(2)]
    return sb


def emit_l0_attn(em, aps, sb):
    mtoks = [causal_keep_mask(em, sb["mask"][j], 128 * j) for j in range(4)]
    ones_t = em.op("pool", lambda g: g.memset(sb["ones1"][:, :], 1.0))
    def _tab_ap(ap, w):
        return bass.AP(ap.tensor, ap.offset, [[w, 128], [128 * w, NT], [1, w]])
    tc_t = em.dma("pool", lambda e: e.dma_start(
        sb["ctab"][:, :], _tab_ap(aps["rope_cos2"], 512)))
    ts_t = em.dma("pool", lambda e: e.dma_start(
        sb["stab"][:, :], _tab_ap(aps["rope_sin"], 256)))
    tn_t = em.dma("pool", lambda e: e.dma_start(
        sb["nstab"][:, :], _tab_ap(aps["rope_nsin"], 256)))
    rope_ready = [tc_t, ts_t, tn_t]

    prev_wo_done = None
    prev_batch = {}

    for i in range(BPC):
        ln_toks = emit_ln_pass(em, aps["x"][i], aps["xln"][i], sb, i)
        xlnT = []
        for j in range(DC):
            dep = list(ln_toks) + ([prev_wo_done] if j == 0 else [])
            xlnT.append(em.dma("sp", lambda e, j=j, i=i: e.dma_start(
                sb["big0"][:, 2048 * j:2048 * (j + 1)],
                aps["xln"][i][:, 128 * j:128 * (j + 1)], transpose=True), deps=dep))
        xlnT_done = xlnT
        vone = em.op("pool", lambda g: g.memset(
            bass.AP(sb["vaug"], 64, [[520 * NT, 128], [520, NT], [65, 8], [1, 1]]), 1.0),
            deps=[prev_batch.get("av_done")])

        def load_w(ap_w, col0, ncols, jbuf, deps):
            wap = aps[ap_w]
            src_ap = bass.AP(wap.tensor, wap.offset + col0,
                             [[wap.shape[1], 128], [128 * wap.shape[1], DC], [1, ncols]])
            return em.dma("pool", lambda g: g.dma_start(
                sb["wbuf"][jbuf][:, 0:ncols * DC], src_ap), deps=deps)

        def proj_tile(t, jb, ncols, psl, deps_extra=()):
            mm = None
            for kc in range(DC):
                st_ = (kc == 0)
                sp_ = (kc == DC - 1)
                mm = em.op("pe", lambda e, t=t, kc=kc, jb=jb, ncols=ncols, psl=psl, st_=st_, sp_=sp_: e.matmul(
                    sb["pp"][psl][0:128, 0:ncols],
                    sb["big0"][:, 2048 * kc + 128 * t:2048 * kc + 128 * (t + 1)],
                    sb["wbuf"][jb][:, ncols * kc:ncols * (kc + 1)],
                    start=st_, stop=sp_),
                    deps=(list(deps_extra) + xlnT_done if kc == 0 else []))
            return mm

        def rope_evac(t, psl, dma_dst, rr, deps_mm, deps_dma=()):
            r1 = em.op("dve", lambda e, psl=psl, rr=rr, t=t: e.scalar_tensor_tensor(
                sb["rt1"][rr][:, :], sb["pp"][psl][:, :], 1.0,
                sb["ctab"][:, 512 * t:512 * (t + 1)],
                ALU.bypass, ALU.mult),
                deps=deps_mm + rope_ready + [prev_batch.get(("rope1", rr))])
            r2 = em.op("dve", lambda e, psl=psl, rr=rr, t=t: e.tensor_tensor(
                bass.AP(sb["rt2"][rr], 0, [[512, 128], [64, 8], [1, 32]]),
                bass.AP(sb["pp"][psl], 32, [[512, 128], [64, 8], [1, 32]]),
                bass.AP(sb["nstab"], 256 * t, [[256 * NT, 128], [32, 8], [1, 32]]),
                ALU.mult),
                deps=deps_mm + rope_ready + [prev_batch.get(("rope2", rr))])
            r3 = em.op("dve", lambda e, psl=psl, rr=rr, t=t: e.tensor_tensor(
                bass.AP(sb["rt2"][rr], 32, [[512, 128], [64, 8], [1, 32]]),
                bass.AP(sb["pp"][psl], 0, [[512, 128], [64, 8], [1, 32]]),
                bass.AP(sb["stab"], 256 * t, [[256 * NT, 128], [32, 8], [1, 32]]),
                ALU.mult),
                deps=deps_mm + [prev_batch.get(("rope2", rr))])
            r4 = em.op("dve", lambda e, rr=rr: e.tensor_add(
                sb["qrot"][rr][:, :], sb["rt1"][rr][:, :], sb["rt2"][rr][:, :]),
                deps=[r1, r2, r3, prev_batch.get(("rope4", rr))])
            prev_batch[("rope1", rr)] = r4
            prev_batch[("rope2", rr)] = r4
            if dma_dst is None:
                return r4, None
            dm = em.dma("sp", lambda e, rr=rr: e.dma_start(dma_dst, sb["qrot"][rr][:, :]),
                        deps=[r4] + list(deps_dma))
            prev_batch[("rope4", rr)] = dm
            return r4, dm

        scr_war = prev_batch.get("scr_load_last")
        wbuf_reader = prev_batch.get("wbuf_reader", [None, None])
        wk_t = load_w("wk", 0, 512, 0, [wbuf_reader[0]])
        scr_k_stores = []
        pp_free = prev_batch.get("pp_free", [None, None])
        rr = 0
        mm = None
        for t in range(NT):
            psl = t % 2
            mm = proj_tile(t, 0, 512, psl, deps_extra=[wk_t, pp_free[psl]])
            r4, _ = rope_evac(t, psl, None, rr, [mm])
            krr = rr
            rr ^= 1
            kap = aps["scr_k"]
            kdms = []
            for half in range(2):
                dst = bass.AP(kap.tensor,
                              kap.offset + (128 * t) * 1024 + 64 * half,
                              [[1024, 128], [128, 8], [1, 64]])
                dm = em.dma("sp", lambda e, dst=dst, krr=krr: e.dma_start(
                    dst, bass.AP(sb["qrot"][krr], 0, [[512, 128], [64, 8], [1, 64]])),
                    deps=[r4] + ([scr_war] if t == 0 and half == 0 else []))
                scr_k_stores.append(dm)
                kdms.append(dm)
            prev_batch[("rope4", krr)] = kdms
            pp_free[psl] = r4
        wbuf_reader[0] = mm
        wv_t = load_w("wv", 0, 512, 1, [wbuf_reader[1]])
        v_evacs = []
        for t in range(NT):
            psl = t % 2
            mm = proj_tile(t, 1, 512, psl, deps_extra=[wv_t, pp_free[psl]])
            ev = em.op("dve", lambda e, t=t, psl=psl: e.tensor_copy(
                bass.AP(sb["vaug"], 520 * t, [[520 * NT, 128], [65, 8], [1, 64]]),
                sb["pp"][psl][:, :]),
                deps=[mm, vone, prev_batch.get("av_done")])
            pp_free[psl] = ev
            v_evacs.append(ev)
        wbuf_reader[1] = mm
        v_done = v_evacs[-1]
        scr_q_stores = []
        for dqc in range(4):
            jb = dqc % 2
            wq_t = load_w("wq", 512 * dqc, 512, jb, [wbuf_reader[jb]])
            for t in range(NT):
                psl = t % 2
                mm = proj_tile(t, jb, 512, psl, deps_extra=[wq_t, pp_free[psl]])
                r4, dm = rope_evac(
                    t, psl, aps["scr_q"][128 * t:128 * (t + 1), 512 * dqc:512 * (dqc + 1)],
                    rr, [mm])
                pp_free[psl] = r4
                rr ^= 1
                scr_q_stores.append(dm)
            wbuf_reader[jb] = mm
        q_done = scr_q_stores
        k_done = scr_k_stores

        kq_free = [None, None]
        kk_free = [None, None]
        kq_tok = {}
        kk_tok = {}
        exp_toks = {}
        ctx_evac = [None, None, None]
        av_done_all = []
        qpair_prev = -1
        kpair_prev = -1
        pending_z = []

        def make_zflush(pcs, h, tc, last_av):
            def zflush():
                zc = em.op("dve", lambda e, pcs=pcs: e.reciprocal(
                    sb["zrb"][:, :], sb["pc"][pcs][64:65, :]),
                    deps=[last_av, prev_batch.get("zrb_free")], lowp=True)
                zmm = em.op("pe", lambda e: e.matmul(
                    sb["pz"][:, :], sb["ones1"][:, 0:64], sb["zrb"][:, :],
                    start=True, stop=True),
                    deps=[zc, prev_batch.get("pz_free")])
                prev_batch["zrb_free"] = zmm
                zcp = em.op("dve", lambda e, pcs=pcs: e.tensor_copy(
                    sb["zb"][pcs][:, :], sb["pz"][:, :]),
                    deps=[zmm, prev_batch.get(("zb_free", pcs))])
                prev_batch["pz_free"] = zcp
                cm = em.op("dve", lambda e, pcs=pcs, h=h, tc=tc: e.tensor_tensor(
                    sb["big0"][64 * (h % 2):64 * (h % 2) + 64,
                               2048 * (h // 2) + 512 * tc:2048 * (h // 2) + 512 * (tc + 1)],
                    sb["pc"][pcs][0:64, :], sb["zb"][pcs][:, :], ALU.mult),
                    deps=[zcp, last_av] + xlnT_done)
                prev_batch[("zb_free", pcs)] = cm
                ctx_evac[pcs] = cm
                return cm
            return zflush

        for h in range(32):
            qp = h // 2
            hq = qp % 2
            if qp != qpair_prev:
                qld = em.dma("sp", lambda e, qp=qp, hq=hq: e.dma_start(
                    sb["kq"][hq][:, :], aps["scr_q"][:, 128 * qp:128 * (qp + 1)],
                    transpose=True), deps=q_done + [kq_free[hq]])
                kq_tok[qp] = qld
                qpair_prev = qp
            hp = h // 4
            hk = hp % 2
            if hp != kpair_prev:
                kld = em.dma("sp", lambda e, hp=hp, hk=hk: e.dma_start(
                    sb["kk"][hk][:, :], aps["scr_k"][:, 128 * hp:128 * (hp + 1)],
                    transpose=True), deps=k_done + [kk_free[hk]])
                kk_tok[hp] = kld
                kpair_prev = hp
            qbase = 64 * (h % 2)
            kbase = qbase
            for tc in range(4):
                pcs = (4 * h + tc) % 3
                n_st = 4 * tc + 4
                avq = []
                last_av = None
                for st_i in range(n_st):
                    pas = st_i % 2
                    smm = em.op("pe", lambda e, st_i=st_i, tc=tc, hq=hq, hk=hk, pas=pas, qbase=qbase, kbase=kbase: e.matmul(
                        sb["pa"][pas][:, :],
                        sb["kk"][hk][kbase:kbase + 64, 128 * st_i:128 * (st_i + 1)],
                        sb["kq"][hq][qbase:qbase + 64, 512 * tc:512 * (tc + 1)],
                        start=True, stop=True),
                        deps=[kq_tok[qp], kk_tok[hp], exp_toks.get(pas)])
                    if st_i == 1 and len(pending_z) >= 2:
                        pending_z.pop(0)()
                    ex = em.op("act", lambda e, st_i=st_i, pas=pas: e.activation(
                        sb["at"][pas][:, :], sb["pa"][pas][:, :], AF.Exp, scale=0.125),
                        deps=[smm, prev_batch.get(("at", pas))])
                    exp_toks[pas] = ex
                    rhs_buf = sb["at"][pas]
                    if st_i >= 4 * tc:
                        mk = st_i - 4 * tc
                        ex = em.op("dve", lambda e, pas=pas, mk=mk: e.tensor_tensor(
                            sb["at2"][pas][:, :], sb["at"][pas][:, :],
                            sb["mask"][mk][:, :], ALU.mult),
                            deps=[ex, mtoks[mk], prev_batch.get(("at2", pas))])
                        rhs_buf = sb["at2"][pas]
                    avq.append((st_i, rhs_buf, ex, pas))
                    if len(avq) >= 2:
                        st_e, rb_e, ex_e, pas_e = avq.pop(0)
                        av = em.op("pe", lambda e, st_e=st_e, rb_e=rb_e, pcs=pcs, first=(st_e == 0), last=False, hp=hp: e.matmul(
                            sb["pc"][pcs][0:65, :],
                            sb["vaug"][:, 520 * st_e + 65 * hp:520 * st_e + 65 * (hp + 1)],
                            rb_e[:, :], start=first, stop=last),
                            deps=[ex_e, v_done, ctx_evac[pcs] if st_e == 0 else None])
                        prev_batch[("at", pas_e)] = av
                        prev_batch[("at2", pas_e)] = av
                        last_av = av
                st_e, rb_e, ex_e, pas_e = avq.pop(0)
                av = em.op("pe", lambda e, st_e=st_e, rb_e=rb_e, pcs=pcs, first=(st_e == 0), hp=hp: e.matmul(
                    sb["pc"][pcs][0:65, :],
                    sb["vaug"][:, 520 * st_e + 65 * hp:520 * st_e + 65 * (hp + 1)],
                    rb_e[:, :], start=first, stop=True),
                    deps=[ex_e, v_done, ctx_evac[pcs] if st_e == 0 else None])
                prev_batch[("at", pas_e)] = av
                prev_batch[("at2", pas_e)] = av
                last_av = av
                av_done_all.append(last_av)
                pending_z.append(make_zflush(pcs, h, tc, last_av))
            kq_free[hq] = smm
            kk_free[hk] = smm
        while pending_z:
            pending_z.pop(0)()
        cm = ctx_evac[0]
        av_done = av_done_all[-1]
        prev_batch["av_done"] = av_done
        prev_batch["scr_load_last"] = kq_tok[15]

        ctx_done = ctx_evac[0] if ctx_evac[0] else None
        ctx_all = [t_ for t_ in ctx_evac if t_ is not None]

        xc_free = prev_batch.get("xc_free", [None, None])
        x1_stores = []
        last_wo_mm = None
        for dc in range(4):
            jb = dc % 2
            wo_t = load_w("wo", 512 * dc, 512, jb, [wbuf_reader[jb]])
            for t in range(NT):
                psl = t % 2
                xl = em.dma("sp", lambda e, i=i, t=t, dc=dc, psl=psl: e.dma_start(
                    sb["xc"][psl][:, :],
                    aps["x"][i][128 * t:128 * (t + 1), 512 * dc:512 * (dc + 1)]),
                    deps=[xc_free[psl]])
                mm = None
                for kc in range(DC):
                    mm = em.op("pe", lambda e, kc=kc, t=t, jb=jb, psl=psl, st_=(kc == 0), sp_=(kc == DC - 1): e.matmul(
                        sb["pp"][psl][:, :],
                        sb["big0"][:, 2048 * kc + 128 * t:2048 * kc + 128 * (t + 1)],
                        sb["wbuf"][jb][:, 512 * kc:512 * (kc + 1)],
                        start=st_, stop=sp_),
                        deps=(ctx_all + [wo_t, pp_free[psl]]) if kc == 0 else [])
                ev = em.op("dve", lambda e, psl=psl: e.scalar_tensor_tensor(
                    sb["x1c"][psl][:, :], sb["pp"][psl][:, :], 1.0,
                    sb["xc"][psl][:, :], ALU.bypass, ALU.add),
                    deps=[mm, xl, prev_batch.get(("x1c", psl))])
                pp_free[psl] = ev
                xc_free[psl] = ev
                dmo = em.dma("sp", lambda e, i=i, t=t, dc=dc, psl=psl: e.dma_start(
                    aps["x1"][i][128 * t:128 * (t + 1), 512 * dc:512 * (dc + 1)],
                    sb["x1c"][psl][:, :]), deps=[ev])
                prev_batch[("x1c", psl)] = dmo
                x1_stores.append(dmo)
                last_wo_mm = mm
            wbuf_reader[jb] = last_wo_mm
        prev_wo_done = last_wo_mm
        prev_batch["wbuf_reader"] = wbuf_reader
        prev_batch["pp_free"] = pp_free
        prev_batch["xc_free"] = xc_free
        all_stores = x1_stores
    return all_stores



def build_l0_attn_program():
    nc = bass.Bass("TRN2", target_bir_lowering=False)
    x = nc.dram_tensor("x", [BPC * T, D], F32, kind="ExternalInput").ap()
    wq = nc.dram_tensor("wq", [D, D], F32, kind="ExternalInput").ap()
    wk = nc.dram_tensor("wk", [D, 512], F32, kind="ExternalInput").ap()
    wv = nc.dram_tensor("wv", [D, 512], F32, kind="ExternalInput").ap()
    wo = nc.dram_tensor("wo", [D, D], F32, kind="ExternalInput").ap()
    rc2 = nc.dram_tensor("rope_cos2", [T, 512], F32, kind="ExternalInput").ap()
    rs = nc.dram_tensor("rope_sin", [T, 256], F32, kind="ExternalInput").ap()
    rn = nc.dram_tensor("rope_nsin", [T, 256], F32, kind="ExternalInput").ap()
    x1 = nc.dram_tensor("x1", [BPC * T, D], F32, kind="ExternalOutput").ap()
    xln = nc.dram_tensor("xln", [BPC * T, D], BF16).ap()
    scr_q = nc.dram_tensor("scr_q", [T, D], BF16).ap()
    scr_k = nc.dram_tensor("scr_k", [T, 1024], BF16).ap()
    em = Em(nc)
    sb = alloc_l0_attn_sbuf(em)
    aps = {
        "x": [x[i * T:(i + 1) * T, :] for i in range(BPC)],
        "xln": [xln[i * T:(i + 1) * T, :] for i in range(BPC)],
        "x1": [x1[i * T:(i + 1) * T, :] for i in range(BPC)],
        "scr_q": scr_q, "scr_k": scr_k,
        "wq": wq, "wk": wk, "wv": wv, "wo": wo,
        "rope_cos2": rc2, "rope_sin": rs, "rope_nsin": rn,
    }
    emit_l0_attn(em, aps, sb)
    em.emit()
    return nc



def alloc_l0_ffn_sbuf(em):
    nc = em.nc
    st = em.stack
    sb = {}
    sb["bigA"] = st.enter_context(nc.sbuf_tensor("bigA", [128, 32768], BF16))
    sb["bigB"] = st.enter_context(nc.sbuf_tensor("bigB", [128, 32768], BF16))
    sb["egbuf"] = st.enter_context(nc.sbuf_tensor("egbuf", [128, 8192], BF16))
    sb["sitg"] = st.enter_context(nc.sbuf_tensor("sitg", [128, 8192], BF16))
    sb["w1buf"] = [st.enter_context(nc.sbuf_tensor("w1b0", [128, 4096], BF16))] * 2
    sb["w2buf"] = [st.enter_context(nc.sbuf_tensor("w2b0", [128, 2048], BF16))] * 2
    sb["lnc"] = [st.enter_context(nc.sbuf_tensor("lnc0", [128, 512], F32))]
    sb["lncol"] = st.enter_context(nc.sbuf_tensor("lncolB", [128, 32], F32))
    sb["etile"] = [st.enter_context(nc.sbuf_tensor(f"et{j}", [128, 512], BF16)) for j in range(2)]
    sb["h_sb"] = st.enter_context(nc.sbuf_tensor("h_sb", [128, 512], BF16))
    sb["hT"] = st.enter_context(nc.sbuf_tensor("hT", [128, 512], BF16))
    sb["lt2"] = [st.enter_context(nc.sbuf_tensor("lt2z", [128, 512], BF16))]
    sb["lt3"] = [st.enter_context(nc.sbuf_tensor("lt3z", [128, 512], BF16))]
    sb["intm"] = st.enter_context(nc.sbuf_tensor("intm", [1, 512], F32))
    sb["mrowb"] = st.enter_context(nc.sbuf_tensor("mrowb", [1, 512], BF16))
    sb["rbc"] = st.enter_context(nc.sbuf_tensor("rbc", [128, 512], BF16))
    sb["grow"] = st.enter_context(nc.sbuf_tensor("grow", [1, 512], BF16))
    sb["gbc"] = st.enter_context(nc.sbuf_tensor("gbc", [128, 2048], BF16))
    sb["bbc"] = st.enter_context(nc.sbuf_tensor("bbc", [128, 2048], BF16))
    sb["ones1"] = st.enter_context(nc.sbuf_tensor("ones1B", [1, 128], BF16))
    sb["onecol"] = st.enter_context(nc.sbuf_tensor("onecolB", [128, 1], BF16))
    sb["scl"] = st.enter_context(nc.sbuf_tensor("scl", [1, 4], F32))
    sb["scol"] = st.enter_context(nc.sbuf_tensor("scol", [128, 4], F32))
    sb["fcol"] = st.enter_context(nc.sbuf_tensor("fcol", [128, 40], F32))
    sb["rscol"] = st.enter_context(nc.sbuf_tensor("rscol", [128, 128], F32))
    sb["csrow"] = st.enter_context(nc.sbuf_tensor("csrow", [1, 2048], BF16))
    sb["rdcol"] = st.enter_context(nc.sbuf_tensor("rdcol", [128, 16], F32))
    sb["ident"] = st.enter_context(nc.sbuf_tensor("identB", [128, 128], BF16))
    sb["b1row"] = st.enter_context(nc.sbuf_tensor("b1row", [1, 512], BF16))
    sb["b2row"] = st.enter_context(nc.sbuf_tensor("b2row", [1, 512], BF16))
    sb["x1c"] = [st.enter_context(nc.sbuf_tensor("x1cB0", [128, 512], F32))] * 2
    sb["x2c"] = [st.enter_context(nc.sbuf_tensor(f"x2cB{j}", [128, 512], F32)) for j in range(2)]
    sb["psA"] = [st.enter_context(nc.psum_tensor(f"psA{j}", [128, 512], F32)) for j in range(2)]
    sb["psB"] = [st.enter_context(nc.psum_tensor(f"psB{j}", [128, 512], F32)) for j in range(2)]
    sb["pcol"] = st.enter_context(nc.psum_tensor("pcol", [1, 512], F32))
    sb["pcc"] = st.enter_context(nc.psum_tensor("pcc", [128, 1], F32))
    sb["pt"] = st.enter_context(nc.psum_tensor("pt", [128, 128], BF16))
    return sb


def emit_l0_ffn(em, aps, sb):
    E_EXP = 16

    ident_t = make_identity_bf16(em, sb["ident"])
    ones_t = em.op("pool", lambda g: g.memset(sb["ones1"][:, :], 1.0))
    onec_t = em.op("pool", lambda g: g.memset(sb["onecol"][:, :], 1.0))

    ps_reader = {}

    def psdep(name):
        return ps_reader.get(name)

    def psread(name, tok):
        ps_reader[name] = tok

    def _row_ap(ap, n):
        return bass.AP(ap.tensor, ap.offset, [[n, 1], [1, n]])
    s_t = em.dma("sp", lambda e: e.dma_start(sb["scl"][:, 0:1], _row_ap(aps["scale"], 1)))
    bc_toks = []
    ldprev = None
    for (wap, dst) in (("gamma", "gbc"), ("beta", "bbc")):
        for c in range(4):
            lr = em.dma("pool", lambda g, wap=wap, c=c: g.dma_start(
                sb["grow"][:, :],
                bass.AP(aps[wap].tensor, aps[wap].offset + 512 * c, [[512, 1], [1, 512]])),
                deps=[ldprev])
            mm = em.op("pe", lambda e: e.matmul(
                sb["psA"][0][:, :], sb["ones1"][:, :], sb["grow"][:, :],
                start=True, stop=True), deps=[lr, ones_t, psdep("psA0")])
            ldprev = mm
            cp = em.op("dve", lambda e, dst=dst, c=c: e.tensor_copy(
                sb[dst][:, 512 * c:512 * (c + 1)], sb["psA"][0][:, :]), deps=[mm])
            psread("psA0", cp)
            bc_toks.append(cp)
    sb16 = em.op("dve", lambda e: e.tensor_copy(sb["scl"][:, 1:2], sb["scl"][:, 0:1]),
                 deps=[s_t])
    sbf = em.op("act", lambda e: e.copy(sb["mrowb"][:, 0:1], sb["scl"][:, 0:1]), deps=[s_t])
    smm = em.op("pe", lambda e: e.matmul(
        sb["pcc"][:, :], sb["ones1"][:, :], sb["mrowb"][:, 0:1],
        start=True, stop=True), deps=[sbf, ones_t, psdep("pcc")])
    sc0 = em.op("dve", lambda e: e.tensor_copy(sb["scol"][:, 0:1], sb["pcc"][:, :]), deps=[smm])
    psread("pcc", sc0)
    sc1a = em.op("dve", lambda e: e.tensor_scalar_mul(
        sb["scol"][:, 3:4], sb["scol"][:, 0:1], -1.0), deps=[sc0])
    sc1 = em.op("dve", lambda e: e.tensor_max(
        sb["scol"][:, 1:2], sb["scol"][:, 0:1], sb["scol"][:, 3:4]), deps=[sc1a])
    sc2 = em.op("dve", lambda e: e.tensor_scalar_mul(
        sb["scol"][:, 2:3], sb["scol"][:, 1:2], -1.0), deps=[sc1])

    prev = {}
    for i in range(BPC):
        ln_done = []
        for t in range(NT):
            sums = []
            for c in range(4):
                j = 0
                ld = em.dma("sp", lambda e, i=i, t=t, c=c, j=j: e.dma_start(
                    sb["lnc"][j][:, :],
                    aps["x1"][i][128 * t:128 * (t + 1), 512 * c:512 * (c + 1)]),
                    deps=[prev.get(("lnld", j))])
                s0 = em.op("dve", lambda e, j=j, c=c: e.reduce_sum(
                    sb["lncol"][:, c:c + 1], sb["lnc"][j][:, :], axis=AX.X),
                    deps=[ld, prev.get("ln_tail")])
                prev[("lnld", j)] = s0
                sums.append(s0)
            a1 = em.op("dve", lambda e: e.tensor_add(
                sb["lncol"][:, 4:5], sb["lncol"][:, 0:1], sb["lncol"][:, 1:2]), deps=sums[:2])
            a2 = em.op("dve", lambda e: e.tensor_add(
                sb["lncol"][:, 5:6], sb["lncol"][:, 2:3], sb["lncol"][:, 3:4]), deps=sums[2:])
            a3 = em.op("dve", lambda e: e.tensor_add(
                sb["lncol"][:, 6:7], sb["lncol"][:, 4:5], sb["lncol"][:, 5:6]), deps=[a1, a2])
            negm = em.op("dve", lambda e: e.tensor_scalar_mul(
                sb["lncol"][:, 7:8], sb["lncol"][:, 6:7], -1.0 / D), deps=[a3])
            sqs = []
            for c in range(4):
                j = 0
                ld = em.dma("sp", lambda e, i=i, t=t, c=c, j=j: e.dma_start(
                    sb["lnc"][j][:, :],
                    aps["x1"][i][128 * t:128 * (t + 1), 512 * c:512 * (c + 1)]),
                    deps=[prev.get(("lnld", j))])
                sq = em.op("act", lambda e, j=j, c=c: e.activation(
                    sb["lt2"][j][:, :], sb["lnc"][j][:, :], AF.Square,
                    bias=sb["lncol"][:, 7:8], accum_out=sb["lncol"][:, 8 + c:9 + c]),
                    deps=[ld, negm, prev.get(("lnsq", j)), prev.get(("ln_st", j)),
                          prev.get(("ln_nq", j))])
                prev[("lnld", j)] = sq
                prev[("lnsq", j)] = sq
                sqs.append(sq)
            v1 = em.op("dve", lambda e: e.tensor_add(
                sb["lncol"][:, 12:13], sb["lncol"][:, 8:9], sb["lncol"][:, 9:10]), deps=sqs[:2])
            v2 = em.op("dve", lambda e: e.tensor_add(
                sb["lncol"][:, 13:14], sb["lncol"][:, 10:11], sb["lncol"][:, 11:12]), deps=sqs[2:])
            v3 = em.op("dve", lambda e: e.tensor_scalar(
                sb["lncol"][:, 14:15],
                sb["lncol"][:, 12:13], 1.0, None, ALU.add), deps=[v1])
            v4 = em.op("dve", lambda e: e.tensor_add(
                sb["lncol"][:, 15:16], sb["lncol"][:, 14:15], sb["lncol"][:, 13:14]), deps=[v3, v2])
            c0 = em.op("dve", lambda e: e.tensor_scalar(
                sb["lncol"][:, 16:17], sb["lncol"][:, 15:16], 1.0 / D, EPS_LN,
                ALU.mult, ALU.add), deps=[v4])
            c1 = em.op("dve", lambda e: e.reciprocal(
                sb["lncol"][:, 17:18], sb["lncol"][:, 16:17]), deps=[c0])
            c2 = em.op("dve", lambda e: e.scalar_tensor_tensor(
                sb["lncol"][:, 18:19], sb["lncol"][:, 15:16], 1.0 / D,
                sb["lncol"][:, 17:18], ALU.mult, ALU.mult), deps=[c1, v4])
            c3 = em.op("dve", lambda e: e.tensor_scalar(
                sb["lncol"][:, 19:20], sb["lncol"][:, 18:19], EPS_LN, None,
                ALU.add), deps=[c2])
            c4 = em.op("dve", lambda e: e.tensor_tensor(
                sb["lncol"][:, 20:21], sb["lncol"][:, 16:17], sb["lncol"][:, 19:20],
                ALU.mult), deps=[c3, c0])
            c5 = em.op("act", lambda e: e.activation(
                sb["lncol"][:, 21:22], sb["lncol"][:, 20:21], AF.Ln), deps=[c4])
            rho = em.op("act", lambda e: e.activation(
                sb["lncol"][:, 22:23], sb["lncol"][:, 21:22], AF.Exp, scale=-0.5), deps=[c5])
            nrm_parts = []
            for c in range(4):
                j = 0
                ld = em.dma("sp", lambda e, i=i, t=t, c=c, j=j: e.dma_start(
                    sb["lnc"][j][:, :],
                    aps["x1"][i][128 * t:128 * (t + 1), 512 * c:512 * (c + 1)]),
                    deps=[prev.get(("lnld", j))])
                y1 = em.op("dve", lambda e, j=j: e.tensor_scalar(
                    sb["lt2"][j][:, :], sb["lnc"][j][:, :],
                    sb["lncol"][:, 7:8], sb["lncol"][:, 22:23], ALU.add, ALU.mult),
                    deps=[ld, rho, prev.get(("ln_nq", j)), prev.get(("ln_st", j))])
                prev[("lnld", j)] = y1
                y2 = em.op("dve", lambda e, j=j, c=c: e.tensor_tensor(
                    sb["lt3"][j][:, :], sb["lt2"][j][:, :], sb["gbc"][:, 512 * c:512 * (c + 1)],
                    ALU.mult), deps=[y1, bc_toks[-1], prev.get(("ln_nq", j))])
                y3 = em.op("dve", lambda e, j=j, c=c: e.tensor_tensor(
                    sb["lt2"][j][:, :], sb["lt3"][j][:, :], sb["bbc"][:, 512 * c:512 * (c + 1)],
                    ALU.add), deps=[y2])
                nq = em.op("act", lambda e, j=j, c=c: e.activation(
                    sb["lt3"][j][:, :], sb["lt2"][j][:, :], AF.Square,
                    accum_out=sb["lncol"][:, 24 + c:25 + c]), deps=[y3])
                prev[("ln_nq", j)] = nq
                nrm_parts.append(nq)
                st_ = em.dma("sp", lambda e, i=i, t=t, c=c, j=j: e.dma_start(
                    aps["xln"][i][128 * t:128 * (t + 1), 512 * c:512 * (c + 1)],
                    sb["lt2"][j][:, :]), deps=[y3])
                prev[("ln_st", j)] = st_
                ln_done.append(st_)
            n1 = em.op("dve", lambda e: e.tensor_add(
                sb["lncol"][:, 28:29], sb["lncol"][:, 24:25], sb["lncol"][:, 25:26]),
                deps=nrm_parts[:2])
            n2 = em.op("dve", lambda e: e.tensor_add(
                sb["lncol"][:, 29:30], sb["lncol"][:, 26:27], sb["lncol"][:, 27:28]),
                deps=nrm_parts[2:])
            n3 = em.op("dve", lambda e: e.tensor_add(
                sb["lncol"][:, 30:31], sb["lncol"][:, 28:29], sb["lncol"][:, 29:30]),
                deps=[n1, n2])
            f1 = em.op("act", lambda e: e.activation(
                sb["lncol"][:, 31:32], sb["lncol"][:, 30:31], AF.Sqrt), deps=[n3])
            f2 = em.op("dve", lambda e: e.tensor_scalar(
                sb["lncol"][:, 23:24], sb["lncol"][:, 31:32], 1e-12, None,
                ALU.max), deps=[f1])
            f3 = em.op("dve", lambda e: e.reciprocal(
                sb["lncol"][:, 23:24], sb["lncol"][:, 23:24]), deps=[f2])
            f4 = em.op("dve", lambda e, i=i, t=t: e.tensor_tensor(
                sb["fcol"][:, i * NT + t:i * NT + t + 1], sb["lncol"][:, 23:24],
                sb["scol"][:, 0:1], ALU.mult), deps=[f3, sc0])
            prev[("fac", t)] = f4
            prev["ln_tail"] = f4
        mu_war = prev.get("bigB_free")
        mload = []
        for dc in range(DC):
            mload.append(em.dma("pool", lambda g, dc=dc: g.dma_start(
                sb["bigB"][:, 2048 * dc:2048 * (dc + 1)],
                aps["mu"][128 * dc:128 * (dc + 1), :]), deps=[mu_war]))
        mun_done = []
        sit_war = [prev.get(("sitg_free", k)) for k in range(DC)]
        for ec in range(4):
            mmt = None
            for dc in range(DC):
                sl = dc % 4
                ms = em.op("dve", lambda e, dc=dc, ec=ec, sl=sl: e.tensor_tensor(
                    sb["sitg"][:, 512 * sl:512 * (sl + 1)],
                    sb["bigB"][:, 2048 * dc + 512 * ec:2048 * dc + 512 * (ec + 1)],
                    sb["bigB"][:, 2048 * dc + 512 * ec:2048 * dc + 512 * (ec + 1)],
                    ALU.mult), deps=[mload[dc], prev.get(("stg_free", sl))] + sit_war)
                mmt = em.op("pe", lambda e, dc=dc, sl=sl: e.matmul(
                    sb["pcol"][:, :], sb["onecol"][:, :],
                    sb["sitg"][:, 512 * sl:512 * (sl + 1)],
                    start=(dc == 0), stop=(dc == DC - 1)),
                    deps=[ms, onec_t] + ([psdep("pcol")] if dc == 0 else []))
                prev[("stg_free", sl)] = mmt
            sit_war = []
            r1 = em.op("act", lambda e: e.activation(
                sb["intm"][:, :], sb["pcol"][:, :], AF.Ln),
                deps=[mmt, prev.get("intm_free")])
            psread("pcol", r1)
            r2 = em.op("act", lambda e: e.activation(
                sb["mrowb"][:, :], sb["intm"][:, :], AF.Exp, scale=-0.5),
                deps=[r1, prev.get("mrowb_free")])
            prev["intm_free"] = r2
            bmm = em.op("pe", lambda e: e.matmul(
                sb["psA"][0][:, :], sb["ones1"][:, :], sb["mrowb"][:, :],
                start=True, stop=True), deps=[r2, ones_t, psdep("psA0")])
            prev["mrowb_free"] = bmm
            rcp = em.op("dve", lambda e: e.tensor_copy(
                sb["rbc"][:, :], sb["psA"][0][:, :]), deps=[bmm, prev.get("rbc_free")])
            psread("psA0", rcp)
            for dc in range(DC):
                sl = dc % 4
                t1 = em.op("dve", lambda e, dc=dc, ec=ec, sl=sl: e.tensor_tensor(
                    sb["sitg"][:, 512 * sl:512 * (sl + 1)],
                    sb["bigB"][:, 2048 * dc + 512 * ec:2048 * dc + 512 * (ec + 1)],
                    sb["rbc"][:, :], ALU.mult),
                    deps=[rcp, prev.get(("stg_free", sl))])
                t2 = em.op("dve", lambda e, dc=dc, ec=ec, sl=sl: e.tensor_copy(
                    sb["bigB"][:, 2048 * dc + 512 * ec:2048 * dc + 512 * (ec + 1)],
                    sb["sitg"][:, 512 * sl:512 * (sl + 1)]), deps=[t1])
                prev[("stg_free", sl)] = t2
                mun_done.append(t2)
            prev["rbc_free"] = t1
        if i == 0:
            prev["mun_st"] = []
            for dc in range(DC):
                prev["mun_st"].append(em.dma("sp", lambda e, dc=dc: e.dma_start(
                    bass.AP(aps["scr_mun"].tensor, aps["scr_mun"].offset + dc * 128 * 2048,
                            [[2048, 128], [1, 2048]]),
                    sb["bigB"][:, 2048 * dc:2048 * (dc + 1)]), deps=mun_done))

        xlnT = []
        for j in range(DC):
            dep = ln_done + ([prev.get("bigA_free")] if j == 0 else [])
            xlnT.append(em.dma("sp" if j % 2 == 0 else "act",
                               lambda e, j=j, i=i: e.dma_start(
                sb["bigA"][:, 2048 * j:2048 * (j + 1)],
                aps["xln"][i][:, 128 * j:128 * (j + 1)], transpose=True), deps=dep))
        csum_done = []
        e_stores = []
        for ec in range(4):
            cs_mm = None
            cs_pend = None
            for nt in range(NT):
                psl = nt % 2
                mm = None
                for dc in range(DC):
                    mm = em.op("pe", lambda e, dc=dc, nt=nt, ec=ec, psl=psl,
                               st_=(dc == 0), sp_=(dc == DC - 1): e.matmul(
                        sb["psA"][psl][:, :],
                        sb["bigA"][:, 2048 * dc + 128 * nt:2048 * dc + 128 * (nt + 1)],
                        sb["bigB"][:, 2048 * dc + 512 * ec:2048 * dc + 512 * (ec + 1)],
                        start=st_, stop=sp_),
                        deps=(xlnT + mun_done + [psdep(f"psA{psl}")]) if dc == 0 else [])
                if cs_pend is not None:
                    cs_mm = cs_pend()
                    cs_pend = None
                et = em.op("act", lambda e, nt=nt, ec=ec, psl=psl, i=i: e.activation(
                    sb["etile"][psl][:, :], sb["psA"][psl][:, :], AF.Exp,
                    bias=sb["scol"][:, 2:3], scale=sb["fcol"][:, i * NT + nt:i * NT + nt + 1],
                    accum_out=sb["rscol"][:, 4 * nt + ec:4 * nt + ec + 1]),
                    deps=[mm, sc2, prev.get(("fac", nt)), prev.get(("etile_free", psl)),
                          prev.get(("rsa", nt)), prev.get(("rr_war", nt))])
                psread(f"psA{psl}", et)
                est = em.dma("act", lambda e, nt=nt, ec=ec, psl=psl: e.dma_start(
                    aps["scr_e"][128 * nt:128 * (nt + 1), 512 * ec:512 * (ec + 1)],
                    sb["etile"][psl][:, :]),
                    deps=[et] + (prev.get("scr_e_readers", []) if (nt == 0 and ec == 0) else []))
                e_stores.append(est)

                def _mk_cs(psl=psl, nt=nt, et=et, est=est):
                    tok = em.op("pe", lambda e, psl=psl, nt=nt,
                                st_=(nt == 0), sp_=(nt == NT - 1): e.matmul(
                        sb["pcol"][:, :], sb["onecol"][:, :], sb["etile"][psl][:, :],
                        start=st_, stop=sp_),
                        deps=[et, onec_t] + ([psdep("pcol")] if nt == 0 else []))
                    prev[("etile_free", psl)] = [tok, est]
                    return tok
                cs_pend = _mk_cs
            cs_mm = cs_pend()
            cv = em.op("dve", lambda e, ec=ec: e.tensor_copy(
                sb["csrow"][:, 512 * ec:512 * (ec + 1)], sb["pcol"][:, :]),
                deps=[cs_mm, prev.get("csrow_free")])
            psread("pcol", cv)
            csum_done.append(cv)
        prev["csrow_free"] = csum_done[-1]
        rr_done = {}
        for nt in range(NT):
            a1 = em.op("dve", lambda e, nt=nt: e.tensor_add(
                sb["rscol"][:, 64 + nt:65 + nt],
                sb["rscol"][:, 4 * nt:4 * nt + 1], sb["rscol"][:, 4 * nt + 1:4 * nt + 2]),
                deps=[psdep("psA0"), psdep("psA1"), prev.get(("rr_war", nt))])
            a2 = em.op("dve", lambda e, nt=nt: e.tensor_add(
                sb["rscol"][:, 80 + nt:81 + nt],
                sb["rscol"][:, 4 * nt + 2:4 * nt + 3], sb["rscol"][:, 4 * nt + 3:4 * nt + 4]),
                deps=[prev.get(("rr_war", nt))])
            a3 = em.op("dve", lambda e, nt=nt: e.tensor_add(
                sb["rscol"][:, 96 + nt:97 + nt],
                sb["rscol"][:, 64 + nt:65 + nt], sb["rscol"][:, 80 + nt:81 + nt]),
                deps=[a1, a2, prev.get(("rr_war2", nt))])
            rr = em.op("dve", lambda e, nt=nt: e.reciprocal(
                sb["rscol"][:, 112 + nt:113 + nt], sb["rscol"][:, 96 + nt:97 + nt]), deps=[a3])
            prev[("rr_war2", nt)] = rr
            prev[("rsa", nt)] = rr
            prev[("rr_war", nt)] = rr
            rr_done[nt] = rr
        csr_st = em.dma("sp", lambda e: e.dma_start(aps["scr_row"], sb["csrow"][:, :]),
                        deps=csum_done + [prev.get("csrow_free2")])
        csr_ld = em.dma("pool", lambda e: e.dma_start(
            sb["rdcol"][:, :],
            bass.AP(aps["scr_row"].tensor, aps["scr_row"].offset, [[1, 128], [128, 16]]),
            allow_slow_non_contiguous=True), deps=[csr_st, prev.get("rdcol_free")])
        rdc = em.op("dve", lambda e: e.reciprocal(sb["rdcol"][:, :], sb["rdcol"][:, :]),
                    deps=[csr_ld])
        prev["csrow_free2"] = csr_ld

        xln_ld = []
        for j in range(NT):
            dep = ([psdep("psA0"), psdep("psA1")] if j == 0 else [])
            xln_ld.append(em.dma("sp" if j % 2 == 0 else "act",
                                 lambda e, j=j, i=i: e.dma_start(
                sb["bigA"][:, 2048 * j:2048 * (j + 1)],
                aps["xln"][i][128 * j:128 * (j + 1), :]), deps=dep + ln_done))
        so_evacs = []
        gel = h_mm = mm = None
        for eg in range(4):
            egl = []
            for ncx in range(NT):
                dep = [prev.get("egbuf_free")] if ncx == 0 else []
                egl.append(em.dma("sp" if ncx % 2 == 0 else "act",
                                  lambda e, ncx=ncx, eg=eg: e.dma_start(
                    sb["egbuf"][:, 512 * ncx:512 * (ncx + 1)],
                    aps["scr_e"][128 * ncx:128 * (ncx + 1), 512 * eg:512 * (eg + 1)]),
                    deps=dep + e_stores))
            sit_done = []
            for dt in range(DC):
                psl = dt % 2
                mm = None
                for ncx in range(NT):
                    mm = em.op("pe", lambda e, ncx=ncx, dt=dt, psl=psl,
                               st_=(ncx == 0), sp_=(ncx == NT - 1): e.matmul(
                        sb["psB"][psl][:, :],
                        sb["bigA"][:, 2048 * ncx + 128 * dt:2048 * ncx + 128 * (dt + 1)],
                        sb["egbuf"][:, 512 * ncx:512 * (ncx + 1)],
                        start=st_, stop=sp_),
                        deps=(xln_ld + egl + [psdep(f"psB{psl}")]) if ncx == 0 else [])
                ev = em.op("dve", lambda e, dt=dt, psl=psl: e.tensor_copy(
                    sb["sitg"][:, 512 * dt:512 * (dt + 1)], sb["psB"][psl][:, :]),
                    deps=[mm, prev.get(("sitg_free", dt))])
                psread(f"psB{psl}", ev)
                sit_done.append(ev)
            prev["egbuf_free"] = mm
            for el in range(4):
                e_ = 4 * eg + el
                h_mm = None
                for half in range(2):
                    jb = half
                    w1l = em.dma("pool", lambda g, e_=e_, half=half, jb=jb: g.dma_start(
                        sb["w1buf"][jb][:, :],
                        bass.AP(aps["w1"].tensor,
                                aps["w1"].offset + e_ * D * 512 + half * 8 * 128 * 512,
                                [[512, 128], [128 * 512, 8], [1, 512]])),
                        deps=[prev.get("w1_free")])
                    for k in range(8):
                        dc = half * 8 + k
                        h_mm = em.op("pe", lambda e, dc=dc, el=el, jb=jb, k=k,
                                     st_=(dc == 0): e.matmul(
                            sb["psA"][0][:, :],
                            sb["sitg"][:, 512 * dc + 128 * el:512 * dc + 128 * (el + 1)],
                            sb["w1buf"][jb][:, 512 * k:512 * (k + 1)],
                            start=st_, stop=False),
                            deps=[w1l] + ((sit_done + [psdep("psA0")]) if dc == 0 else []))
                    prev["w1_free"] = h_mm
                for dtx in range(DC):
                    prev[("sitg_free", dtx)] = h_mm
                b1l = em.dma("pool", lambda g, e_=e_: g.dma_start(
                    sb["b1row"][:, :],
                    bass.AP(aps["b1"].tensor, aps["b1"].offset + 512 * e_, [[512, 1], [1, 512]])),
                    deps=[prev.get("b1row_free")])
                h_mm = em.op("pe", lambda e, e_=e_: e.matmul(
                    sb["psA"][0][:, :],
                    sb["csrow"][:, 128 * e_:128 * (e_ + 1)],
                    sb["b1row"][:, :], start=False, stop=True),
                    deps=[b1l, h_mm] + csum_done)
                prev["b1row_free"] = h_mm
                gel = em.op("act", lambda e, e_=e_: e.activation(
                    sb["h_sb"][:, :], sb["psA"][0][:, :], AF.Gelu,
                    scale=sb["rdcol"][:, e_:e_ + 1]),
                    deps=[h_mm, rdc, prev.get("h_sb_free")])
                psread("psA0", gel)
                tts = []
                for hc in range(4):
                    tt = em.op("pe", lambda e, hc=hc: e.transpose(
                        sb["pt"][:, :], sb["h_sb"][:, 128 * hc:128 * (hc + 1)],
                        sb["ident"][:, :]),
                        deps=[gel, ident_t, psdep("pt")])
                    tc_ = em.op("dve", lambda e, hc=hc: e.tensor_copy(
                        sb["hT"][:, 128 * hc:128 * (hc + 1)], sb["pt"][:, :]),
                        deps=[tt, prev.get(("hT_free", hc))])
                    psread("pt", tc_)
                    prev[("hT_free", hc)] = None
                    tts.append(tc_)
                prev["h_sb_free"] = tts[-1]
                for dc2 in range(4):
                    b2l = em.dma("pool", lambda g, e_=e_, dc2=dc2: g.dma_start(
                        sb["b2row"][:, :],
                        bass.AP(aps["b2"].tensor, aps["b2"].offset + 2048 * e_ + 512 * dc2,
                                [[512, 1], [1, 512]])),
                        deps=[prev.get("b2row_free")])
                    jb = dc2 % 2
                    w2l = em.dma("pool", lambda g, e_=e_, dc2=dc2, jb=jb: g.dma_start(
                        sb["w2buf"][jb][:, :],
                        bass.AP(aps["w2"].tensor,
                                aps["w2"].offset + e_ * 512 * D + 512 * dc2,
                                [[D, 128], [128 * D, 4], [1, 512]])),
                        deps=[prev.get("w2_free")])
                    so_mm = None
                    for hc in range(4):
                        so_mm = em.op("pe", lambda e, hc=hc, jb=jb, st_=(hc == 0): e.matmul(
                            sb["psB"][0][:, :],
                            sb["hT"][:, 128 * hc:128 * (hc + 1)],
                            sb["w2buf"][jb][:, 512 * hc:512 * (hc + 1)],
                            start=st_, stop=False),
                            deps=[w2l] + ((tts + [psdep("psB0")]) if hc == 0 else []))
                    so_mm = em.op("pe", lambda e: e.matmul(
                        sb["psB"][0][:, :], sb["ones1"][:, :],
                        sb["b2row"][:, :],
                        start=False, stop=True), deps=[b2l, so_mm])
                    prev["b2row_free"] = so_mm
                    prev["w2_free"] = so_mm
                    for hc in range(4):
                        prev[("hT_free", hc)] = so_mm
                    evo = em.op("dve", lambda e, e_=e_, dc2=dc2: e.tensor_copy(
                        sb["bigB"][:, 2048 * e_ + 512 * dc2:2048 * e_ + 512 * (dc2 + 1)],
                        sb["psB"][0][:, :]),
                        deps=[so_mm] + (mun_done if e_ == 0 and dc2 == 0 else []))
                    psread("psB0", evo)
                    so_evacs.append(evo)

        et_ld = []
        for j in range(NT):
            dep = ([psdep("psB0"), psdep("psB1"), mm] if j == 0 else [])
            et_ld.append(em.dma("sp" if j % 2 == 0 else "act",
                                lambda e, j=j: e.dma_start(
                sb["bigA"][:, 2048 * j:2048 * (j + 1)],
                aps["scr_e"][:, 128 * j:128 * (j + 1)], transpose=True),
                deps=dep + (e_stores if j == 0 else [])))
        x2_stores = []
        xc_free = prev.get("xc_freeB", [None, None])
        for nt in range(NT):
            for dc2 in range(4):
                psl = (nt * 4 + dc2) % 2
                xl = em.dma("sp", lambda e, i=i, nt=nt, dc2=dc2, psl=psl: e.dma_start(
                    sb["x1c"][psl][:, :],
                    aps["x1"][i][128 * nt:128 * (nt + 1), 512 * dc2:512 * (dc2 + 1)]),
                    deps=[xc_free[psl]])
                mm = None
                for ecc in range(NT):
                    mm = em.op("pe", lambda e, ecc=ecc, nt=nt, dc2=dc2, psl=psl,
                               st_=(ecc == 0), sp_=(ecc == NT - 1): e.matmul(
                        sb["psA"][psl][:, :],
                        sb["bigA"][:, 2048 * ecc + 128 * nt:2048 * ecc + 128 * (nt + 1)],
                        sb["bigB"][:, 2048 * ecc + 512 * dc2:2048 * ecc + 512 * (dc2 + 1)],
                        start=st_, stop=sp_),
                        deps=(et_ld + so_evacs + [psdep(f"psA{psl}")]) if ecc == 0 else [])
                ev = em.op("dve", lambda e, nt=nt, psl=psl: e.scalar_tensor_tensor(
                    sb["x2c"][psl][:, :], sb["psA"][psl][:, :],
                    sb["rscol"][:, 112 + nt:113 + nt], sb["x1c"][psl][:, :],
                    ALU.mult, ALU.add),
                    deps=[mm, xl, rr_done[nt], prev.get(("x2c_free", psl))])
                psread(f"psA{psl}", ev)
                xc_free[0] = ev
                xc_free[1] = ev
                dmo = em.dma("sp", lambda e, i=i, nt=nt, dc2=dc2, psl=psl: e.dma_start(
                    aps["x2"][i][128 * nt:128 * (nt + 1), 512 * dc2:512 * (dc2 + 1)],
                    sb["x2c"][psl][:, :]), deps=[ev])
                prev[("x2c_free", psl)] = dmo
                x2_stores.append(dmo)
        prev["xc_freeB"] = xc_free
        prev["bigA_free"] = mm
        prev["bigB_free"] = mm
        prev["scr_e_readers"] = et_ld + egl
        prev["csrowb_free"] = h_mm
        prev["rdcol_free"] = gel
        for dt in range(DC):
            prev[("sitg_free", dt)] = h_mm
    return x2_stores


def build_l0_ffn_program():
    nc = bass.Bass("TRN2", target_bir_lowering=False)
    x1 = nc.dram_tensor("x1", [BPC * T, D], F32, kind="ExternalInput").ap()
    mu = nc.dram_tensor("mu", [D, 2048], F32, kind="ExternalInput").ap()
    gamma = nc.dram_tensor("gamma", [D], F32, kind="ExternalInput").ap()
    beta = nc.dram_tensor("beta", [D], F32, kind="ExternalInput").ap()
    scale = nc.dram_tensor("scale", [1], F32, kind="ExternalInput").ap()
    w1 = nc.dram_tensor("w1", [16, D, 512], F32, kind="ExternalInput").ap()
    b1 = nc.dram_tensor("b1", [16, 512], F32, kind="ExternalInput").ap()
    w2 = nc.dram_tensor("w2", [16, 512, D], F32, kind="ExternalInput").ap()
    b2 = nc.dram_tensor("b2", [16, D], F32, kind="ExternalInput").ap()
    x2 = nc.dram_tensor("x2", [BPC * T, D], F32, kind="ExternalOutput").ap()
    xln = nc.dram_tensor("xlnB", [BPC * T, D], BF16).ap()
    scr_e = nc.dram_tensor("scr_e", [T, 2048], BF16).ap()
    scr_row = nc.dram_tensor("scr_row", [1, 2048], BF16).ap()
    scr_mun = nc.dram_tensor("scr_mun", [D * 2048], BF16).ap()
    em = Em(nc)
    sb = alloc_l0_ffn_sbuf(em)
    aps = {
        "x1": [x1[i * T:(i + 1) * T, :] for i in range(BPC)],
        "x2": [x2[i * T:(i + 1) * T, :] for i in range(BPC)],
        "xln": [xln[i * T:(i + 1) * T, :] for i in range(BPC)],
        "scr_e": scr_e, "scr_row": scr_row, "scr_mun": scr_mun,
        "mu": mu, "gamma": gamma, "beta": beta, "scale": scale,
        "w1": w1, "b1": b1, "w2": w2, "b2": b2,
    }
    emit_l0_ffn(em, aps, sb)
    em.emit()
    return nc



EPS_C = 1e-8
FD6 = 6


def alloc_l1_attn_sbuf(em):
    nc = em.nc
    st = em.stack
    sb = {}
    sb["bigA"] = st.enter_context(nc.sbuf_tensor("bigAC", [128, 32768], BF16))
    sb["wbuf"] = [st.enter_context(nc.sbuf_tensor("wbufC", [128, 8192], BF16))]
    sb["vbuf"] = st.enter_context(nc.sbuf_tensor("vbufC", [128, 8192], BF16))
    sb["mem"] = st.enter_context(nc.sbuf_tensor("memC", [128, 129 * FD6 * 16], BF16))
    sb["mkT"] = st.enter_context(nc.sbuf_tensor("mkTC", [128, 2048 * FD6], BF16))
    sb["wbuf"].append(sb["mkT"])
    sb["wmv4"] = st.enter_context(nc.sbuf_tensor("wmv4C", [128, 516 * NT], BF16))
    sb["mkt"] = [st.enter_context(nc.sbuf_tensor(f"mktC{j}", [128, 768], BF16)) for j in range(2)]
    sb["mb4"] = [st.enter_context(nc.sbuf_tensor(f"mb4C{j}", [128, 512], BF16)) for j in range(2)]
    sb["dstage"] = st.enter_context(nc.sbuf_tensor("dstageC", [128, 516], F32))
    sb["mv4"] = sb["dstage"]
    sb["rp"] = st.enter_context(nc.sbuf_tensor("rpC", [128, 1024], BF16))
    sb["junk"] = sb["rp"]
    sb["mst"] = st.enter_context(nc.sbuf_tensor("mstC", [128, 3072], BF16))
    sb["wst"] = sb["mst"]
    pass
    sb["rq"] = st.enter_context(nc.sbuf_tensor("rqC", [128, 640], F32))
    sb["rq2"] = st.enter_context(nc.sbuf_tensor("rq2C", [128, 640], F32))
    sb["colsc"] = st.enter_context(nc.sbuf_tensor("colscC", [128, 32], F32))
    sb["lnx"] = [st.enter_context(nc.sbuf_tensor("lnxC", [128, D], F32))]
    sb["lnxln"] = [st.enter_context(nc.sbuf_tensor("lnxlnC", [128, D], BF16))]
    sb["lncol"] = st.enter_context(nc.sbuf_tensor("lncolC", [128, 16], F32))
    sb["stats"] = st.enter_context(nc.sbuf_tensor("statsC", [128, 2 * NT * BPC], F32))
    sb["brow"] = st.enter_context(nc.sbuf_tensor("browC", [1, 512], BF16))
    sb["ones1"] = st.enter_context(nc.sbuf_tensor("ones1C", [1, 128], BF16))
    sb["x2c"] = [st.enter_context(nc.sbuf_tensor("x2cC0", [128, 512], F32))] * 2
    sb["mhc"] = [st.enter_context(nc.sbuf_tensor(f"mhcC{j}", [128, 512], BF16)) for j in range(2)]
    sb["gc"] = [st.enter_context(nc.sbuf_tensor("gcC0", [128, 512], BF16))] * 2
    sb["f1"] = st.enter_context(nc.sbuf_tensor("f1C", [128, 512], F32))
    sb["f2"] = st.enter_context(nc.sbuf_tensor("f2C", [128, 512], F32))
    sb["x3c"] = [st.enter_context(nc.sbuf_tensor("x3cC0", [128, 512], F32))] * 2
    sb["sg"] = [st.enter_context(nc.sbuf_tensor("sgC0", [128, 512], BF16))] * 2
    sb["psA"] = [st.enter_context(nc.psum_tensor(f"psAC{j}", [128, 512], F32)) for j in range(2)]
    sb["pden"] = st.enter_context(nc.psum_tensor("pdenC", [128, 16], F32))
    sb["pd"] = sb["psA"] + [st.enter_context(nc.psum_tensor(f"pdC{j}", [128, 512], F32)) for j in range(2, 6)]
    sb["pdn"] = st.enter_context(nc.psum_tensor("pdnC", [128, 32], F32))
    return sb


def emit_l1_attn(em, aps, sb, n_cores=8):
    P = {}
    ps_reader = {}

    def psdep(name):
        return ps_reader.get(name)

    def psread(name, tok):
        ps_reader[name] = tok

    ones_t = em.op("pool", lambda g: g.memset(sb["ones1"][:, :], 1.0))

    def psbuf_c2(psl):
        return sb["psA"][psl] if psl < 2 else sb["pd"][psl]

    def psnm_c2(psl):
        return f"psA{psl}" if psl < 2 else f"pd{psl}"

    def psdep_c2(psl):
        return ps_reader.get(psnm_c2(psl))

    wspec = [("wq", 2048, 0), ("wg", 2048, 2048 * 2048),
             ("wmb", 2048, 2 * 2048 * 2048), ("wk", 512, 3 * 2048 * 2048),
             ("wv", 512, 3 * 2048 * 2048 + 512 * 2048)]
    prep_done = []
    pst = None
    for (wn, rows, off) in wspec:
        for r0 in range(0, rows, 128):
            for half in range(2):
                ld = em.dma("pool", lambda g, wn=wn, r0=r0, half=half: g.dma_start(
                    sb["wst"][:, 0:1024], aps[wn][r0:r0 + 128, 1024 * half:1024 * (half + 1)]),
                    deps=[pst])
                st_ = em.dma("sp", lambda e, off=off, r0=r0, half=half: e.dma_start(
                    bass.AP(aps["scr_w"].tensor,
                            aps["scr_w"].offset + off + r0 * 2048 + 1024 * half,
                            [[2048, 128], [1, 1024]]),
                    sb["wst"][:, 0:1024]), deps=[ld])
                pst = st_
                prep_done.append(st_)
    P["mst_free"] = prep_done[-DMA_SLOTS:]
    mem_done = []
    for h in range(16):
        for fc in range(FD6):
            c0 = 129 * (FD6 * h + fc)
            m1 = em.dma("pool", lambda g, h=h, fc=fc, c0=c0: g.dma_start(
                sb["mem"][:, c0:c0 + 128],
                bass.AP(aps["memory"].tensor,
                        aps["memory"].offset + h * 768 * 128 + 128 * fc * 128,
                        [[128, 128], [1, 128]])))
            m2 = em.dma("pool", lambda g, h=h, fc=fc, c0=c0: g.dma_start(
                sb["mem"][:, c0 + 128:c0 + 129],
                bass.AP(aps["norm"].tensor, aps["norm"].offset + h * 768 + 128 * fc,
                        [[1, 128], [1, 1]])))
            mem_done += [m1, m2]

    kv_delta_evacs = []
    for i in range(BPC):
        ln_toks = emit_ln_pass(em, aps["x2"][i], aps["xln"][i], sb, i, store_stats=True)
        xlnT = []
        for j in range(DC):
            dep = ln_toks + ([P.get("bigA_free")] if j == 0 else [])
            xlnT.append(em.dma("sp", lambda e, j=j, i=i: e.dma_start(
                sb["bigA"][:, 2048 * j:2048 * (j + 1)],
                aps["xln"][i][:, 128 * j:128 * (j + 1)], transpose=True), deps=dep))

        def wchunk_load(woff, wrows, oc, jb, deps):
            toks = []
            for dcx in range(DC):
                toks.append(em.dma("sp", lambda e, woff=woff, oc=oc, dcx=dcx, jb=jb: e.dma_start(
                    sb["wbuf"][jb][:, 512 * dcx:512 * (dcx + 1)],
                    bass.AP(aps["scr_w"].tensor,
                            aps["scr_w"].offset + woff + 512 * oc * 2048 + 128 * dcx,
                            [[2048, 512], [1, 128]]), transpose=True),
                    deps=deps if dcx == 0 else []))
            return toks

        def bias_row(bname, oc, deps):
            return em.dma("pool", lambda g, bname=bname, oc=oc: g.dma_start(
                sb["brow"][:, :],
                bass.AP(aps[bname].tensor, aps[bname].offset + 512 * oc, [[512, 1], [1, 512]])),
                deps=deps)

        def proj_chunk(nt, jb, psl, deps0):
            mm = None
            for dcx in range(DC):
                mm = em.op("pe", lambda e, nt=nt, dcx=dcx, jb=jb, psl=psl,
                           st_=(dcx == 0): e.matmul(
                    psbuf_c2(psl)[:, 0:512],
                    sb["bigA"][:, 2048 * dcx + 128 * nt:2048 * dcx + 128 * (nt + 1)],
                    sb["wbuf"][jb][:, 512 * dcx:512 * (dcx + 1)],
                    start=st_, stop=False),
                    deps=deps0 if dcx == 0 else [])
            mm = em.op("pe", lambda e, psl=psl: e.matmul(
                psbuf_c2(psl)[:, 0:512], sb["ones1"][:, :], sb["brow"][:, :],
                start=False, stop=True), deps=[mm])
            return mm

        def dpfp_chain(nt, psl, mmtok, nheads4, scr_dst, rq_col0, junk_deps):
            t_rp1 = em.op("act", lambda e, psl=psl: e.activation(
                bass.AP(sb["rp"], 0, [[1024, 128], [256, 4], [1, 128]]),
                psbuf_c2(psl)[:, 0:512], AF.Relu), deps=[mmtok, P.get("rp_free")])
            t_rp2 = em.op("act", lambda e, psl=psl: e.activation(
                bass.AP(sb["rp"], 128, [[1024, 128], [256, 4], [1, 128]]),
                psbuf_c2(psl)[:, 0:512], AF.Relu, scale=-1.0), deps=[mmtok, P.get("rp_free")])
            prods = []
            for jr in range(1, 4):
                peng = "dve" if jr == 1 else "pool"
                pm = em.op(peng, lambda e, jr=jr: e.tensor_tensor(
                    bass.AP(sb["mst"], (jr - 1) * 256 + jr, [[3072, 128], [768, 4], [1, 256 - jr]]),
                    bass.AP(sb["rp"], jr, [[1024, 128], [256, 4], [1, 256 - jr]]),
                    bass.AP(sb["rp"], 0, [[1024, 128], [256, 4], [1, 256 - jr]]),
                    ALU.mult), deps=[t_rp1, t_rp2, P.get("mst_free")])
                pw = em.op(peng, lambda e, jr=jr: e.tensor_tensor(
                    bass.AP(sb["mst"], (jr - 1) * 256, [[3072, 128], [768, 4], [1, jr]]),
                    bass.AP(sb["rp"], 0, [[1024, 128], [256, 4], [1, jr]]),
                    bass.AP(sb["rp"], 256 - jr, [[1024, 128], [256, 4], [1, jr]]),
                    ALU.mult), deps=[t_rp1, t_rp2, P.get("mst_free")])
                prods += [pm, pw]
            rtoks = []
            for hh in range(4):
                if hh % 2 == 0:
                    sq = em.op("dve", lambda e, hh=hh: e.scalar_tensor_tensor(
                        sb["junk"][:, 0:768], sb["mst"][:, 768 * hh:768 * (hh + 1)], 1.0,
                        sb["mst"][:, 768 * hh:768 * (hh + 1)], ALU.bypass, ALU.mult,
                        accum_out=sb["colsc"][:, hh:hh + 1]),
                        deps=prods + [P.get("junk_free"), P.get(("colsc", hh))])
                else:
                    sq = em.op("act", lambda e, hh=hh: e.activation(
                        sb["junk"][:, 0:768], sb["mst"][:, 768 * hh:768 * (hh + 1)], AF.Square,
                        accum_out=sb["colsc"][:, hh:hh + 1]),
                        deps=prods + [P.get("junk_free"), P.get(("colsc", hh))])
                P["junk_free"] = sq
                s1 = em.op("act", lambda e, hh=hh: e.activation(
                    sb["colsc"][:, 4 + hh:5 + hh], sb["colsc"][:, hh:hh + 1], AF.Sqrt),
                    deps=[sq])
                s2 = em.op("dve", lambda e, hh=hh: e.tensor_scalar(
                    sb["colsc"][:, 8 + hh:9 + hh], sb["colsc"][:, 4 + hh:5 + hh],
                    1e-12, None, ALU.max), deps=[s1])
                cr = rq_col0 + hh
                s3 = em.op("dve", lambda e, hh=hh, cr=cr: e.tensor_copy(
                    sb["rq2"][:, cr:cr + 1], sb["colsc"][:, 8 + hh:9 + hh]), deps=[s2])
                s4 = em.op("dve", lambda e, hh=hh, cr=cr: e.reciprocal(
                    sb["rq"][:, cr:cr + 1], sb["colsc"][:, 8 + hh:9 + hh]), deps=[s2],
                    lowp=True)
                P[("colsc", hh)] = s4
                rtoks += [s3, s4]
            stm = em.dma("sp", lambda e: e.dma_start(scr_dst, sb["mst"][:, :]),
                         deps=prods)
            P["mst_free"] = [stm] + [t for t in rtoks]
            P["rp_free"] = rtoks + prods
            return stm, rtoks

        wl = wchunk_load(wspec[3][2], 512, 0, 0, [P.get(("wbuf_free", 0))] + xlnT)
        bl = bias_row("bk", 0, [P.get("brow_free")])
        mk_stores = []
        rk_done = []
        for nt in range(NT):
            psl = nt % 4
            mm = proj_chunk(nt, 0, psl, wl + [bl, psdep_c2(psl), ones_t])
            stm, rt = dpfp_chain(nt, psl, mm,
                                 4, aps["scr_mk"][i][128 * nt:128 * (nt + 1), :],
                                 i * 320 + 256 + 4 * nt, [])
            rk_done.extend(rt)
            psread(psnm_c2(psl), stm)
            mk_stores.append(stm)
        P["brow_free"] = mm
        P["wbuf_free"] = mm
        wl = wchunk_load(wspec[4][2], 512, 0, 1, [P.get(("wbuf_free", 1)), P.get("mkT_free")])
        bl = bias_row("bv", 0, [P.get("brow_free")])
        v_evs = []
        for nt in range(NT):
            psl = nt % 4
            mm = proj_chunk(nt, 1, psl, wl + [bl, psdep_c2(psl)])
            ev = em.op("dve", lambda e, nt=nt, psl=psl: e.tensor_copy(
                sb["vbuf"][:, 512 * nt:512 * (nt + 1)], psbuf_c2(psl)[:, :]),
                deps=[mm, P.get("vbuf_free")])
            psread(psnm_c2(psl), ev)
            v_evs.append(ev)
        P["brow_free"] = mm
        P[("wbuf_free", 1)] = mm
        mb_stores = []
        P[("mb_scr_rd", i)] = []
        for (wn, boff, bname, scr) in (("wmb", 2, "bmb", "scr_mb"), ("wg", 1, "bg", "scr_g")):
            woff = wspec[2][2] if wn == "wmb" else wspec[1][2]
            for oc in range(4):
                jbw = oc % 2
                wl = wchunk_load(woff, 2048, oc, jbw,
                                 [P.get(("wbuf_free", jbw))] + ([P.get("mkT_free")] if jbw == 1 else []))
                bl = bias_row(bname, oc, [P.get("brow_free")])
                for nt in range(NT):
                    psl = nt % 4
                    mm = proj_chunk(nt, jbw, psl, wl + [bl, psdep_c2(psl)])
                    sg = em.op("act", lambda e, psl=psl: e.activation(
                        sb["sg"][psl % 2][:, :], psbuf_c2(psl)[:, :], AF.Sigmoid),
                        deps=[mm, P.get("sg_free")])
                    psread(psnm_c2(psl), sg)
                    dmo = em.dma("sp", lambda e, i=i, nt=nt, oc=oc, psl=psl, scr=scr: e.dma_start(
                        aps[scr][i][128 * nt:128 * (nt + 1), 512 * oc:512 * (oc + 1)],
                        sb["sg"][psl % 2][:, :]),
                        deps=[sg] + (P.get(("mh_scr_rd", i), []) if (scr == "scr_mb" and nt == 0 and oc == 0) else []))
                    P["sg_free"] = dmo
                    if scr == "scr_mb":
                        mb_stores.append(dmo)
                P["brow_free"] = mm
                P[("wbuf_free", jbw)] = mm
        mq_stores = []
        rq_done = []
        P[("rq_done", i)] = rq_done
        P[("mq_stores", i)] = mq_stores
        for oc in range(4):
            jbw = oc % 2
            wl = wchunk_load(wspec[0][2], 2048, oc, jbw,
                             [P.get(("wbuf_free", jbw))] + ([P.get("mkT_free")] if jbw == 1 else []))
            bl = bias_row("bq", oc, [P.get("brow_free")])
            for nt in range(NT):
                psl = nt % 4
                mm = proj_chunk(nt, jbw, psl, wl + [bl, psdep_c2(psl)])
                stm, rt = dpfp_chain(
                    nt, psl, mm, 4,
                    aps["scr_mq"][i][128 * nt:128 * (nt + 1), 3072 * oc:3072 * (oc + 1)],
                    i * 320 + 16 * nt + 4 * oc, [])
                rq_done.extend(rt)
                psread(psnm_c2(psl), stm)
                mq_stores.append(stm)
            P["brow_free"] = mm
            P[("wbuf_free", jbw)] = mm
        P["bigA_free"] = mm
        for hp in range(4):
            mkT_toks = []
            for fc in range(FD6):
                mkT_toks.append(em.dma("sp" if fc % 2 == 0 else "act",
                                       lambda e, i=i, hp=hp, fc=fc: e.dma_start(
                    sb["mkT"][:, 2048 * fc:2048 * (fc + 1)],
                    aps["scr_mk"][i][:, 768 * hp + 128 * fc:768 * hp + 128 * (fc + 1)],
                    transpose=True),
                    deps=(mk_stores + [P.get("mkT_free"), P.get(("wbuf_free", 1))]) if fc < 2 else mk_stores))
            wmv_toks = []
            for nt in range(NT):
                psl = nt % 2
                mm = None
                for fc in range(FD6):
                    mm = em.op("pe", lambda e, hp=hp, nt=nt, fc=fc, psl=psl,
                               st_=(fc == 0), sp_=(fc == FD6 - 1): e.matmul(
                        sb["psA"][psl][:, :],
                        sb["mkT"][:, 2048 * fc + 128 * nt:2048 * fc + 128 * (nt + 1)],
                        bass.AP(sb["mem"], 129 * (FD6 * 4 * hp) + 129 * fc,
                                [[129 * FD6 * 16, 128], [129 * FD6, 4], [1, 128]]),
                        start=st_, stop=sp_),
                        deps=[mkT_toks[fc]] + ((mem_done + [psdep(f"psA{psl}")]) if fc == 0 else []))
                md = None
                for fc in range(FD6):
                    md = em.op("pe", lambda e, hp=hp, nt=nt, fc=fc,
                               st_=(fc == 0), sp_=(fc == FD6 - 1): e.matmul(
                        sb["pden"][:, 0:4],
                        sb["mkT"][:, 2048 * fc + 128 * nt:2048 * fc + 128 * (nt + 1)],
                        bass.AP(sb["mem"], 129 * (FD6 * 4 * hp) + 129 * fc + 128,
                                [[129 * FD6 * 16, 128], [129 * FD6, 4], [1, 1]]),
                        start=st_, stop=sp_),
                        deps=[psdep("pden")] if fc == 0 else [])
                mbl = em.dma("sp", lambda e, i=i, hp=hp, nt=nt, j=nt % 2: e.dma_start(
                    sb["mb4"][nt % 2][:, :],
                    aps["scr_mb"][i][128 * nt:128 * (nt + 1), 512 * hp:512 * (hp + 1)]),
                    deps=[P.get(("mb4_free", nt % 2))] + mb_stores)
                P.setdefault(("mb_rd_done", i), []).append(mbl)
                kc = i * 320 + 256 + 4 * nt + hp
                c2 = em.op("dve", lambda e, kc=kc: e.tensor_scalar(
                    sb["colsc"][:, 20:24], sb["pden"][:, 0:4],
                    sb["colsc"][:, 15:16], None, ALU.add),
                    deps=[md, P.get("colsc2_free"),
                          em.op("dve", lambda e2, kc=kc: e2.tensor_scalar(
                              sb["colsc"][:, 15:16], sb["rq2"][:, kc:kc + 1],
                              EPS_C, None, ALU.mult),
                              deps=rk_done + [P.get("colsc15_free")])])
                P["colsc15_free"] = c2
                c3 = em.op("dve", lambda e: e.reciprocal(
                    sb["colsc"][:, 24:28], sb["colsc"][:, 20:24]), deps=[c2])
                c4 = em.op("dve", lambda e, kc=kc: e.tensor_scalar(
                    sb["colsc"][:, 16:20], sb["pden"][:, 0:4],
                    sb["rq"][:, kc:kc + 1], EPS_C, ALU.mult, ALU.add),
                    deps=[md, c2] + rk_done)
                c5 = em.op("dve", lambda e: e.tensor_scalar(
                    sb["colsc"][:, 20:24], sb["colsc"][:, 16:20], 1.0, -1.0,
                    ALU.subtract, ALU.mult), deps=[c4, c3])
                c6 = em.op("dve", lambda e: e.tensor_scalar(
                    sb["colsc"][:, 16:20], sb["colsc"][:, 20:24], 0.0, 1.0,
                    ALU.max, ALU.min), deps=[c5])
                c7 = em.op("dve", lambda e, kc=kc: e.tensor_scalar(
                    sb["colsc"][:, 28:32], sb["colsc"][:, 16:20],
                    sb["rq"][:, kc:kc + 1], None, ALU.mult), deps=[c6])
                psread("pden", c5)
                mvts = []
                for hh in range(4):
                    ng = em.op("dve", lambda e, hh=hh: e.tensor_scalar_mul(
                        sb["colsc"][:, 16 + hh:17 + hh], sb["colsc"][:, 24 + hh:25 + hh],
                        -1.0), deps=[c3, c7])
                    mv = em.op("dve", lambda e, hh=hh, hp=hp, nt=nt, psl=psl: e.scalar_tensor_tensor(
                        sb["mv4"][:, 128 * hh:128 * (hh + 1)],
                        sb["psA"][psl][:, 128 * hh:128 * (hh + 1)],
                        sb["colsc"][:, 16 + hh:17 + hh],
                        sb["vbuf"][:, 512 * nt + 128 * hp:512 * nt + 128 * (hp + 1)],
                        ALU.mult, ALU.add),
                        deps=[ng, mm, P.get("mv4_free"), P.get("dstage_free")] + v_evs)
                    mvts.append(mv)
                psread(f"psA{psl}", mvts[-1])
                w1 = em.op("dve", lambda e, nt=nt: e.tensor_tensor(
                    sb["mv4"][:, 0:512], sb["mv4"][:, 0:512], sb["mb4"][nt % 2][:, :],
                    ALU.mult), deps=mvts + [mbl])
                P[("mb4_free", nt % 2)] = w1
                P["mv4_free"] = w1
                wts = []
                for hh in range(4):
                    kc = i * 320 + 256 + 4 * nt + hp
                    wv_ = em.op("dve", lambda e, hh=hh, nt=nt, kc=kc: e.tensor_scalar(
                        bass.AP(sb["wmv4"], 516 * nt + 129 * hh, [[516 * NT, 128], [1, 128]]),
                        sb["mv4"][:, 128 * hh:128 * (hh + 1)],
                        sb["rq"][:, kc:kc + 1], None, ALU.mult),
                        deps=[w1, P.get(("wmv4_free", nt))])
                    P["mv4_free"] = wv_
                    cc_ = em.op("dve", lambda e, hh=hh, nt=nt: e.tensor_copy(
                        bass.AP(sb["wmv4"], 516 * nt + 129 * hh + 128, [[516 * NT, 128], [1, 1]]),
                        sb["colsc"][:, 28 + hh:29 + hh]), deps=[c7, P.get(("wmv4_free", nt))])
                    wts += [wv_, cc_]
                P["colsc2_free"] = wts[-1]
                wmv_toks.append(wts)
                P[("wmv", nt)] = wts
            P["mkT_free"] = md
            dmm = {ft: None for ft in range(FD6)}
            dnm = None
            for nt in range(NT):
                j = nt % 2
                mkl = em.dma("sp", lambda e, i=i, hp=hp, nt=nt, j=j: e.dma_start(
                    sb["mkt"][j][:, :],
                    aps["scr_mk"][i][128 * nt:128 * (nt + 1), 768 * hp:768 * (hp + 1)]),
                    deps=[P.get(("mkt_free", j))] + mk_stores)
                for ft in range(FD6):
                    pnm = f"psA{ft}" if ft < 2 else f"pd{ft}"
                    dmm[ft] = em.op("pe", lambda e, ft=ft, nt=nt, j=j,
                                    st_=(nt == 0), sp_=(nt == NT - 1): e.matmul(
                        sb["pd"][ft][:, :],
                        sb["mkt"][j][:, 128 * ft:128 * (ft + 1)],
                        bass.AP(sb["wmv4"], 516 * nt, [[516 * NT, 128], [129, 4], [1, 128]]),
                        start=st_, stop=sp_),
                        deps=[mkl] + P[("wmv", nt)] + ([psdep(pnm)] if nt == 0 else []))
                for ft in range(FD6):
                    dnm = em.op("pe", lambda e, ft=ft, nt=nt, j=j,
                                st_=(nt == 0 and ft == 0),
                                sp_=(nt == NT - 1 and ft == FD6 - 1): e.matmul(
                        sb["pdn"][:, 4 * ft:4 * (ft + 1)],
                        sb["mkt"][j][:, 128 * ft:128 * (ft + 1)],
                        bass.AP(sb["wmv4"], 516 * nt + 128, [[516 * NT, 128], [129, 4], [1, 1]]),
                        start=st_, stop=sp_, skip_group_check=True),
                        deps=[mkl] + P[("wmv", nt)] + ([psdep("pdn")] if nt == 0 and ft == 0 else []))
                P[("mkt_free", j)] = [dmm[FD6 - 1], dnm]
            for nt in range(NT):
                P[("wmv4_free", nt)] = [dmm[FD6 - 1], dnm]
            for ft in range(FD6):
                pnm = f"psA{ft}" if ft < 2 else f"pd{ft}"
                ev = em.op("dve", lambda e, ft=ft: e.tensor_copy(
                    sb["dstage"][:, 0:512], sb["pd"][ft][:, :]),
                    deps=[dmm[ft], P.get("dstage_free")])
                psread(pnm, ev)
                dst_off = (hp * FD6 + ft) * 512
                dmo = em.dma("pool", lambda g, dst_off=dst_off, first=(i == 0): g.dma_start(
                    bass.AP(aps["cc_in"].tensor, aps["cc_in"].offset + dst_off,
                            [[12384, 128], [1, 512]]),
                    sb["dstage"][:, 0:512],
                    accum_op=(ALU.bypass if first else ALU.add)),
                    deps=[ev, P.get(("delta_dma", hp, ft))])
                P[("delta_dma", hp, ft)] = dmo
                P["dstage_free"] = dmo
                kv_delta_evacs.append(dmo)
            evn = em.op("dve", lambda e: e.tensor_copy(
                sb["dstage"][:, 0:24], sb["pdn"][:, 0:24]),
                deps=[dnm, P.get("dstage_free")])
            psread("pdn", evn)
            dst_off = 4 * FD6 * 512 + hp * 24
            dmo = em.dma("pool", lambda g, dst_off=dst_off, first=(i == 0): g.dma_start(
                bass.AP(aps["cc_in"].tensor, aps["cc_in"].offset + dst_off,
                        [[12384, 128], [1, 24]]),
                sb["dstage"][:, 0:24],
                accum_op=(ALU.bypass if first else ALU.add)),
                deps=[evn, P.get(("delta_dma_n", hp))])
            P[("delta_dma_n", hp)] = dmo
            P["dstage_free"] = dmo
            kv_delta_evacs.append(dmo)
        P["vbuf_free"] = wts[-1]

    cc = em.op("pool", lambda g: g.collective_compute(
        "AllReduce", ALU.add, replica_groups=[list(range(n_cores))],
        ins=[aps["cc_in"].opt()], outs=[aps["cc_out"].opt()]),
        deps=kv_delta_evacs)
    newmem_toks = []
    for h in range(16):
        hp, hh = h // 4, h % 4
        for fc in range(FD6):
            c0 = 129 * (FD6 * h + fc)
            l1 = em.dma("sp", lambda e, h=h, fc=fc: e.dma_start(
                sb["f1"][:, 0:128],
                bass.AP(aps["memory"].tensor,
                        aps["memory"].offset + h * 768 * 128 + 128 * fc * 128,
                        [[128, 128], [1, 128]])), deps=[P.get("f1_free"), cc])
            l2 = em.dma("sp", lambda e, hp=hp, fc=fc, hh=hh: e.dma_start(
                sb["f2"][:, 0:128],
                bass.AP(aps["cc_out"].tensor,
                        aps["cc_out"].offset + (hp * FD6 + fc) * 512 + 128 * hh,
                        [[12384, 128], [1, 128]])), deps=[P.get("f2_free"), cc])
            a1 = em.op("dve", lambda e, c0=c0: e.tensor_add(
                sb["mem"][:, c0:c0 + 128], sb["f1"][:, 0:128], sb["f2"][:, 0:128]),
                deps=[l1, l2, P.get("mem_w_ok")])
            P["f1_free"] = a1
            P["f2_free"] = a1
            l3 = em.dma("sp", lambda e, h=h, fc=fc: e.dma_start(
                sb["f1"][:, 128:129],
                bass.AP(aps["norm"].tensor, aps["norm"].offset + h * 768 + 128 * fc,
                        [[1, 128], [1, 1]])), deps=[a1])
            l4 = em.dma("sp", lambda e, hp=hp, fc=fc, hh=hh: e.dma_start(
                sb["f2"][:, 128:129],
                bass.AP(aps["cc_out"].tensor,
                        aps["cc_out"].offset + 4 * FD6 * 512 + hp * 24 + fc * 4 + hh,
                        [[12384, 128], [1, 1]]),
                allow_slow_non_contiguous=True), deps=[a1])
            a2 = em.op("dve", lambda e, c0=c0: e.tensor_add(
                sb["mem"][:, c0 + 128:c0 + 129], sb["f1"][:, 128:129], sb["f2"][:, 128:129]),
                deps=[l3, l4])
            P["f1_free"] = a2
            P["f2_free"] = a2
            newmem_toks.append(a2)
    P["mem_w_ok"] = None

    x3_stores = []
    RING = [(sb["mkT"], 2048 * j) for j in range(FD6)] + \
           [(sb["wbuf"][0], 2048 * j) for j in range(4)]
    ring_reader = {}
    gtile = [0]
    for i in range(BPC):
        mh_stores = []
        v_evs_last = [P.get("vbuf_free")]
        for h in range(16):
            mh_toks = []
            tile_sl = []
            tile_tok = []
            for fc in range(FD6):
                g = gtile[0]
                gtile[0] += 1
                sl = g % len(RING)
                buf, c0 = RING[sl]
                war = [ring_reader.get(sl)]
                if sl not in ring_reader:
                    war += ([P.get("mkT_free"), P.get(("wbuf_free", 1))]
                            if sl < FD6 else [P.get(("wbuf_free", 0))])
                tile_tok.append(em.dma("sp" if g % 2 == 0 else "act",
                                       lambda e, i=i, h=h, fc=fc, buf=buf, c0=c0: e.dma_start(
                    buf[:, c0:c0 + 2048],
                    aps["scr_mq"][i][:, 768 * h + 128 * fc:768 * h + 128 * (fc + 1)],
                    transpose=True),
                    deps=war + P.get(("mq_stores", i), [])))
                tile_sl.append(sl)
            for nt in range(NT):
                psl = nt % 2
                mm = None
                for fc in range(FD6):
                    buf, c0 = RING[tile_sl[fc]]
                    mm = em.op("pe", lambda e, h=h, nt=nt, fc=fc, psl=psl, buf=buf, c0=c0,
                               st_=(fc == 0), sp_=(fc == FD6 - 1): e.matmul(
                        sb["psA"][psl][:, 0:129],
                        buf[:, c0 + 128 * nt:c0 + 128 * (nt + 1)],
                        sb["mem"][:, 129 * (FD6 * h + fc):129 * (FD6 * h + fc) + 129],
                        start=st_, stop=sp_),
                        deps=[tile_tok[fc]] + ((newmem_toks + [psdep(f"psA{psl}")]) if fc == 0 else []))
                qcr = i * 320 + 16 * nt + h
                d1 = em.op("dve", lambda e, qcr=qcr, psl=psl: e.scalar_tensor_tensor(
                    sb["colsc"][:, 0:1], sb["rq2"][:, qcr:qcr + 1], EPS_C,
                    sb["psA"][psl][:, 128:129], ALU.mult, ALU.add),
                    deps=[mm, P.get("colsc9_free")] + P.get(("rq_done", i), []))
                d2 = em.op("dve", lambda e: e.reciprocal(
                    sb["colsc"][:, 1:2], sb["colsc"][:, 0:1]), deps=[d1])
                mh = em.op("dve", lambda e, psl=psl, nt=nt: e.tensor_scalar(
                    sb["vbuf"][:, 128 * nt:128 * (nt + 1)], sb["psA"][psl][:, 0:128],
                    sb["colsc"][:, 1:2], None, ALU.mult),
                    deps=[d2, mm, P.get(("mhh_free", nt))] + (
                        v_evs_last if (h == 0 and nt == 0) else []))
                P["colsc9_free"] = mh
                psread(f"psA{psl}", mh)
                mh_toks.append(mh)
            dmo = em.dma("sp", lambda e, i=i, h=h: e.dma_start(
                bass.AP(aps["scr_mb"][i].tensor,
                        aps["scr_mb"][i].offset + 128 * h,
                        [[2048, 128], [128 * 2048, NT], [1, 128]]),
                sb["vbuf"][:, 0:2048]),
                deps=mh_toks + (P.get(("mb_rd_done", i), []) if h == 0 else []))
            for nt in range(NT):
                P[("mhh_free", nt)] = dmo
            mh_stores.append(dmo)
            for sl in tile_sl:
                ring_reader[sl] = mm
            P["mkT_free"] = mm
        for nt in range(NT):
            for dc2 in range(4):
                psl = (nt * 4 + dc2) % 2
                lx = em.dma("sp", lambda e, i=i, nt=nt, dc2=dc2, psl=psl: e.dma_start(
                    sb["x2c"][psl][:, :],
                    aps["x2"][i][128 * nt:128 * (nt + 1), 512 * dc2:512 * (dc2 + 1)]),
                    deps=[P.get("x2c_free")])
                lm = em.dma("sp", lambda e, i=i, nt=nt, dc2=dc2, psl=psl: e.dma_start(
                    sb["mhc"][1][:, :],
                    aps["scr_mb"][i][128 * nt:128 * (nt + 1), 512 * dc2:512 * (dc2 + 1)]),
                    deps=[P.get("mhc1_free")] + mh_stores)
                lg = em.dma("sp", lambda e, i=i, nt=nt, dc2=dc2, psl=psl: e.dma_start(
                    sb["gc"][psl][:, :],
                    aps["scr_g"][i][128 * nt:128 * (nt + 1), 512 * dc2:512 * (dc2 + 1)]),
                    deps=[P.get("gc_free")])
                c = 2 * (i * NT + nt)
                o1 = em.op("dve", lambda e, c=c, psl=psl: e.tensor_scalar(
                    sb["f1"][:, :], sb["x2c"][psl][:, :],
                    sb["stats"][:, c:c + 1], sb["stats"][:, c + 1:c + 2],
                    ALU.add, ALU.mult), deps=[lx, P.get("f1_free")])
                o2 = em.op("dve", lambda e, psl=psl: e.tensor_tensor(
                    sb["f2"][:, :], sb["gc"][psl][:, :], sb["mhc"][1][:, :],
                    ALU.mult), deps=[lg, lm, P.get("f2_free")])
                P["mhc1_free"] = o2
                o3 = em.op("dve", lambda e, psl=psl: e.tensor_add(
                    sb["x3c"][psl][:, :], sb["f1"][:, :], sb["f2"][:, :]),
                    deps=[o1, o2, P.get("x3c_free")])
                P["f2_free"] = o3
                P["gc_free"] = o2
                o4 = em.op("dve", lambda e, psl=psl: e.tensor_add(
                    sb["f1"][:, :], sb["x3c"][psl][:, :], sb["x2c"][psl][:, :]),
                    deps=[o3])
                P["x3c_free"] = o4
                P["x2c_free"] = o4
                dmo = em.dma("sp", lambda e, i=i, nt=nt, dc2=dc2, psl=psl: e.dma_start(
                    aps["x3"][i][128 * nt:128 * (nt + 1), 512 * dc2:512 * (dc2 + 1)],
                    sb["f1"][:, :]), deps=[o4])
                P["f1_free"] = dmo
                x3_stores.append(dmo)
    return x3_stores


def build_l1_attn_program(n_cores=8):
    nc = bass.Bass("TRN2", target_bir_lowering=False)
    x2 = nc.dram_tensor("x2", [BPC * T, D], F32, kind="ExternalInput").ap()
    wq = nc.dram_tensor("wq", [D, D], F32, kind="ExternalInput").ap()
    wk = nc.dram_tensor("wk", [512, D], F32, kind="ExternalInput").ap()
    wv = nc.dram_tensor("wv", [512, D], F32, kind="ExternalInput").ap()
    wg = nc.dram_tensor("wg", [D, D], F32, kind="ExternalInput").ap()
    wmb = nc.dram_tensor("wmb", [D, D], F32, kind="ExternalInput").ap()
    bq = nc.dram_tensor("bq", [D], F32, kind="ExternalInput").ap()
    bk = nc.dram_tensor("bk", [512], F32, kind="ExternalInput").ap()
    bv = nc.dram_tensor("bv", [512], F32, kind="ExternalInput").ap()
    bg = nc.dram_tensor("bg", [D], F32, kind="ExternalInput").ap()
    bmb = nc.dram_tensor("bmb", [D], F32, kind="ExternalInput").ap()
    memory = nc.dram_tensor("memory", [16, 768, 128], F32, kind="ExternalInput").ap()
    norm = nc.dram_tensor("norm", [16, 768], F32, kind="ExternalInput").ap()
    x3 = nc.dram_tensor("x3", [BPC * T, D], F32, kind="ExternalOutput").ap()
    xln = nc.dram_tensor("xlnC", [BPC * T, D], BF16).ap()
    scr_w = nc.dram_tensor("scr_wC", [3 * D * D + 2 * 512 * D], BF16).ap()
    scr_mk = nc.dram_tensor("scr_mkC", [BPC * T, 3072], BF16).ap()
    scr_mq = nc.dram_tensor("scr_mqC", [BPC * T, 12288], BF16).ap()
    scr_mb = nc.dram_tensor("scr_mbC", [BPC * T, 2048], BF16).ap()
    scr_g = nc.dram_tensor("scr_gC", [BPC * T, 2048], BF16).ap()
    cc_in = nc.dram_tensor("cc_inC", [128, 12384], F32).ap()
    cc_out = nc.dram_tensor("cc_outC", [128, 12384], F32).ap()
    em = Em(nc)
    sb = alloc_l1_attn_sbuf(em)
    aps = {
        "x2": [x2[i * T:(i + 1) * T, :] for i in range(BPC)],
        "x3": [x3[i * T:(i + 1) * T, :] for i in range(BPC)],
        "xln": [xln[i * T:(i + 1) * T, :] for i in range(BPC)],
        "scr_mk": [scr_mk[i * T:(i + 1) * T, :] for i in range(BPC)],
        "scr_mq": [scr_mq[i * T:(i + 1) * T, :] for i in range(BPC)],
        "scr_mb": [scr_mb[i * T:(i + 1) * T, :] for i in range(BPC)],
        "scr_g": [scr_g[i * T:(i + 1) * T, :] for i in range(BPC)],
        "scr_w": scr_w, "cc_in": cc_in, "cc_out": cc_out,
        "wq": wq, "wk": wk, "wv": wv, "wg": wg, "wmb": wmb,
        "bq": bq, "bk": bk, "bv": bv, "bg": bg, "bmb": bmb,
        "memory": memory, "norm": norm,
    }
    emit_l1_attn(em, aps, sb, n_cores=n_cores)
    em.emit()
    return nc



from concourse.aot_env import allow_ici_all_to_all

NH_D = 16
HD_D = 128
SEL = 256
ST_D = 4096


def alloc_l1_ffn_sbuf(em):
    nc = em.nc
    st = em.stack
    sb = {}
    sb["lnx"] = [st.enter_context(nc.sbuf_tensor("lnxD", [128, D], F32))]
    sb["lnxln"] = [st.enter_context(nc.sbuf_tensor("lnxlnD", [128, D], BF16))]
    sb["lncol"] = st.enter_context(nc.sbuf_tensor("lncolD", [128, 32], F32))
    sb["stats"] = st.enter_context(nc.sbuf_tensor("statsD", [128, 2 * NT * BPC], F32))
    sb["xt"] = [st.enter_context(nc.sbuf_tensor(f"xtD{j}", [128, 2048], BF16)) for j in range(2)]
    sb["wrt"] = st.enter_context(nc.sbuf_tensor("wrtD", [128, 16], BF16))
    sb["scol"] = st.enter_context(nc.sbuf_tensor("scolD", [128, 192], F32))
    sb["sbc"] = st.enter_context(nc.sbuf_tensor("sbcD", [128, 2048], F32))
    sb["cmp"] = st.enter_context(nc.sbuf_tensor("cmpD", [128, 2048], BF16))
    sb["tri1"] = st.enter_context(nc.sbuf_tensor("tri1D", [128, 128], BF16))
    sb["triu"] = st.enter_context(nc.sbuf_tensor("triuD", [128, 128], BF16))
    sb["iota"] = st.enter_context(nc.sbuf_tensor("iotaD", [128, 256], F32))
    sb["iotc"] = st.enter_context(nc.sbuf_tensor("iotcD", [128, 2], F32))
    sb["Pm"] = [st.enter_context(nc.sbuf_tensor("PmD0", [128, 4096], BF16))] * 2
    sb["PT"] = st.enter_context(nc.sbuf_tensor("PTD", [128, 4096], BF16))
    sb["xsel"] = [st.enter_context(nc.sbuf_tensor(f"xselD{j}", [128, 4096], BF16)) for j in range(2)]
    sb["cs4"] = st.enter_context(nc.sbuf_tensor("cs4D", [128, 1024], BF16))
    sb["cs4b"] = st.enter_context(nc.sbuf_tensor("cs4bD", [128, 1024], BF16))
    sb["bigD"] = st.enter_context(nc.sbuf_tensor("bigDD", [128, 16384], BF16))
    class _View:
        def __init__(self, t, off, size):
            self.t, self.off, self.size = t, off, size
        def __getitem__(self, idx):
            p, f = idx
            f0 = 0 if f.start is None else f.start
            f1 = self.size if f.stop is None else f.stop
            return self.t[p, f0 + self.off:f1 + self.off]
    sb["hD"] = _View(sb["bigD"], 0, 4096)
    sb["hT"] = _View(sb["bigD"], 4096, 4096)
    sb["kq"] = _View(sb["bigD"], 0, 4096)
    sb["kk"] = _View(sb["bigD"], 4096, 4096)
    sb["vv"] = _View(sb["bigD"], 8192, 4096)
    sb["ctxT"] = _View(sb["bigD"], 12288, 4096)
    sb["aout"] = _View(sb["bigD"], 0, 8192)
    sb["h2T"] = _View(sb["bigD"], 8192, 8192)
    sb["ln1b"] = st.enter_context(nc.sbuf_tensor("ln1bD", [128, 2048], BF16))
    sb["ln2b"] = st.enter_context(nc.sbuf_tensor("ln2bD", [128, 2048], BF16))
    sb["wbuf"] = [st.enter_context(nc.sbuf_tensor(f"wbufD{j}", [128, 8192], BF16)) for j in range(2)]
    sb["brow"] = st.enter_context(nc.sbuf_tensor("browD", [1, 512], BF16))
    sb["ones1"] = st.enter_context(nc.sbuf_tensor("ones1D", [1, 128], BF16))
    sb["onesf"] = st.enter_context(nc.sbuf_tensor("onesfD", [1, 128], F32))
    sb["onecol"] = st.enter_context(nc.sbuf_tensor("onecolD", [128, 1], BF16))
    sb["rt1"] = st.enter_context(nc.sbuf_tensor("rt1D", [128, 512], BF16))
    sb["rt2"] = st.enter_context(nc.sbuf_tensor("rt2D", [128, 512], BF16))
    sb["qro"] = [st.enter_context(nc.sbuf_tensor(f"qroD{j}", [128, 512], BF16)) for j in range(2)]
    sb["at"] = [st.enter_context(nc.sbuf_tensor(f"atD{j}", [128, 512], BF16)) for j in range(2)]
    sb["at2"] = [st.enter_context(nc.sbuf_tensor(f"at2D{j}", [128, 512], BF16)) for j in range(2)]
    sb["mask"] = [st.enter_context(nc.sbuf_tensor(f"maskD{j}", [128, 512], BF16)) for j in range(4)]
    sb["zrb"] = st.enter_context(nc.sbuf_tensor("zrbD", [1, 512], BF16))
    sb["zb"] = st.enter_context(nc.sbuf_tensor("zbD", [128, 512], BF16))
    sb["cxl"] = [st.enter_context(nc.sbuf_tensor(f"cxlD{j}", [128, 2048], BF16)) for j in range(2)]
    sb["selb"] = st.enter_context(nc.sbuf_tensor("selbD", [128, 16], BF16))
    sb["scb"] = st.enter_context(nc.sbuf_tensor("scbD", [128, 16], BF16))
    sb["x1t"] = st.enter_context(nc.sbuf_tensor("x1tD", [128, 2048], F32))
    sb["h2"] = st.enter_context(nc.sbuf_tensor("h2D", [128, 2048], BF16))
    sb["silu"] = st.enter_context(nc.sbuf_tensor("siluD", [128, 512], BF16))
    sb["mit"] = st.enter_context(nc.sbuf_tensor("mitD", [128, 512], BF16))
    sb["gcol"] = st.enter_context(nc.sbuf_tensor("gcolD", [128, 8], F32))
    sb["gD"] = st.enter_context(nc.sbuf_tensor("gDD", [128, 8192], BF16))
    sb["x3c"] = [st.enter_context(nc.sbuf_tensor("x3cD0", [128, 512], F32))] * 2
    sb["f1"] = st.enter_context(nc.sbuf_tensor("f1D", [128, 512], F32))
    sb["f2"] = st.enter_context(nc.sbuf_tensor("f2D", [128, 512], F32))
    sb["x4c"] = [st.enter_context(nc.sbuf_tensor("x4cD0", [128, 512], F32))] * 2
    sb["ident"] = st.enter_context(nc.sbuf_tensor("identD", [128, 128], BF16))
    sb["psA"] = [st.enter_context(nc.psum_tensor(f"psAD{j}", [128, 512], F32)) for j in range(2)]
    sb["psB"] = [st.enter_context(nc.psum_tensor(f"psBD{j}", [128, 512], F32)) for j in range(2)]
    sb["psc"] = st.enter_context(nc.psum_tensor("pscD", [128, 16], F32))
    sb["pz"] = st.enter_context(nc.psum_tensor("pzD", [128, 512], F32))
    sb["pz1"] = st.enter_context(nc.psum_tensor("pz1D", [1, 512], F32))
    sb["pt"] = st.enter_context(nc.psum_tensor("ptD", [128, 128], BF16))
    return sb


def emit_l1_ffn(em, aps, sb, n_cores=8):
    P = {}
    ps_reader = {}

    def psdep(name):
        return ps_reader.get(name)

    def psread(name, tok):
        ps_reader[name] = tok

    ident_t = make_identity_bf16(em, sb["ident"])
    ones_t = em.op("pool", lambda g: g.memset(sb["ones1"][:, :], 1.0))
    onesf_t = em.op("pool", lambda g: g.memset(sb["onesf"][:, :], 1.0))
    onec_t = em.op("pool", lambda g: g.memset(sb["onecol"][:, :], 1.0))
    t1_t = em.op("pool", lambda g: g.memset(sb["tri1"][:, :], 1.0))
    tu0 = em.op("pool", lambda g: g.memset(sb["triu"][:, :], 1.0))
    tu_t = em.op("pool", lambda g: g.affine_select(
        sb["triu"][:, :], sb["triu"][:, :], [[1, 128]], ALU.is_gt, 0.0,
        base=0, channel_multiplier=-1), deps=[tu0])
    io_t = em.op("pool", lambda g: g.iota(
        sb["iota"][:, :], [[1, 256]], base=0, channel_multiplier=0,
        allow_small_or_imprecise_dtypes=True))
    ioc_t = em.op("pool", lambda g: g.iota(
        sb["iotc"][:, :], [[128, 2]], base=0, channel_multiplier=1,
        allow_small_or_imprecise_dtypes=True))
    mtoks = [causal_keep_mask(em, sb["mask"][j], 128 * j) for j in range(4)]
    wrt_t = em.dma("pool", lambda g: g.dma_start(
        sb["wrt"][:, :],
        bass.AP(aps["w_router"].tensor, aps["w_router"].offset, [[1, 128], [128, 16]]),
        allow_slow_non_contiguous=True))
    bc_done = []
    prevmm = None
    for (wn, dst) in (("ln1_w", "ln1b"), ("ln2_w", "ln2b")):
        for c in range(4):
            lr = em.dma("pool", lambda g, wn=wn, c=c: g.dma_start(
                sb["brow"][:, :],
                bass.AP(aps[wn].tensor, aps[wn].offset + 512 * c, [[512, 1], [1, 512]])),
                deps=[prevmm])
            mm = em.op("pe", lambda e: e.matmul(
                sb["psA"][0][:, :], sb["ones1"][:, :], sb["brow"][:, :],
                start=True, stop=True), deps=[lr, ones_t, psdep("psA0")])
            prevmm = mm
            cp = em.op("dve", lambda e, dst=dst, c=c: e.tensor_copy(
                sb[dst][:, 512 * c:512 * (c + 1)], sb["psA"][0][:, :]), deps=[mm])
            psread("psA0", cp)
            bc_done.append(cp)
    a2a_stores = []
    for i in range(BPC):
        ln_toks = emit_ln_pass(em, aps["x3"][i], aps["xln"][i], sb, i, store_stats=True)
        sc0 = i * 96
        rmm = None
        for dc in range(DC):
            j = dc % 2
            xl = em.dma("sp", lambda e, i=i, dc=dc, j=j: e.dma_start(
                sb["xt"][j][:, :], aps["xln"][i][:, 128 * dc:128 * (dc + 1)],
                transpose=True), deps=ln_toks + [P.get(("xt_free", j))])
            for nt in range(NT):
                rmm = em.op("pe", lambda e, dc=dc, nt=nt, j=j,
                            st_=(dc == 0 and nt == 0),
                            sp_=(dc == DC - 1 and nt == NT - 1): e.matmul(
                    sb["psc"][:, nt:nt + 1],
                    sb["xt"][j][:, 128 * nt:128 * (nt + 1)],
                    sb["wrt"][:, dc:dc + 1],
                    start=st_, stop=sp_, skip_group_check=True),
                    deps=[xl, wrt_t] + ([psdep("psc")] if dc == 0 and nt == 0 else []))
            P[("xt_free", j)] = rmm
        sev = em.op("dve", lambda e, sc0=sc0: e.tensor_copy(
            sb["scol"][:, sc0:sc0 + 16], sb["psc"][:, :]), deps=[rmm, P.get("scol_w")])
        psread("psc", sev)
        scb_t = em.op("act", lambda e, sc0=sc0: e.copy(
            sb["scb"][:, :], sb["scol"][:, sc0:sc0 + 16]), deps=[sev, P.get("scb_free")])
        r1 = em.dma("sp", lambda e, sc0=sc0: e.dma_start(
            bass.AP(aps["scr_srow"].tensor, aps["scr_srow"].offset, [[1, 128], [128, 16]]),
            sb["scol"][:, sc0:sc0 + 16], allow_slow_non_contiguous=True),
            deps=[sev, P.get("srow_free")])
        r2 = em.dma("sp", lambda e: e.dma_start(
            sb["sbc"][0:1, :],
            bass.AP(aps["scr_srow"].tensor, aps["scr_srow"].offset, [[2048, 1], [1, 2048]])),
            deps=[r1, P.get("sbc_free")])
        r3s = []
        for bcc in range(4):
            bmm = em.op("pe", lambda e, bcc=bcc: e.matmul(
                sb["pz"][:, :], sb["onesf"][:, :], sb["sbc"][0:1, 512 * bcc:512 * (bcc + 1)],
                start=True, stop=True), deps=[r2, onesf_t, psdep("pz")])
            bcp = em.op("dve", lambda e, bcc=bcc: e.tensor_copy(
                sb["sbc"][:, 512 * bcc:512 * (bcc + 1)], sb["pz"][:, :]), deps=[bmm])
            psread("pz", bcp)
            r3s.append(bcp)
        r3 = r3s
        pass
        selb_toks = []
        for nt in range(NT):
            cm = em.op("dve", lambda e, nt=nt, sc0=sc0: e.tensor_scalar(
                sb["cmp"][:, :], sb["sbc"][:, :],
                sb["scol"][:, sc0 + nt:sc0 + nt + 1], None, ALU.is_gt),
                deps=[r3, sev, P.get("cmp_free")])
            rk = em.op("dve", lambda e, nt=nt, sc0=sc0: e.reduce_sum(
                sb["scol"][:, sc0 + 16 + nt:sc0 + 17 + nt], sb["cmp"][:, :], axis=AX.X),
                deps=[cm])
            P["cmp_free"] = rk
            sl = em.op("dve", lambda e, nt=nt, sc0=sc0: e.tensor_scalar(
                sb["scol"][:, sc0 + 32 + nt:sc0 + 33 + nt],
                sb["scol"][:, sc0 + 16 + nt:sc0 + 17 + nt], 256.0, None, ALU.is_lt),
                deps=[rk])
            sb_ = em.op("dve", lambda e, nt=nt, sc0=sc0: e.tensor_copy(
                sb["selb"][:, nt:nt + 1],
                sb["scol"][:, sc0 + 32 + nt:sc0 + 33 + nt]), deps=[sl, P.get("selb_free")])
            selb_toks.append(sb_)
        P["sbc_free"] = P["cmp_free"]
        omm = None
        for nt in range(NT):
            for c in range(nt + 1):
                omm = em.op("pe", lambda e, nt=nt, c=c,
                            st_=(nt == 0 and c == 0),
                            sp_=(nt == NT - 1 and c == NT - 1): e.matmul(
                    sb["psc"][:, nt:nt + 1],
                    sb["tri1"][:, :] if c < nt else sb["triu"][:, :],
                    sb["selb"][:, c:c + 1],
                    start=st_, stop=sp_, skip_group_check=True),
                    deps=selb_toks + [t1_t, tu_t] + ([psdep("psc")] if nt == 0 and c == 0 else []))
        oev = em.op("dve", lambda e, sc0=sc0: e.tensor_copy(
            sb["scol"][:, sc0 + 48:sc0 + 64], sb["psc"][:, :]), deps=[omm])
        psread("psc", oev)
        om1 = em.op("dve", lambda e, sc0=sc0: e.tensor_scalar(
            sb["scol"][:, sc0 + 80:sc0 + 96], sb["scol"][:, sc0 + 48:sc0 + 64],
            1.0, None, ALU.add), deps=[oev])
        om2 = em.op("dve", lambda e, sc0=sc0: e.tensor_tensor(
            sb["scol"][:, sc0 + 64:sc0 + 80], sb["scol"][:, sc0 + 80:sc0 + 96],
            sb["scol"][:, sc0 + 32:sc0 + 48], ALU.mult), deps=[om1])
        om3 = em.op("dve", lambda e, sc0=sc0: e.tensor_scalar(
            sb["scol"][:, sc0 + 64:sc0 + 80], sb["scol"][:, sc0 + 64:sc0 + 80],
            1.0, None, ALU.subtract), deps=[om2])
        P["scol_w"] = om3
        p_toks = []
        for nt in range(NT):
            pt_ = em.op("dve", lambda e, nt=nt, i=i, sc0=sc0: e.tensor_scalar(
                sb["Pm"][i][:, 256 * nt:256 * (nt + 1)], sb["iota"][:, :],
                sb["scol"][:, sc0 + 64 + nt:sc0 + 65 + nt],
                sb["scol"][:, sc0 + 32 + nt:sc0 + 33 + nt],
                ALU.is_equal, ALU.mult),
                deps=[om3, io_t, P.get("Pm_free")])
            p_toks.append(pt_)
        P[("p_toks", i)] = p_toks
        xs_toks = []
        for dc in range(4):
            mm0 = mm1 = None
            for ncx in range(NT):
                j = ncx % 2
                xl = em.dma("sp", lambda e, i=i, ncx=ncx, dc=dc, j=j: e.dma_start(
                    sb["xt"][j][:, 0:512],
                    aps["xln"][i][128 * ncx:128 * (ncx + 1), 512 * dc:512 * (dc + 1)]),
                    deps=ln_toks + [P.get(("xt_free", j))])
                mm0 = em.op("pe", lambda e, ncx=ncx, dc=dc, i=i, j=j,
                            st_=(ncx == 0), sp_=(ncx == NT - 1): e.matmul(
                    sb["psA"][0][:, :],
                    sb["Pm"][i][:, 256 * ncx:256 * ncx + 128],
                    sb["xt"][j][:, 0:512], start=st_, stop=sp_),
                    deps=[xl] + (p_toks + [psdep("psA0")] if ncx == 0 else []))
                mm1 = em.op("pe", lambda e, ncx=ncx, dc=dc, i=i, j=j,
                            st_=(ncx == 0), sp_=(ncx == NT - 1): e.matmul(
                    sb["psA"][1][:, :],
                    sb["Pm"][i][:, 256 * ncx + 128:256 * (ncx + 1)],
                    sb["xt"][j][:, 0:512], start=st_, stop=sp_),
                    deps=[xl] + (p_toks + [psdep("psA1")] if ncx == 0 else []))
                P[("xt_free", j)] = [mm0, mm1]
            for jt in range(2):
                ev = em.op("dve", lambda e, i=i, dc=dc, jt=jt: e.tensor_copy(
                    sb["xsel"][i][:, 2048 * jt + 512 * dc:2048 * jt + 512 * (dc + 1)],
                    sb["psA"][jt][:, :]), deps=[mm0 if jt == 0 else mm1,
                                                P.get(("xsel_free", i))])
                psread(f"psA{jt}", ev)
                xs_toks.append(ev)
        cs_toks = []
        for half in range(2):
            mm0 = mm1 = None
            for ncx in range(NT):
                j = ncx % 2
                cl = em.dma("pool", lambda g, ncx=ncx, half=half, j=j: g.dma_start(
                    sb["xt"][j][:, 0:512],
                    aps["rope_cs"][128 * ncx:128 * (ncx + 1), 512 * half:512 * (half + 1)]),
                    deps=[P.get(("xt_free", j))])
                mm0 = em.op("pe", lambda e, ncx=ncx, i=i, j=j,
                            st_=(ncx == 0), sp_=(ncx == NT - 1): e.matmul(
                    sb["psA"][0][:, :],
                    sb["Pm"][i][:, 256 * ncx:256 * ncx + 128],
                    sb["xt"][j][:, 0:512], start=st_, stop=sp_),
                    deps=[cl] + ([psdep("psA0")] if ncx == 0 else []))
                mm1 = em.op("pe", lambda e, ncx=ncx, i=i, j=j,
                            st_=(ncx == 0), sp_=(ncx == NT - 1): e.matmul(
                    sb["psA"][1][:, :],
                    sb["Pm"][i][:, 256 * ncx + 128:256 * (ncx + 1)],
                    sb["xt"][j][:, 0:512], start=st_, stop=sp_),
                    deps=[cl] + ([psdep("psA1")] if ncx == 0 else []))
                P[("xt_free", j)] = [mm0, mm1]
            dstb = sb["cs4"] if half == 0 else sb["cs4b"]
            for jt in range(2):
                ev = em.op("dve", lambda e, dstb=dstb, jt=jt: e.tensor_copy(
                    dstb[:, 512 * jt:512 * (jt + 1)], sb["psA"][jt][:, :]),
                    deps=[mm0 if jt == 0 else mm1, P.get("cs_free")])
                psread(f"psA{jt}", ev)
                cs_toks.append(ev)
        gmm = {}
        for ncx in range(NT):
            for jt in range(2):
                gmm[jt] = em.op("pe", lambda e, ncx=ncx, i=i, jt=jt,
                                st_=(ncx == 0), sp_=(ncx == NT - 1): e.matmul(
                    sb["psc"][:, 2 * i + jt:2 * i + jt + 1],
                    sb["Pm"][i][:, 256 * ncx + 128 * jt:256 * ncx + 128 * (jt + 1)],
                    sb["scb"][:, ncx:ncx + 1],
                    start=(st_ and i == 0 and jt == 0), sp_=None, stop=(sp_ and jt == 1),
                    skip_group_check=True) if False else e.matmul(
                    sb["psc"][:, 2 * i + jt:2 * i + jt + 1],
                    sb["Pm"][i][:, 256 * ncx + 128 * jt:256 * ncx + 128 * (jt + 1)],
                    sb["scb"][:, ncx:ncx + 1],
                    start=(ncx == 0 and jt == 0), stop=(ncx == NT - 1 and jt == 1),
                    skip_group_check=True),
                    deps=[scb_t] + (p_toks + [psdep("psc")] if ncx == 0 and jt == 0 else []))
        gsg = em.op("act", lambda e, i=i: e.activation(
            sb["gcol"][:, 2 * i:2 * i + 2], sb["psc"][:, 2 * i:2 * i + 2], AF.Sigmoid),
            deps=[gmm[1], P.get("gcol_free")])
        psread("psc", gsg)
        P[("gsg", i)] = gsg
        h_toks = []
        for jt in range(2):
            sqs = []
            for c in range(4):
                sq = em.op("act", lambda e, i=i, jt=jt, c=c: e.activation(
                    sb["cmp"][:, 0:512],
                    sb["xsel"][i][:, 2048 * jt + 512 * c:2048 * jt + 512 * (c + 1)],
                    AF.Square, accum_out=sb["lncol"][:, 20 + c:21 + c]),
                    deps=xs_toks + [P.get("cmp_free"), P.get(("lncol8", c))])
                P["cmp_free"] = sq
                sqs.append(sq)
            v1 = em.op("dve", lambda e: e.tensor_add(
                sb["lncol"][:, 24:25], sb["lncol"][:, 20:21], sb["lncol"][:, 21:22]),
                deps=sqs[:2])
            v2 = em.op("dve", lambda e: e.tensor_add(
                sb["lncol"][:, 25:26], sb["lncol"][:, 22:23], sb["lncol"][:, 23:24]),
                deps=sqs[2:])
            v3 = em.op("dve", lambda e: e.tensor_add(
                sb["lncol"][:, 24:25], sb["lncol"][:, 24:25], sb["lncol"][:, 25:26]),
                deps=[v1, v2]) if False else em.op("dve", lambda e: e.tensor_add(
                sb["lncol"][:, 26:27], sb["lncol"][:, 24:25], sb["lncol"][:, 25:26]),
                deps=[v1, v2])
            c0 = em.op("dve", lambda e: e.tensor_scalar(
                sb["lncol"][:, 27:28], sb["lncol"][:, 26:27], 1.0 / D, 1e-6,
                ALU.mult, ALU.add), deps=[v3])
            c1 = em.op("act", lambda e: e.activation(
                sb["lncol"][:, 24:25], sb["lncol"][:, 27:28], AF.Ln), deps=[c0])
            rst = em.op("act", lambda e: e.activation(
                sb["lncol"][:, 25:26], sb["lncol"][:, 24:25], AF.Exp, scale=-0.5),
                deps=[c1])
            for c in range(4):
                y1 = em.op("dve", lambda e, i=i, jt=jt, c=c: e.tensor_scalar(
                    sb["cmp"][:, 0:512],
                    sb["xsel"][i][:, 2048 * jt + 512 * c:2048 * jt + 512 * (c + 1)],
                    sb["lncol"][:, 25:26], None, ALU.mult),
                    deps=[rst, P.get("cmp_free")])
                y2 = em.op("dve", lambda e, jt=jt, c=c: e.tensor_tensor(
                    sb["hD"][:, 2048 * jt + 512 * c:2048 * jt + 512 * (c + 1)],
                    sb["cmp"][:, 0:512], sb["ln1b"][:, 512 * c:512 * (c + 1)],
                    ALU.mult), deps=[y1, bc_done[3], P.get(("hD_free", jt))])
                P["cmp_free"] = y2
                P[("lncol8", c)] = y1
                h_toks.append(y2)
        ht_toks = []
        for jt in range(2):
            for dcx in range(DC):
                tt_ = em.op("pe", lambda e, jt=jt, dcx=dcx: e.transpose(
                    sb["pt"][:, :], sb["hD"][:, 2048 * jt + 128 * dcx:2048 * jt + 128 * (dcx + 1)],
                    sb["ident"][:, :]), deps=h_toks + [ident_t, psdep("pt")])
                tc_ = em.op("dve", lambda e, jt=jt, dcx=dcx: e.tensor_copy(
                    sb["hT"][:, 256 * dcx + 128 * jt:256 * dcx + 128 * (jt + 1)],
                    sb["pt"][:, :]), deps=[tt_, P.get(("hT_free2", jt, dcx))])
                psread("pt", tc_)
                ht_toks.append(tc_)
        def rope_d(psl, jt, deps_mm):
            r1 = em.op("dve", lambda e, psl=psl, jt=jt: e.scalar_tensor_tensor(
                sb["rt1"][:, :], sb["psA"][psl][:, :], 1.0,
                sb["cs4"][:, 512 * jt:512 * (jt + 1)], ALU.bypass, ALU.mult),
                deps=deps_mm + cs_toks + [P.get("rt1_free")])
            r2 = em.op("dve", lambda e, psl=psl, jt=jt: e.tensor_tensor(
                bass.AP(sb["rt2"], 0, [[512, 128], [128, 4], [1, 64]]),
                bass.AP(sb["psA"][psl], 64, [[512, 128], [128, 4], [1, 64]]),
                bass.AP(sb["cs4b"], 512 * jt, [[1024, 128], [128, 4], [1, 64]]),
                ALU.mult), deps=deps_mm + cs_toks + [P.get("rt2_free")])
            r3 = em.op("dve", lambda e, psl=psl, jt=jt: e.tensor_tensor(
                bass.AP(sb["rt2"], 64, [[512, 128], [128, 4], [1, 64]]),
                bass.AP(sb["psA"][psl], 0, [[512, 128], [128, 4], [1, 64]]),
                bass.AP(sb["cs4b"], 512 * jt + 64, [[1024, 128], [128, 4], [1, 64]]),
                ALU.mult), deps=deps_mm + [P.get("rt2_free")])
            rr = jt
            r4 = em.op("dve", lambda e, rr=rr: e.tensor_add(
                sb["qro"][rr][:, :], sb["rt1"][:, :], sb["rt2"][:, :]),
                deps=[r1, r2, r3] + (P.get(("qro_free", rr)) or []))
            P["rt1_free"] = r4
            P["rt2_free"] = r4
            psread(f"psA{psl}", r4)
            return r4, rr

        def proj_d(jb, jt, psl, wl, bl, deps0):
            mm = None
            for dcx in range(DC):
                mm = em.op("pe", lambda e, dcx=dcx, jt=jt, jb=jb, psl=psl,
                           st_=(dcx == 0): e.matmul(
                    sb["psA"][psl][:, :],
                    sb["hT"][:, 256 * dcx + 128 * jt:256 * dcx + 128 * (jt + 1)],
                    sb["wbuf"][jb][:, 512 * dcx:512 * (dcx + 1)],
                    start=st_, stop=False),
                    deps=[wl] + (deps0 if dcx == 0 else []))
            mm = em.op("pe", lambda e, psl=psl: e.matmul(
                sb["psA"][psl][:, :], sb["ones1"][:, :], sb["brow"][:, :],
                start=False, stop=True), deps=[bl, mm, ones_t])
            return mm

        def wload_d(wname, oc, jb, ncols=512):
            wap = aps[wname]
            return em.dma("pool", lambda g, oc=oc, jb=jb, wap=wap, ncols=ncols: g.dma_start(
                sb["wbuf"][jb][:, 0:ncols * DC],
                bass.AP(wap.tensor, wap.offset + ncols * oc,
                        [[wap.shape[1], 128], [128 * wap.shape[1], DC], [1, ncols]])),
                deps=[P.get(("wbufD_free", jb))])

        def brow_d(bname, oc):
            return em.dma("pool", lambda g, bname=bname, oc=oc: g.dma_start(
                sb["brow"][:, :],
                bass.AP(aps[bname].tensor, aps[bname].offset + 512 * oc, [[512, 1], [1, 512]])),
                deps=[P.get("browD_free")])

        for oc in range(4):
            jb = oc % 2
            wl = wload_d("wq", oc, jb)
            bl = brow_d("bq", oc)
            for jt in range(2):
                psl = jt
                mm = proj_d(jb, jt, psl, wl, bl,
                            ht_toks + [psdep(f"psA{psl}")])
                P["browD_free"] = mm
                r4, rr = rope_d(psl, jt, [mm])
                dms = []
                for s_ in range(2):
                    dm = em.dma("sp", lambda e, oc=oc, s_=s_, i=i, jt=jt, rr=rr: e.dma_start(
                        bass.AP(aps["cc_a2a_in"].tensor,
                                aps["cc_a2a_in"].offset + (2 * oc + s_) * 512 * 512
                                + (256 * i + 128 * jt) * 512,
                                [[512, 128], [1, 256]]),
                        bass.AP(sb["qro"][rr], 256 * s_, [[512, 128], [1, 256]])),
                        deps=[r4] + (P.get("a2a_war") or []))
                    dms.append(dm)
                P[("qro_free", rr)] = dms
                a2a_stores.extend(dms)
            P[("wbufD_free", jb)] = mm
        for oc2 in range(2):
            jb = oc2 % 2
            wl = wload_d("wk", oc2, jb)
            bl = brow_d("bk", oc2)
            for jt in range(2):
                psl = jt
                mm = proj_d(jb, jt, psl, wl, bl, ht_toks + [psdep(f"psA{psl}")])
                P["browD_free"] = mm
                r4, rr = rope_d(psl, jt, [mm])
                dms = []
                for m_ in range(4):
                    dm = em.dma("sp", lambda e, oc2=oc2, m_=m_, i=i, jt=jt, rr=rr: e.dma_start(
                        bass.AP(aps["cc_a2a_in"].tensor,
                                aps["cc_a2a_in"].offset + (4 * oc2 + m_) * 512 * 512
                                + (256 * i + 128 * jt) * 512 + 256,
                                [[512, 128], [1, 128]]),
                        bass.AP(sb["qro"][rr], 128 * m_, [[512, 128], [1, 128]])),
                        deps=[r4] + (P.get("a2a_war") or []))
                    dms.append(dm)
                P[("qro_free", rr)] = dms
                a2a_stores.extend(dms)
            P[("wbufD_free", jb)] = mm
        for oc2 in range(2):
            jb = oc2 % 2
            wl = wload_d("wv", oc2, jb)
            bl = brow_d("bv", oc2)
            for jt in range(2):
                psl = jt
                mm = proj_d(jb, jt, psl, wl, bl, ht_toks + [psdep(f"psA{psl}")])
                P["browD_free"] = mm
                rr = jt
                cv_ = em.op("dve", lambda e, psl=psl, rr=rr: e.tensor_copy(
                    sb["qro"][rr][:, :], sb["psA"][psl][:, :]),
                    deps=[mm] + (P.get(("qro_free", rr)) or []))
                psread(f"psA{psl}", cv_)
                dms = []
                for m_ in range(4):
                    dm = em.dma("sp", lambda e, oc2=oc2, m_=m_, i=i, jt=jt, rr=rr: e.dma_start(
                        bass.AP(aps["cc_a2a_in"].tensor,
                                aps["cc_a2a_in"].offset + (4 * oc2 + m_) * 512 * 512
                                + (256 * i + 128 * jt) * 512 + 384,
                                [[512, 128], [1, 128]]),
                        bass.AP(sb["qro"][rr], 128 * m_, [[512, 128], [1, 128]])),
                        deps=[cv_] + (P.get("a2a_war") or []))
                    dms.append(dm)
                P[("qro_free", rr)] = dms
                a2a_stores.extend(dms)
            P[("wbufD_free", jb)] = mm
        P[("xsel_free", i)] = None
        P["Pm_free"] = gmm[1]
        P["cs_free"] = r4
        P[("hD_free", 0)] = mm
        P[("hD_free", 1)] = mm
        for jt in range(2):
            for dcx in range(DC):
                P[("hT_free2", jt, dcx)] = mm
        P["gcol_free"] = None
        P["selb_free"] = omm
        P["scb_free"] = gmm[1]

    def _a2a1(g):
        with allow_ici_all_to_all():
            return g.collective_compute(
                "AllToAll", ALU.bypass, replica_groups=[list(range(n_cores))],
                ins=[aps["cc_a2a_in"].opt()], outs=[aps["cc_a2a_out"].opt()])
    a2a1 = em.op("pool", _a2a1, deps=a2a_stores)
    wd_last = [None, None]
    for r0 in range(0, 8192, 128):
        j = (r0 // 128) % 2
        ld = em.dma("pool", lambda g, r0=r0, j=j: g.dma_start(
            sb["cxl"][j][:, :], aps["w_down"][r0:r0 + 128, :]), deps=[wd_last[j]])
        st_ = em.dma("sp", lambda e, r0=r0, j=j: e.dma_start(
            bass.AP(aps["scr_wd"].tensor, aps["scr_wd"].offset + r0 * D,
                    [[D, 128], [1, D]]),
            sb["cxl"][j][:, :]), deps=[ld])
        wd_last[j] = st_
    P[("cxl_free", 0)] = wd_last[0]
    P[("cxl_free", 1)] = wd_last[1]
    P["wd_done"] = em.qbar("sp")
    kkl = em.dma("sp", lambda e: e.dma_start(
        sb["kk"][:, :], aps["cc_a2a_out"][:, 256:384], transpose=True), deps=[a2a1])
    vvl = em.dma("sp", lambda e: e.dma_start(
        sb["vv"][:, :],
        bass.AP(aps["cc_a2a_out"].tensor, aps["cc_a2a_out"].offset + 384,
                [[512, 128], [128 * 512, 32], [1, 128]])), deps=[a2a1])
    ctx_st = []
    for hl in range(2):
        kql = em.dma("sp", lambda e, hl=hl: e.dma_start(
            sb["kq"][:, :], aps["cc_a2a_out"][:, 128 * hl:128 * (hl + 1)],
            transpose=True), deps=[a2a1, P.get("kq_free")])
        exp_toks = {}
        for tc in range(8):
            n_st = 4 * tc + 4
            last_av = None
            zmm = None
            avq = []
            def em_av(st_e, rb_e, ex_e, pas_e, last):
                av = em.op("pe", lambda e, st_e=st_e, rb_e=rb_e, first=(st_e == 0), last=last: e.matmul(
                    sb["pz"][:, :],
                    sb["vv"][:, 128 * st_e:128 * (st_e + 1)],
                    rb_e[:, :], start=first, stop=last),
                    deps=[ex_e, vvl] + ([psdep("pz")] if st_e == 0 else []))
                zm = em.op("pe", lambda e, st_e=st_e, rb_e=rb_e, first=(st_e == 0), last=last: e.matmul(
                    sb["pz1"][:, :], sb["onecol"][:, :],
                    rb_e[:, :], start=first, stop=last),
                    deps=[onec_t] + ([psdep("pz1")] if st_e == 0 else []))
                P[("atD", pas_e)] = [av, zm]
                P[("at2D", pas_e)] = [av, zm]
                return av, zm
            for st_i in range(n_st):
                pas = st_i % 2
                smm = em.op("pe", lambda e, st_i=st_i, tc=tc, pas=pas: e.matmul(
                    sb["psA"][pas][:, :],
                    sb["kk"][:, 128 * st_i:128 * (st_i + 1)],
                    sb["kq"][:, 512 * tc:512 * (tc + 1)],
                    start=True, stop=True),
                    deps=[kql, kkl, exp_toks.get(pas)])
                if st_i == 1 and P.get("pendD") is not None:
                    P["pendD"]()
                    P["pendD"] = None
                ex = em.op("act", lambda e, pas=pas: e.activation(
                    sb["at"][pas][:, :], sb["psA"][pas][:, :], AF.Exp,
                    scale=1.0 / np.sqrt(128.0)),
                    deps=[smm, P.get(("atD", pas))])
                exp_toks[pas] = ex
                rhs_buf = sb["at"][pas]
                if st_i >= 4 * tc:
                    mk = st_i - 4 * tc
                    ex = em.op("dve", lambda e, pas=pas, mk=mk: e.tensor_tensor(
                        sb["at2"][pas][:, :], sb["at"][pas][:, :],
                        sb["mask"][mk][:, :], ALU.mult),
                        deps=[ex, mtoks[mk], P.get(("at2D", pas))])
                    rhs_buf = sb["at2"][pas]
                avq.append((st_i, rhs_buf, ex, pas))
                if len(avq) >= 2:
                    last_av, zmm = em_av(*avq.pop(0), False)
            last_av, zmm = em_av(*avq.pop(0), True)

            def mk_zflushD(hl=hl, tc=tc, last_av=last_av, zmm=zmm):
                def zf():
                    zc = em.op("dve", lambda e: e.reciprocal(
                        sb["zrb"][:, :], sb["pz1"][0:1, :]),
                        deps=[zmm, P.get("zrbD_free")], lowp=True)
                    psread("pz1", zc)
                    zbm = em.op("pe", lambda e: e.matmul(
                        sb["psB"][0][:, :], sb["ones1"][:, :], sb["zrb"][:, :],
                        start=True, stop=True), deps=[zc, psdep("psB0")])
                    P["zrbD_free"] = zbm
                    zcp = em.op("dve", lambda e: e.tensor_copy(
                        sb["zb"][:, :], sb["psB"][0][:, :]), deps=[zbm, P.get("zbD_free")])
                    psread("psB0", zcp)
                    cm = em.op("dve", lambda e, tc=tc: e.tensor_tensor(
                        sb["ctxT"][:, 512 * tc:512 * (tc + 1)],
                        sb["pz"][:, :], sb["zb"][:, :], ALU.mult),
                        deps=[zcp, last_av, P.get(("ctxT_free", tc))])
                    P["zbD_free"] = cm
                    psread("pz", cm)
                    dm = em.dma("sp", lambda e, hl=hl, tc=tc: e.dma_start(
                        bass.AP(aps["cc_ctx_in"].tensor,
                                aps["cc_ctx_in"].offset + (256 * tc + 128 * hl) * 512,
                                [[512, 128], [1, 512]]),
                        sb["ctxT"][:, 512 * tc:512 * (tc + 1)]),
                        deps=[cm] + (P.get("ctx_war") or []))
                    P[("ctxT_free", tc)] = dm
                    ctx_st.append(dm)
                return zf
            P["pendD"] = mk_zflushD()
        if P.get("pendD") is not None:
            P["pendD"]()
            P["pendD"] = None
        P["kq_free"] = smm
    def _a2a2(g):
        with allow_ici_all_to_all():
            return g.collective_compute(
                "AllToAll", ALU.bypass, replica_groups=[list(range(n_cores))],
                ins=[aps["cc_ctx_in"].opt()], outs=[aps["cc_ctx_out"].opt()])
    a2a2 = em.op("pool", _a2a2, deps=ctx_st)
    for tt in range(4):
        i_, jt = tt // 2, tt % 2
        jb = tt % 2
        cxl_t = em.dma("sp", lambda e, tt=tt, jb=jb: e.dma_start(
            sb["cxl"][jb][:, :],
            bass.AP(aps["cc_ctx_out"].tensor, aps["cc_ctx_out"].offset + 128 * tt,
                    [[512, 128], [128 * 512, DC], [1, 128]]),
            allow_slow_non_contiguous=False),
            deps=[a2a2, P.get(("cxl_free", jb))] + ([P.get("cxl0_free")] if jb == 0 else []))
        a_toks = []
        for dc2 in range(4):
            jw = dc2 % 2
            wl = em.dma("pool", lambda g, dc2=dc2, jw=jw: g.dma_start(
                sb["wbuf"][jw][:, :],
                bass.AP(aps["wo"].tensor, aps["wo"].offset + 512 * dc2,
                        [[2048, 128], [128 * 2048, DC], [1, 512]])),
                deps=[P.get(("wbufD_free", jw))])
            psl = dc2 % 2
            mm = None
            for kc in range(DC):
                mm = em.op("pe", lambda e, kc=kc, dc2=dc2, jb=jb, jw=jw, psl=psl,
                           st_=(kc == 0), sp_=(kc == DC - 1): e.matmul(
                    sb["psA"][psl][:, :],
                    sb["cxl"][jb][:, 128 * kc:128 * (kc + 1)],
                    sb["wbuf"][jw][:, 512 * kc:512 * (kc + 1)],
                    start=st_, stop=sp_),
                    deps=[wl, cxl_t] + ([psdep(f"psA{psl}")] if kc == 0 else []))
            P[("wbufD_free", jw)] = mm
            P[("cxl_free", jb)] = mm
            ao = em.op("dve", lambda e, tt=tt, dc2=dc2, psl=psl: e.tensor_copy(
                sb["aout"][:, 2048 * tt + 512 * dc2:2048 * tt + 512 * (dc2 + 1)],
                sb["psA"][psl][:, :]), deps=[mm, P.get(("aout_free", tt, dc2))])
            x1_ = em.op("dve", lambda e, i_=i_, jt=jt, dc2=dc2, psl=psl: e.scalar_tensor_tensor(
                sb["x1t"][:, 512 * dc2:512 * (dc2 + 1)], sb["psA"][psl][:, :], 1.0,
                sb["xsel"][i_][:, 2048 * jt + 512 * dc2:2048 * jt + 512 * (dc2 + 1)],
                ALU.bypass, ALU.add), deps=[mm, P.get("x1t_free")])
            psread(f"psA{psl}", x1_)
            a_toks.append([ao, x1_])
        sqs = []
        for c in range(4):
            sq = em.op("act", lambda e, c=c: e.activation(
                sb["cmp"][:, 0:512], sb["x1t"][:, 512 * c:512 * (c + 1)],
                AF.Square, accum_out=sb["lncol"][:, 20 + c:21 + c]),
                deps=[a_toks[c][1], P.get("cmp_free"), P.get(("lncol8", c))])
            P["cmp_free"] = sq
            sqs.append(sq)
        v1 = em.op("dve", lambda e: e.tensor_add(
            sb["lncol"][:, 24:25], sb["lncol"][:, 20:21], sb["lncol"][:, 21:22]), deps=sqs[:2])
        v2 = em.op("dve", lambda e: e.tensor_add(
            sb["lncol"][:, 25:26], sb["lncol"][:, 22:23], sb["lncol"][:, 23:24]), deps=sqs[2:])
        v3 = em.op("dve", lambda e: e.tensor_add(
            sb["lncol"][:, 26:27], sb["lncol"][:, 24:25], sb["lncol"][:, 25:26]), deps=[v1, v2])
        c0 = em.op("dve", lambda e: e.tensor_scalar(
            sb["lncol"][:, 27:28], sb["lncol"][:, 26:27], 1.0 / D, 1e-6,
            ALU.mult, ALU.add), deps=[v3])
        c1 = em.op("act", lambda e: e.activation(
            sb["lncol"][:, 24:25], sb["lncol"][:, 27:28], AF.Ln), deps=[c0])
        rst = em.op("act", lambda e: e.activation(
            sb["lncol"][:, 25:26], sb["lncol"][:, 24:25], AF.Exp, scale=-0.5), deps=[c1])
        h2_toks = []
        for c in range(4):
            y1 = em.op("dve", lambda e, c=c: e.tensor_scalar(
                sb["cmp"][:, 0:512], sb["x1t"][:, 512 * c:512 * (c + 1)],
                sb["lncol"][:, 25:26], None, ALU.mult),
                deps=[rst, P.get("cmp_free")])
            P[("lncol8", c)] = y1
            y2 = em.op("dve", lambda e, c=c: e.tensor_tensor(
                sb["h2"][:, 512 * c:512 * (c + 1)], sb["cmp"][:, 0:512],
                sb["ln2b"][:, 512 * c:512 * (c + 1)], ALU.mult),
                deps=[y1, bc_done[7], P.get(("h2_free", c))])
            P["cmp_free"] = y2
            h2_toks.append(y2)
        P["x1t_free"] = sqs + h2_toks
        for dcx in range(DC):
            tt_ = em.op("pe", lambda e, dcx=dcx: e.transpose(
                sb["pt"][:, :], sb["h2"][:, 128 * dcx:128 * (dcx + 1)],
                sb["ident"][:, :]), deps=h2_toks + [ident_t, psdep("pt")])
            tc_ = em.op("dve", lambda e, dcx=dcx, tt=tt: e.tensor_copy(
                sb["h2T"][:, 512 * dcx + 128 * tt:512 * dcx + 128 * (tt + 1)],
                sb["pt"][:, :]), deps=[tt_, P.get(("h2T_free", dcx, tt))])
            psread("pt", tc_)
            P.setdefault("h2T_toks", []).append(tc_)
        for c in range(4):
            P[("h2_free", c)] = P["h2T_toks"][-DC:]
    mi_st = []
    for ict in range(64):
        jb = ict % 2
        wl1 = em.dma("pool", lambda g, ict=ict, jb=jb: g.dma_start(
            sb["wbuf"][jb][:, 0:2048],
            bass.AP(aps["w_gate"].tensor, aps["w_gate"].offset + 128 * ict,
                    [[8192, 128], [128 * 8192, DC], [1, 128]])),
            deps=[P.get(("wbufD_free", jb))])
        wl2 = em.dma("pool", lambda g, ict=ict, jb=jb: g.dma_start(
            sb["wbuf"][jb][:, 2048:4096],
            bass.AP(aps["w_up"].tensor, aps["w_up"].offset + 128 * ict,
                    [[8192, 128], [128 * 8192, DC], [1, 128]])),
            deps=[P.get(("wbufD_free", jb))])
        mmg = mmu = None
        for dcx in range(DC):
            mmg = em.op("pe", lambda e, dcx=dcx, jb=jb, st_=(dcx == 0),
                        sp_=(dcx == DC - 1): e.matmul(
                sb["psA"][0][:, :],
                sb["wbuf"][jb][:, 128 * dcx:128 * (dcx + 1)],
                sb["h2T"][:, 512 * dcx:512 * (dcx + 1)],
                start=st_, stop=sp_),
                deps=[wl1] + (P.get("h2T_toks", []) + [psdep("psA0")] if dcx == 0 else []))
            mmu = em.op("pe", lambda e, dcx=dcx, jb=jb, st_=(dcx == 0),
                        sp_=(dcx == DC - 1): e.matmul(
                sb["psA"][1][:, :],
                sb["wbuf"][jb][:, 2048 + 128 * dcx:2048 + 128 * (dcx + 1)],
                sb["h2T"][:, 512 * dcx:512 * (dcx + 1)],
                start=st_, stop=sp_),
                deps=[wl2] + ([psdep("psA1")] if dcx == 0 else []))
        P[("wbufD_free", jb)] = [mmg, mmu]
        P["h2T_rd"] = [mmg, mmu]
        sl_ = em.op("act", lambda e: e.activation(
            sb["silu"][:, :], sb["psA"][0][:, :], AF.Silu),
            deps=[mmg, P.get("silu_free")])
        psread("psA0", sl_)
        mi_ = em.op("dve", lambda e: e.tensor_tensor(
            sb["mit"][:, :], sb["psA"][1][:, :], sb["silu"][:, :], ALU.mult),
            deps=[sl_, mmu, P.get("mit_free")])
        psread("psA1", mi_)
        P["silu_free"] = mi_
        dm = em.dma("sp", lambda e, ict=ict: e.dma_start(
            aps["scr_mi"][128 * ict:128 * (ict + 1), :], sb["mit"][:, :]), deps=[mi_])
        P["mit_free"] = dm
        mi_st.append(dm)
    for tt in range(4):
        i_, jt = tt // 2, tt % 2
        ml = em.dma("sp", lambda e, tt=tt: e.dma_start(
            sb["h2T"][:, :],
            bass.AP(aps["scr_mi"].tensor, aps["scr_mi"].offset + 128 * tt,
                    [[512, 128], [128 * 512, 64], [1, 128]])),
            deps=mi_st + [P.get("h2T_rd")])
        for dc2 in range(4):
            psl = dc2 % 2
            mm = None
            for q4 in range(4):
                jw = q4 % 2
                wl = em.dma("pool" if jw == 0 else "sp",
                            lambda g, dc2=dc2, q4=q4, jw=jw: g.dma_start(
                    sb["wbuf"][jw][:, 0:512 * DC],
                    bass.AP(aps["scr_wd"].tensor,
                            aps["scr_wd"].offset + (2048 * q4) * D + 512 * dc2,
                            [[D, 128], [128 * D, DC], [1, 512]])),
                    deps=[P.get(("wbufD_free", jw))] + P.get("wd_done", []))
                for icr in range(DC):
                    icc = 16 * q4 + icr
                    mm = em.op("pe", lambda e, icc=icc, icr=icr, jw=jw, psl=psl,
                               st_=(icc == 0), sp_=(icc == 63): e.matmul(
                        sb["psB"][psl][:, :],
                        sb["h2T"][:, 128 * icc:128 * (icc + 1)],
                        sb["wbuf"][jw][:, 512 * icr:512 * (icr + 1)],
                        start=st_, stop=sp_),
                        deps=[wl, ml] + ([psdep(f"psB{psl}")] if icc == 0 else []))
                P[("wbufD_free", jw)] = mm
            P["h2T_rd"] = mm
            g1 = em.op("dve", lambda e, tt=tt, dc2=dc2, psl=psl: e.scalar_tensor_tensor(
                sb["f1"][:, :], sb["psB"][psl][:, :], 1.0,
                sb["aout"][:, 2048 * tt + 512 * dc2:2048 * tt + 512 * (dc2 + 1)],
                ALU.bypass, ALU.add), deps=[mm, P.get("f1D_free")])
            psread(f"psB{psl}", g1)
            g2 = em.op("dve", lambda e, tt=tt, dc2=dc2: e.tensor_scalar(
                sb["gD"][:, 2048 * tt + 512 * dc2:2048 * tt + 512 * (dc2 + 1)],
                sb["f1"][:, :], sb["gcol"][:, tt:tt + 1], None, ALU.mult),
                deps=[g1, P[("gsg", i_)], P.get(("gD_free", tt, dc2))])
            P["f1D_free"] = g2
            P.setdefault("gD_toks", []).append(g2)
            P[("aout_free", tt, dc2)] = None
    x4_stores = []
    for i in range(BPC):
        sc0 = i * 96
        b1 = em.dma("sp", lambda e, sc0=sc0: e.dma_start(
            bass.AP(aps["scr_srow"].tensor, aps["scr_srow"].offset, [[1, 128], [128, 16]]),
            sb["scol"][:, sc0 + 64:sc0 + 80], allow_slow_non_contiguous=True),
            deps=[P["scol_w"]])
        b2 = em.dma("sp", lambda e: e.dma_start(
            sb["sbc"][0:1, :],
            bass.AP(aps["scr_srow"].tensor, aps["scr_srow"].offset, [[2048, 1], [1, 2048]])),
            deps=[b1, P.get("sbc_free")])
        b3s = []
        for bcc in range(4):
            bmm = em.op("pe", lambda e, bcc=bcc: e.matmul(
                sb["pz"][:, :], sb["onesf"][:, :], sb["sbc"][0:1, 512 * bcc:512 * (bcc + 1)],
                start=True, stop=True), deps=[b2, onesf_t, psdep("pz")])
            bcp = em.op("dve", lambda e, bcc=bcc: e.tensor_copy(
                sb["sbc"][:, 512 * bcc:512 * (bcc + 1)], sb["pz"][:, :]), deps=[bmm])
            psread("pz", bcp)
            b3s.append(bcp)
        b3 = b3s
        pass
        pt_toks = []
        for jc in range(2):
            pt_ = em.op("dve", lambda e, jc=jc: e.tensor_scalar(
                sb["PT"][:, 2048 * jc:2048 * (jc + 1)], sb["sbc"][:, :],
                sb["iotc"][:, jc:jc + 1], None, ALU.is_equal),
                deps=[b3, ioc_t, P.get("PT_free")])
            pt_toks.append(pt_)
        P["sbc_free"] = pt_toks
        for nt in range(NT):
            for dc2 in range(4):
                psl = (nt * 4 + dc2) % 2
                lx = em.dma("sp", lambda e, i=i, nt=nt, dc2=dc2, psl=psl: e.dma_start(
                    sb["x3c"][psl][:, :],
                    aps["x3"][i][128 * nt:128 * (nt + 1), 512 * dc2:512 * (dc2 + 1)]),
                    deps=[P.get("x3cD_free")])
                mm = None
                for jc in range(2):
                    mm = em.op("pe", lambda e, jc=jc, nt=nt, dc2=dc2, i=i, psl=psl,
                               st_=(jc == 0), sp_=(jc == 1): e.matmul(
                        sb["psA"][psl][:, :],
                        sb["PT"][:, 2048 * jc + 128 * nt:2048 * jc + 128 * (nt + 1)],
                        sb["gD"][:, 2048 * (2 * i + jc) + 512 * dc2:2048 * (2 * i + jc) + 512 * (dc2 + 1)],
                        start=st_, stop=sp_),
                        deps=pt_toks + (P.get("gD_toks", []) + [psdep(f"psA{psl}")] if jc == 0 else []))
                c = 2 * (i * NT + nt)
                o1 = em.op("dve", lambda e, c=c, psl=psl: e.tensor_scalar(
                    sb["f2"][:, :], sb["x3c"][psl][:, :],
                    sb["stats"][:, c:c + 1], sb["stats"][:, c + 1:c + 2],
                    ALU.add, ALU.mult), deps=[lx, P.get("f2D_free")])
                o2 = em.op("dve", lambda e, psl=psl: e.scalar_tensor_tensor(
                    sb["x4c"][psl][:, :], sb["psA"][psl][:, :], 1.0,
                    sb["f2"][:, :], ALU.bypass, ALU.add),
                    deps=[mm, o1, P.get("x4c_free")])
                psread(f"psA{psl}", o2)
                P["f2D_free"] = o2
                o3 = em.op("dve", lambda e, psl=psl: e.tensor_add(
                    sb["f2"][:, :], sb["x4c"][psl][:, :], sb["x3c"][psl][:, :]),
                    deps=[o2])
                P["x3cD_free"] = o3
                P["x4c_free"] = o3
                dmo = em.dma("sp", lambda e, i=i, nt=nt, dc2=dc2: e.dma_start(
                    aps["x4"][i][128 * nt:128 * (nt + 1), 512 * dc2:512 * (dc2 + 1)],
                    sb["f2"][:, :]), deps=[o3])
                P["f2D_free"] = dmo
                x4_stores.append(dmo)
        P["PT_free"] = mm
    return x4_stores


def build_l1_ffn_program(n_cores=8):
    nc = bass.Bass("TRN2", target_bir_lowering=False)
    x3 = nc.dram_tensor("x3", [BPC * T, D], F32, kind="ExternalInput").ap()
    w_router = nc.dram_tensor("w_router", [D], F32, kind="ExternalInput").ap()
    wq = nc.dram_tensor("wq", [D, 2048], F32, kind="ExternalInput").ap()
    bq = nc.dram_tensor("bq", [2048], F32, kind="ExternalInput").ap()
    wk = nc.dram_tensor("wk", [D, 1024], F32, kind="ExternalInput").ap()
    bk = nc.dram_tensor("bk", [1024], F32, kind="ExternalInput").ap()
    wv = nc.dram_tensor("wv", [D, 1024], F32, kind="ExternalInput").ap()
    bv = nc.dram_tensor("bv", [1024], F32, kind="ExternalInput").ap()
    wo = nc.dram_tensor("wo", [2048, D], F32, kind="ExternalInput").ap()
    w_gate = nc.dram_tensor("w_gate", [D, 8192], F32, kind="ExternalInput").ap()
    w_up = nc.dram_tensor("w_up", [D, 8192], F32, kind="ExternalInput").ap()
    w_down = nc.dram_tensor("w_down", [8192, D], F32, kind="ExternalInput").ap()
    ln1_w = nc.dram_tensor("ln1_w", [D], F32, kind="ExternalInput").ap()
    ln2_w = nc.dram_tensor("ln2_w", [D], F32, kind="ExternalInput").ap()
    rope_cs = nc.dram_tensor("rope_cs", [T, 1024], F32, kind="ExternalInput").ap()
    x4 = nc.dram_tensor("x4", [BPC * T, D], F32, kind="ExternalOutput").ap()
    xln = nc.dram_tensor("xlnD", [BPC * T, D], BF16).ap()
    scr_srow = nc.dram_tensor("scr_srowD", [2048], F32).ap()
    scr_mi = nc.dram_tensor("scr_miD", [8192, 512], BF16).ap()
    scr_wd = nc.dram_tensor("scr_wdD", [8192 * D], BF16).ap()
    cc_a2a_in = nc.dram_tensor("cc_a2a_inD", [4096, 512], BF16).ap()
    cc_a2a_out = nc.dram_tensor("cc_a2a_outD", [4096, 512], BF16).ap()
    cc_ctx_in = nc.dram_tensor("cc_ctx_inD", [2048, 512], BF16).ap()
    cc_ctx_out = nc.dram_tensor("cc_ctx_outD", [2048, 512], BF16).ap()
    em = Em(nc)
    sb = alloc_l1_ffn_sbuf(em)
    aps = {
        "x3": [x3[i * T:(i + 1) * T, :] for i in range(BPC)],
        "x4": [x4[i * T:(i + 1) * T, :] for i in range(BPC)],
        "xln": [xln[i * T:(i + 1) * T, :] for i in range(BPC)],
        "w_router": w_router, "wq": wq, "bq": bq, "wk": wk, "bk": bk,
        "wv": wv, "bv": bv, "wo": wo, "w_gate": w_gate, "w_up": w_up,
        "w_down": w_down, "ln1_w": ln1_w, "ln2_w": ln2_w, "rope_cs": rope_cs,
        "scr_srow": scr_srow, "scr_mi": scr_mi, "scr_wd": scr_wd,
        "cc_a2a_in": cc_a2a_in, "cc_a2a_out": cc_a2a_out,
        "cc_ctx_in": cc_ctx_in, "cc_ctx_out": cc_ctx_out,
    }
    emit_l1_ffn(em, aps, sb, n_cores=n_cores)
    em.emit()
    return nc



_NC_CACHE = {}


def _get_nc(name, builder):
    if name not in _NC_CACHE:
        _NC_CACHE[name] = builder()
    return _NC_CACHE[name]


def kernel_unfused(**inp):
    f32 = np.float32
    consts = host_constants()
    x = np.ascontiguousarray(np.asarray(inp["L0_attn_x"], f32))

    def shards(arr):
        return [np.ascontiguousarray(arr[2 * c:2 * c + 2].reshape(BPC * T, D))
                for c in range(N_CORES)]

    ncA = _get_nc("A", build_l0_attn_program)
    common = {
        "wq": np.asarray(inp["L0_attn_Wq"], f32), "wk": np.asarray(inp["L0_attn_Wk"], f32),
        "wv": np.asarray(inp["L0_attn_Wv"], f32), "wo": np.asarray(inp["L0_attn_Wo"], f32),
        "rope_cos2": consts["rope64_cos2"], "rope_sin": consts["rope64_sin"],
        "rope_nsin": consts["rope64_nsin"],
    }
    im = [dict(common, x=s) for s in shards(x)]
    res = run_bass_kernel_spmd(ncA, im, core_ids=list(range(N_CORES)))
    x1s = [r["x1"] for r in res.results]

    ncB = _get_nc("B", build_l0_ffn_program)
    common = {
        "mu": np.ascontiguousarray(np.asarray(inp["L0_ffn_mu"], f32).reshape(D, 2048)),
        "gamma": np.asarray(inp["L0_ffn_gamma"], f32),
        "beta": np.asarray(inp["L0_ffn_beta"], f32),
        "scale": np.asarray(inp["L0_ffn_scale"], f32),
        "w1": np.asarray(inp["L0_ffn_W1"], f32), "b1": np.asarray(inp["L0_ffn_b1"], f32),
        "w2": np.asarray(inp["L0_ffn_W2"], f32), "b2": np.asarray(inp["L0_ffn_b2"], f32),
    }
    im = [dict(common, x1=x1s[c]) for c in range(N_CORES)]
    res = run_bass_kernel_spmd(ncB, im, core_ids=list(range(N_CORES)))
    x2s = [r["x2"] for r in res.results]

    ncC = _get_nc("C", build_l1_attn_program)
    common = {
        "wq": np.asarray(inp["L1_attn_Wq"], f32), "bq": np.asarray(inp["L1_attn_bq"], f32),
        "wk": np.asarray(inp["L1_attn_Wk"], f32), "bk": np.asarray(inp["L1_attn_bk"], f32),
        "wv": np.asarray(inp["L1_attn_Wv"], f32), "bv": np.asarray(inp["L1_attn_bv"], f32),
        "wg": np.asarray(inp["L1_attn_Wg"], f32), "bg": np.asarray(inp["L1_attn_bg"], f32),
        "wmb": np.asarray(inp["L1_attn_Wmb"], f32), "bmb": np.asarray(inp["L1_attn_bmb"], f32),
        "memory": np.asarray(inp["L1_attn_memory"], f32),
        "norm": np.asarray(inp["L1_attn_norm"], f32),
    }
    im = [dict(common, x2=x2s[c]) for c in range(N_CORES)]
    res = run_bass_kernel_spmd(ncC, im, core_ids=list(range(N_CORES)))
    x3s = [r["x3"] for r in res.results]

    ncD = _get_nc("D", lambda: build_l1_ffn_program(n_cores=8))
    common = {
        "w_router": np.asarray(inp["L1_ffn_w_router"], f32),
        "wq": np.asarray(inp["L1_ffn_wq"], f32), "bq": np.asarray(inp["L1_ffn_bq"], f32),
        "wk": np.asarray(inp["L1_ffn_wk"], f32), "bk": np.asarray(inp["L1_ffn_bk"], f32),
        "wv": np.asarray(inp["L1_ffn_wv"], f32), "bv": np.asarray(inp["L1_ffn_bv"], f32),
        "wo": np.asarray(inp["L1_ffn_wo"], f32),
        "w_gate": np.asarray(inp["L1_ffn_w_gate"], f32),
        "w_up": np.asarray(inp["L1_ffn_w_up"], f32),
        "w_down": np.asarray(inp["L1_ffn_w_down"], f32),
        "ln1_w": np.asarray(inp["L1_ffn_ln1_w"], f32),
        "ln2_w": np.asarray(inp["L1_ffn_ln2_w"], f32),
        "rope_cs": consts["rope128_cs"],
    }
    im = [dict(common, x3=x3s[c]) for c in range(N_CORES)]
    res = run_bass_kernel_spmd(ncD, im, core_ids=list(range(N_CORES)))
    out = np.concatenate([r["x4"].reshape(BPC, T, D) for r in res.results], axis=0)
    return out



def build_fused_program():
    nc = bass.Bass("TRN2", target_bir_lowering=False)
    sem_stack = ExitStack()
    shared = {}

    x = nc.dram_tensor("x", [BPC * T, D], F32, kind="ExternalInput").ap()
    out = nc.dram_tensor("out", [BPC * T, D], F32, kind="ExternalOutput").ap()
    x1 = nc.dram_tensor("fx1", [BPC * T, D], F32).ap()
    x2 = nc.dram_tensor("fx2", [BPC * T, D], F32).ap()
    x3 = nc.dram_tensor("fx3", [BPC * T, D], F32).ap()

    def ext(name, shape):
        return nc.dram_tensor(name, shape, F32, kind="ExternalInput").ap()

    a_aps = {
        "x": [x[i * T:(i + 1) * T, :] for i in range(BPC)],
        "x1": [x1[i * T:(i + 1) * T, :] for i in range(BPC)],
        "scr_q": nc.dram_tensor("a_scr_q", [T, D], BF16).ap(),
        "scr_k": nc.dram_tensor("a_scr_k", [T, 1024], BF16).ap(),
        "wq": ext("a_wq", [D, D]), "wk": ext("a_wk", [D, 512]),
        "wv": ext("a_wv", [D, 512]), "wo": ext("a_wo", [D, D]),
        "rope_cos2": ext("a_rc2", [T, 512]), "rope_sin": ext("a_rs", [T, 256]),
        "rope_nsin": ext("a_rn", [T, 256]),
    }
    axln = nc.dram_tensor("a_xln", [BPC * T, D], BF16).ap()
    a_aps["xln"] = [axln[i * T:(i + 1) * T, :] for i in range(BPC)]
    em = Em(nc, sem_stack, shared)
    sb = alloc_l0_attn_sbuf(em)
    emit_l0_attn(em, a_aps, sb)
    em.emit()
    em.stack.close()

    bxln = nc.dram_tensor("b_xln", [BPC * T, D], BF16).ap()
    b_aps = {
        "x1": [x1[i * T:(i + 1) * T, :] for i in range(BPC)],
        "x2": [x2[i * T:(i + 1) * T, :] for i in range(BPC)],
        "xln": [bxln[i * T:(i + 1) * T, :] for i in range(BPC)],
        "scr_e": nc.dram_tensor("b_scr_e", [T, 2048], BF16).ap(),
        "scr_row": nc.dram_tensor("b_scr_row", [1, 2048], BF16).ap(),
        "scr_mun": nc.dram_tensor("b_scr_mun", [D * 2048], BF16).ap(),
        "mu": ext("b_mu", [D, 2048]), "gamma": ext("b_gamma", [D]),
        "beta": ext("b_beta", [D]), "scale": ext("b_scale", [1]),
        "w1": ext("b_w1", [16, D, 512]), "b1": ext("b_b1", [16, 512]),
        "w2": ext("b_w2", [16, 512, D]), "b2": ext("b_b2", [16, D]),
    }
    em = Em(nc, sem_stack, shared)
    sb = alloc_l0_ffn_sbuf(em)
    emit_l0_ffn(em, b_aps, sb)
    em.emit()
    em.stack.close()

    cxln = nc.dram_tensor("c_xln", [BPC * T, D], BF16).ap()
    cmk = nc.dram_tensor("c_scr_mk", [BPC * T, 3072], BF16).ap()
    cmq = nc.dram_tensor("c_scr_mq", [BPC * T, 12288], BF16).ap()
    cmb = nc.dram_tensor("c_scr_mb", [BPC * T, 2048], BF16).ap()
    cg = nc.dram_tensor("c_scr_g", [BPC * T, 2048], BF16).ap()
    c_aps = {
        "x2": [x2[i * T:(i + 1) * T, :] for i in range(BPC)],
        "x3": [x3[i * T:(i + 1) * T, :] for i in range(BPC)],
        "xln": [cxln[i * T:(i + 1) * T, :] for i in range(BPC)],
        "scr_mk": [cmk[i * T:(i + 1) * T, :] for i in range(BPC)],
        "scr_mq": [cmq[i * T:(i + 1) * T, :] for i in range(BPC)],
        "scr_mb": [cmb[i * T:(i + 1) * T, :] for i in range(BPC)],
        "scr_g": [cg[i * T:(i + 1) * T, :] for i in range(BPC)],
        "scr_w": nc.dram_tensor("c_scr_w", [3 * D * D + 2 * 512 * D], BF16).ap(),
        "cc_in": nc.dram_tensor("c_cc_in", [128, 12384], F32).ap(),
        "cc_out": nc.dram_tensor("c_cc_out", [128, 12384], F32).ap(),
        "wq": ext("c_wq", [D, D]), "wk": ext("c_wk", [512, D]),
        "wv": ext("c_wv", [512, D]), "wg": ext("c_wg", [D, D]),
        "wmb": ext("c_wmb", [D, D]),
        "bq": ext("c_bq", [D]), "bk": ext("c_bk", [512]), "bv": ext("c_bv", [512]),
        "bg": ext("c_bg", [D]), "bmb": ext("c_bmb", [D]),
        "memory": ext("c_memory", [16, 768, 128]), "norm": ext("c_norm", [16, 768]),
    }
    em = Em(nc, sem_stack, shared)
    sb = alloc_l1_attn_sbuf(em)
    emit_l1_attn(em, c_aps, sb, n_cores=N_CORES)
    em.emit()
    em.stack.close()

    dxln = nc.dram_tensor("d_xln", [BPC * T, D], BF16).ap()
    d_aps = {
        "x3": [x3[i * T:(i + 1) * T, :] for i in range(BPC)],
        "x4": [out[i * T:(i + 1) * T, :] for i in range(BPC)],
        "xln": [dxln[i * T:(i + 1) * T, :] for i in range(BPC)],
        "w_router": ext("d_w_router", [D]),
        "wq": ext("d_wq", [D, 2048]), "bq": ext("d_bq", [2048]),
        "wk": ext("d_wk", [D, 1024]), "bk": ext("d_bk", [1024]),
        "wv": ext("d_wv", [D, 1024]), "bv": ext("d_bv", [1024]),
        "wo": ext("d_wo", [2048, D]),
        "w_gate": ext("d_w_gate", [D, 8192]), "w_up": ext("d_w_up", [D, 8192]),
        "w_down": ext("d_w_down", [8192, D]),
        "ln1_w": ext("d_ln1_w", [D]), "ln2_w": ext("d_ln2_w", [D]),
        "rope_cs": ext("d_rope_cs", [T, 1024]),
        "scr_srow": nc.dram_tensor("d_scr_srow", [2048], F32).ap(),
        "scr_mi": nc.dram_tensor("d_scr_mi", [8192, 512], BF16).ap(),
        "scr_wd": nc.dram_tensor("d_scr_wd", [8192 * D], BF16).ap(),
        "cc_a2a_in": nc.dram_tensor("d_cc_a2a_in", [4096, 512], BF16).ap(),
        "cc_a2a_out": nc.dram_tensor("d_cc_a2a_out", [4096, 512], BF16).ap(),
        "cc_ctx_in": nc.dram_tensor("d_cc_ctx_in", [2048, 512], BF16).ap(),
        "cc_ctx_out": nc.dram_tensor("d_cc_ctx_out", [2048, 512], BF16).ap(),
    }
    em = Em(nc, sem_stack, shared)
    sb = alloc_l1_ffn_sbuf(em)
    emit_l1_ffn(em, d_aps, sb, n_cores=N_CORES)
    em.emit()
    em.stack.close()
    return nc


def kernel_fused(**inp):
    f32 = np.float32
    consts = host_constants()
    x = np.ascontiguousarray(np.asarray(inp["L0_attn_x"], f32))
    nc = _get_nc("FUSED", build_fused_program)
    common = {
        "a_wq": np.asarray(inp["L0_attn_Wq"], f32), "a_wk": np.asarray(inp["L0_attn_Wk"], f32),
        "a_wv": np.asarray(inp["L0_attn_Wv"], f32), "a_wo": np.asarray(inp["L0_attn_Wo"], f32),
        "a_rc2": consts["rope64_cos2"], "a_rs": consts["rope64_sin"],
        "a_rn": consts["rope64_nsin"],
        "b_mu": np.ascontiguousarray(np.asarray(inp["L0_ffn_mu"], f32).reshape(D, 2048)),
        "b_gamma": np.asarray(inp["L0_ffn_gamma"], f32),
        "b_beta": np.asarray(inp["L0_ffn_beta"], f32),
        "b_scale": np.asarray(inp["L0_ffn_scale"], f32),
        "b_w1": np.asarray(inp["L0_ffn_W1"], f32), "b_b1": np.asarray(inp["L0_ffn_b1"], f32),
        "b_w2": np.asarray(inp["L0_ffn_W2"], f32), "b_b2": np.asarray(inp["L0_ffn_b2"], f32),
        "c_wq": np.asarray(inp["L1_attn_Wq"], f32), "c_bq": np.asarray(inp["L1_attn_bq"], f32),
        "c_wk": np.asarray(inp["L1_attn_Wk"], f32), "c_bk": np.asarray(inp["L1_attn_bk"], f32),
        "c_wv": np.asarray(inp["L1_attn_Wv"], f32), "c_bv": np.asarray(inp["L1_attn_bv"], f32),
        "c_wg": np.asarray(inp["L1_attn_Wg"], f32), "c_bg": np.asarray(inp["L1_attn_bg"], f32),
        "c_wmb": np.asarray(inp["L1_attn_Wmb"], f32),
        "c_bmb": np.asarray(inp["L1_attn_bmb"], f32),
        "c_memory": np.asarray(inp["L1_attn_memory"], f32),
        "c_norm": np.asarray(inp["L1_attn_norm"], f32),
        "d_w_router": np.asarray(inp["L1_ffn_w_router"], f32),
        "d_wq": np.asarray(inp["L1_ffn_wq"], f32), "d_bq": np.asarray(inp["L1_ffn_bq"], f32),
        "d_wk": np.asarray(inp["L1_ffn_wk"], f32), "d_bk": np.asarray(inp["L1_ffn_bk"], f32),
        "d_wv": np.asarray(inp["L1_ffn_wv"], f32), "d_bv": np.asarray(inp["L1_ffn_bv"], f32),
        "d_wo": np.asarray(inp["L1_ffn_wo"], f32),
        "d_w_gate": np.asarray(inp["L1_ffn_w_gate"], f32),
        "d_w_up": np.asarray(inp["L1_ffn_w_up"], f32),
        "d_w_down": np.asarray(inp["L1_ffn_w_down"], f32),
        "d_ln1_w": np.asarray(inp["L1_ffn_ln1_w"], f32),
        "d_ln2_w": np.asarray(inp["L1_ffn_ln2_w"], f32),
        "d_rope_cs": consts["rope128_cs"],
    }
    im = [dict(common, x=np.ascontiguousarray(x[2 * c:2 * c + 2].reshape(BPC * T, D)))
          for c in range(N_CORES)]
    res = run_bass_kernel_spmd(nc, im, core_ids=list(range(N_CORES)))
    return np.concatenate([r["out"].reshape(BPC, T, D) for r in res.results], axis=0)


def kernel(**inp):
    return kernel_fused(**inp)
